# version 20
# baseline (speedup 1.0000x reference)
import numpy as np
from contextlib import ExitStack

import concourse.bass as bass
import concourse.mybir as mybir
from concourse.bass_utils import run_bass_kernel_spmd

F32 = mybir.dt.float32
BF16 = mybir.dt.bfloat16
AF = mybir.ActivationFunctionType
ALU = mybir.AluOpType

D, S, QOWN = 1024, 8192, 2048
NDC, NPAIR, NCH, NCP, NQT, QT = 8, 8, 64, 32, 4, 512
FF, NFC, TB = 4096, 32, 512
NB1 = S // TB
NB2 = QOWN // TB
GP1B, GP4 = NB1, NB1 + NB2
NG = GP4 + NB2
NGA = GP4
RT2 = 0.35355339059327373
INV_D = 1.0 / D
TAUF = 0.5 / (D * FF)

TERN_SCHED = {
    0: [('u1', k) for k in range(16)],
    1: [('u1', k) for k in range(16, 32)],
    2: [('d1', k) for k in range(16)],
    3: [('d1', k) for k in range(16, 32)],
    4: [('u2', k) for k in range(16)],
    5: [('u2', k) for k in range(16, 32)],
    6: [('d2', k) for k in range(16)],
    7: [('d2', k) for k in range(16, 32)],
}
TERN_UNITS = [u for p in range(8) for u in TERN_SCHED[p]]


def tern_out_dmas(kind):
    return 1 if kind in ('u2', 'd2') else 0


TERN_OUT_BEFORE = [0]
for _u in TERN_UNITS:
    TERN_OUT_BEFORE.append(TERN_OUT_BEFORE[-1] + tern_out_dmas(_u[0]))
N_TERN_OUT = TERN_OUT_BEFORE[-1]

F4_PER = 41
FA4_PER = 41
FE4_PER = 42
FS_PER = 32


def blk_kind(g):
    return 'p1' if g < GP1B else ('p1b' if g < GP4 else 'p4')


def n_units(g):
    return {'p1': 16, 'p1b': 8, 'p4': 0}[blk_kind(g)]


UB = [0] * (NG + 1)
KO = [0] * (NG + 1)
VO = [0] * (NG + 1)
for g in range(NG):
    UB[g + 1] = UB[g] + n_units(g)
    KO[g + 1] = KO[g] + (8 if blk_kind(g) in ('p1', 'p1b') else 0)
    VO[g + 1] = VO[g] + (8 if blk_kind(g) == 'p1' else 0)


def tc128(i):
    return (i // 2) * 128


def build_program(stop='full'):
    rank = {'p1': 1, 'p2': 2, 'p3': 3, 'full': 4, 'dbg': 5, 'p4s': 6}[stop]
    p4_full = rank != 6
    nc = bass.Bass("TRN2", target_bir_lowering=False)

    xT = nc.dram_tensor("xT", [D, S], F32, kind="ExternalInput").ap()
    xTo = nc.dram_tensor("xTo", [D, QOWN], F32, kind="ExternalInput").ap()
    wq_d = nc.dram_tensor("wq", [D, D], F32, kind="ExternalInput").ap()
    wk_d = nc.dram_tensor("wk", [D, D], F32, kind="ExternalInput").ap()
    wv_d = nc.dram_tensor("wv", [D, D], F32, kind="ExternalInput").ap()
    wo_d = nc.dram_tensor("wo", [D, D], F32, kind="ExternalInput").ap()
    nsr = nc.dram_tensor("nsr", [1, D], F32, kind="ExternalInput").ap()
    boc = nc.dram_tensor("boc", [128, NDC], F32, kind="ExternalInput").ap()
    wupT_d = nc.dram_tensor("wupT", [D, FF], F32, kind="ExternalInput").ap()
    wdnT_d = nc.dram_tensor("wdnT", [FF, D], F32, kind="ExternalInput").ap()
    yT = nc.dram_tensor("yT", [D, QOWN], F32, kind="ExternalOutput").ap()
    dbg1 = nc.dram_tensor("dbg1", [128, S], BF16, kind="ExternalOutput").ap()
    dbg2 = nc.dram_tensor("dbg2", [128, S], BF16, kind="ExternalOutput").ap()
    dbg3 = nc.dram_tensor("dbg3", [128, QOWN], BF16, kind="ExternalOutput").ap()
    dbg4 = nc.dram_tensor("dbg4", [128, NPAIR * QOWN], BF16, kind="ExternalOutput").ap()
    dbg5 = nc.dram_tensor("dbg5", [128, 6144], F32, kind="ExternalOutput").ap()
    dbgH = nc.dram_tensor("dbgH", [128, 16384], BF16, kind="ExternalOutput").ap()
    dbgM = nc.dram_tensor("dbgM", [128, 16384], BF16, kind="ExternalOutput").ap()

    KT_d = nc.dram_tensor("KT_d", [D, S], BF16).ap()
    V_d = nc.dram_tensor("V_d", [S, D], BF16).ap()
    QT_d = nc.dram_tensor("QT_d", [D, QOWN], BF16).ap()
    WUPt_d = nc.dram_tensor("WUPt_d", [NDC * NFC, 128 * 128], BF16).ap()
    WDNt_d = nc.dram_tensor("WDNt_d", [NFC * NDC, 128 * 128], BF16).ap()
    recs_d = nc.dram_tensor("recs_d", [2, 8 * 2048], F32).ap()

    ctx = ExitStack()
    ten = lambda *a: ctx.enter_context(nc.sbuf_tensor(*a))
    psu = lambda *a: ctx.enter_context(nc.psum_tensor(*a))
    S_ = {}
    for n in ("ld wl woc xd xq stm sa sb ab pbc hh kv ev eo vo p2 ksq kn knr cb tau sc ex av tl "
              "bc tw tv to wo p3 p3e f4 fs fa4 fe4 fy out dmp wd wx rc rl nn p3n p2b wu0 wu1 wu2 wu3 wd0 wd1 wd2 xd4 tw1 tw2 tw3").split():
        S_[n] = ctx.enter_context(nc.semaphore("s_" + n))

    ones_b = ten("ones_b", [128, 1], BF16)
    onesr_b = ten("onesr_b", [1, 128], BF16)
    onesr_f = ten("onesr_f", [1, 128], F32)
    ones_f = ten("ones_f", [128, 1], F32)
    sel = ten("sel", [128, 128], F32)
    selk = ten("selk", [128, 64], BF16)
    nsrow_b = ten("nsrow_b", [1, D], BF16)
    nscol = ten("nscol", [128, NDC], F32)
    nsrow_f = ten("nsrow_f", [1, D], F32)
    bout_c = ten("bout_c", [128, NDC], F32)
    eps_ln = ten("eps_ln", [1, 1], F32)
    scr1 = ten("scr1", [1, 1], F32)
    scrV = ten("scrV", [1, 1], F32)
    eps_rms = ten("eps_rms", [1, 1], F32)
    eps_t = ten("eps_t", [1, 1], F32)
    eps_f = ten("eps_f", [1, 1], F32)
    ST = ten("ST", [128, NDC * QOWN], BF16)

    PS = [psu("ps%d" % i, [128, 1024], F32) for i in range(4)]
    pA = [PS[0][:, 0:512], PS[1][:, 0:512]]
    pB = [PS[0][:, 512:1024], PS[1][:, 512:1024]]
    pKV = [PS[2][:, 0:512], PS[2][:, 512:1024]]
    pS1 = PS[3][0:1, 0:512]
    pS2 = PS[3][0:1, 512:1024]
    p_nv = PS[3][:, 0:512]
    p_sum = PS[3][:, 512:1024]

    block = ctx.enter_context(nc.Block())

    def padA(sc, sem, n=1):
        sc.activation(scr1[0:1, 0:1], scr1[0:1, 0:1], AF.Identity).then_inc(S_[sem], n)

    def padV(v, sem, n=1):
        v.tensor_copy(scrV[0:1, 0:1], scrV[0:1, 0:1]).then_inc(S_[sem], n)

    def RS(env, w, i):
        return env[w][0:1, i * TB:(i + 1) * TB]

    def stats_pe_stats(t, env, g):
        t.wait_ge(S_['xq'], g + 1)
        if g >= 1:
            t.wait_ge(S_['sa'], 3 * (g - 1) + 1)
        for c in range(NDC):
            t.matmul(pS1, ones_b[:, :], env['xbf'][:, c * TB:(c + 1) * TB],
                     start=(c == 0), stop=(c == NDC - 1), skip_group_check=True)
        mm = None
        for c in range(NDC):
            mm = t.matmul(pS2, ones_b[:, :], env['xsq'][:, c * TB:(c + 1) * TB],
                          start=(c == 0), stop=(c == NDC - 1), skip_group_check=True)
        mm.then_inc(S_['stm'], 1)

    def stats_dve_pre(v, env, g):
        if 'xd4base' in env:
            v.wait_ge(S_['xd4'], 128 * (g - env['xd4base'] + 1))
        else:
            v.wait_ge(S_['xd'], 16 * (g + 1))
        v.tensor_copy(env['xbf'][:, :], env['xblk'][g % 2][:, :])
        v.tensor_tensor(env['xsq'][:, :], env['xbf'][:, :], env['xbf'][:, :],
                        ALU.mult).then_inc(S_['xq'], 1)

    def stats_dve_rows1(v, env, g, n=1):
        v.wait_ge(S_['stm'], g + 1)
        v.tensor_single_scalar(RS(env, 'rowA', 0), pS1, INV_D, ALU.mult)
        v.tensor_single_scalar(RS(env, 'rowA', 5), pS2, INV_D, ALU.mult)
        v.tensor_tensor(RS(env, 'rowB', 0), RS(env, 'rowA', 0), RS(env, 'rowA', 0), ALU.mult)
        v.tensor_tensor(RS(env, 'rowA', 1), RS(env, 'rowA', 5), RS(env, 'rowB', 0),
                        ALU.subtract)
        padV(v, 'sa', n)

    def stats_act_fused(sc, env, g):
        sc.wait_ge(S_['sa'], 3 * g + 3)
        sc.activation(RS(env, 'rowA', 2), RS(env, 'rowA', 1), AF.Ln,
                      scale=1.0 + 1e-6, bias=eps_f[0:1, 0:1])
        sc.activation(RS(env, 'rowA', 4), RS(env, 'rowA', 2), AF.Exp, scale=-0.5)
        padA(sc, 'sb', 3)

    def stats_act_rows1(sc, env, g):
        sc.wait_ge(S_['sa'], 3 * g + 1)
        sc.activation(RS(env, 'rowA', 2), RS(env, 'rowA', 1), AF.Ln, bias=eps_ln[0:1, 0:1])
        sc.activation(RS(env, 'rowB', 1), RS(env, 'rowA', 2), AF.Exp, scale=-1.0)
        padA(sc, 'sb')

    def stats_dve_rows2(v, env, g):
        v.wait_ge(S_['sb'], 3 * g + 1)
        v.tensor_tensor(RS(env, 'rowA', 3), RS(env, 'rowA', 1), RS(env, 'rowB', 1),
                        ALU.mult)
        padV(v, 'sa')

    def stats_act_rows2(sc, env, g, kind):
        sc.wait_ge(S_['sa'], 3 * g + 2)
        if kind != 'p4':
            sc.activation(RS(env, 'rowB', 0), RS(env, 'rowA', 3), AF.Ln,
                          bias=eps_rms[0:1, 0:1])
            padA(sc, 'sb')
        else:
            sc.activation(RS(env, 'rowB', 0), RS(env, 'rowA', 3), AF.Ln)
            sc.activation(RS(env, 'rowB', 1), RS(env, 'rowB', 0), AF.Exp, scale=0.5)
            sc.activation(RS(env, 'rowB', 3), RS(env, 'rowA', 2), AF.Exp, scale=0.5)
            padA(sc, 'sb')

    def stats_dve_rows3(v, env, g, kind):
        v.wait_ge(S_['sb'], 3 * g + 2)
        if kind != 'p4':
            v.tensor_tensor(RS(env, 'rowB', 0), RS(env, 'rowA', 2), RS(env, 'rowB', 0),
                            ALU.add)
            padV(v, 'sa')
        else:
            v.tensor_single_scalar(RS(env, 'rowB', 0), RS(env, 'rowB', 1), 1e-8, ALU.add)
            v.tensor_tensor(RS(env, 'rowB', 0), RS(env, 'rowB', 3), RS(env, 'rowB', 0),
                            ALU.mult)
            padV(v, 'sa')

    def stats_act_rows3(sc, env, g, kind):
        sc.wait_ge(S_['sa'], 3 * g + 3)
        if kind != 'p4':
            sc.activation(RS(env, 'rowA', 4), RS(env, 'rowB', 0), AF.Exp, scale=-0.5)
            padA(sc, 'sb')
        else:
            sc.activation(RS(env, 'rowB', 3), RS(env, 'rowB', 0), AF.Ln)
            sc.activation(RS(env, 'rowA', 4), RS(env, 'rowB', 3), AF.Exp, scale=-1.0)
            padA(sc, 'sb')

    def stats_dve_rows4(v, env, g):
        v.wait_ge(S_['sb'], 3 * g + 3)
        v.tensor_tensor(RS(env, 'rowB', 2), RS(env, 'rowA', 0), RS(env, 'rowA', 4), ALU.mult)
        v.tensor_copy(RS(env, 'rowC', 0), RS(env, 'rowA', 4))
        v.tensor_tensor(RS(env, 'rowC', 1), RS(env, 'rowA', 4), RS(env, 'rowC', 0), ALU.subtract)
        v.tensor_scalar(RS(env, 'rowC', 2), RS(env, 'rowB', 2), -1.0, None, ALU.mult)
        v.scalar_tensor_tensor(RS(env, 'rowC', 3), RS(env, 'rowB', 2), -1.0,
                               RS(env, 'rowC', 2), ALU.mult, ALU.subtract)
        padV(v, 'ab')

    def stats_pe_work_bcast(t, env, g):
        t.wait_ge(S_['ab'], g + 1)
        if g >= 2:
            t.wait_ge(S_['hh'], 8 * (g - 1))
        ls = onesr_b[0:1, 0:128]
        t.matmul(pA[g % 2], ls, RS(env, 'rowC', 0), start=True, stop=False,
                 skip_group_check=True)
        t.matmul(pA[g % 2], ls, RS(env, 'rowC', 1), start=False, stop=True,
                 skip_group_check=True)
        t.matmul(pB[g % 2], ls, RS(env, 'rowC', 2), start=True, stop=False,
                 skip_group_check=True)
        t.matmul(pB[g % 2], ls, RS(env, 'rowC', 3), start=False, stop=True,
                 skip_group_check=True).then_inc(S_['pbc'], 1)

    def stats_pe_work(t, env, g, kind, wkt=None, wvt=None, wqt=None, split=False):
        if not split:
            stats_pe_work_bcast(t, env, g)
        nu = n_units(g)
        if nu == 0:
            return
        t.wait_ge(S_['hh'], 8 * (g + 1))
        for u in range(nu):
            if UB[g] + u - 1 >= 1:
                t.wait_ge(S_['ev'], UB[g] + u - 1)
            mm = None
            if kind == 'p1' and u < 8:
                for dc in range(NDC):
                    mm = t.matmul(pKV[u % 2], wkt[:, dc * D + u * 128: dc * D + (u + 1) * 128],
                                  env['hT'][g % 2][:, dc * TB:(dc + 1) * TB],
                                  start=(dc == 0), stop=(dc == NDC - 1), skip_group_check=True)
            elif kind == 'p1':
                i = u - 8
                tc, half = i // 2, i % 2
                for dc in range(NDC):
                    mm = t.matmul(pKV[u % 2],
                                  env['hT'][g % 2][:, dc * TB + tc * 128: dc * TB + (tc + 1) * 128],
                                  wvt[:, dc * D + half * 512: dc * D + (half + 1) * 512],
                                  start=(dc == 0), stop=(dc == NDC - 1), skip_group_check=True)
            else:
                for dc in range(NDC):
                    mm = t.matmul(pKV[u % 2], wqt[:, dc * D + u * 128: dc * D + (u + 1) * 128],
                                  env['hT'][g % 2][:, dc * TB:(dc + 1) * TB],
                                  start=(dc == 0), stop=(dc == NDC - 1), skip_group_check=True)
            mm.then_inc(S_['kv'], 1)

    def stats_dve_h(v, env, g, kind):
        for c in range(NDC):
            if c == 0:
                v.wait_ge(S_['pbc'], g + 1)
            if kind == 'p4':
                v.tensor_tensor(env['hsc'][:, :],
                                env['xblk'][g % 2][:, c * TB:(c + 1) * TB], pA[g % 2],
                                ALU.mult)
                boff = (g - GP4) * TB
                dst = env['hdst'][:, c * QOWN + boff: c * QOWN + boff + TB]
                v.tensor_tensor(dst, env['hsc'][:, :], pB[g % 2],
                                ALU.add).then_inc(S_['hh'], 1)
            else:
                v.scalar_tensor_tensor(env['hsc'][:, :],
                                       env['xblk'][g % 2][:, c * TB:(c + 1) * TB],
                                       nscol[:, c:c + 1], pA[g % 2], ALU.mult, ALU.mult)
                dst = env['hT'][g % 2][:, c * TB:(c + 1) * TB]
                v.scalar_tensor_tensor(dst, pB[g % 2], nscol[:, c:c + 1],
                                       env['hsc'][:, :],
                                       ALU.mult, ALU.add).then_inc(S_['hh'], 1)

    def stats_dve_evac(v, env, g, kind, evK=None, evV=None, hooks=None):
        for u in range(n_units(g)):
            if hooks and u in hooks:
                hooks[u]()
            v.wait_ge(S_['kv'], UB[g] + u + 1)
            if kind == 'p1' and u >= 8:
                i = u - 8
                if VO[g] + i - 2 >= 0:
                    v.wait_ge(S_['vo'], 16 * (VO[g] + i - 1))
                v.tensor_copy(evV[i % 2][:, :], pKV[u % 2]).then_inc(S_['ev'], 1)
            else:
                if KO[g] + u - 2 >= 0:
                    v.wait_ge(S_['eo'], 16 * (KO[g] + u - 1))
                v.tensor_single_scalar(evK[u % 2][:, :], pKV[u % 2], RT2,
                                       ALU.mult).then_inc(S_['ev'], 1)

    with ExitStack() as sc1:
        t1 = lambda *a: sc1.enter_context(nc.sbuf_tensor(*a))
        xblk = [t1("xblk%d" % i, [128, NDC * TB], F32) for i in range(2)]
        xbf = t1("xbf", [128, NDC * TB], BF16)
        xsq = t1("xsq", [128, NDC * TB], BF16)
        hT = [t1("hT%d" % i, [128, NDC * TB], BF16) for i in range(2)]
        evK = [t1("evK%d" % i, [128, TB], BF16) for i in range(2)]
        evV = [t1("evV%d" % i, [128, TB], BF16) for i in range(2)]
        wqt = t1("wqt", [128, NDC * D], BF16)
        wkt = t1("wkt", [128, NDC * D], BF16)
        wvt = t1("wvt", [128, NDC * D], BF16)
        rowA = t1("rowA", [1, 6 * TB], F32)
        rowB = t1("rowB", [1, 4 * TB], F32)
        rowC = t1("rowC", [1, 4 * TB], BF16)
        hsc = t1("hsc", [128, TB], BF16)
        env1 = dict(xblk=xblk, xbf=xbf, xsq=xsq, hT=hT, rowA=rowA, rowB=rowB, rowC=rowC,
                    hsc=hsc)

        def blk_src(g):
            if g < GP1B:
                return xT, g * TB
            return xTo, (g - GP1B) * TB

        @block.sync
        def _(sy):
            sy.dma_start(nsrow_f[:, :], nsr).then_inc(S_['ld'], 16)
            sy.dma_start(nscol[:, :], nsr.rearrange("a (c p) -> p (a c)", p=128),
                         allow_slow_non_contiguous=True).then_inc(S_['ld'], 16)
            sy.dma_start(bout_c[:, :], boc).then_inc(S_['ld'], 16)
            for g in range(NGA):
                src, off = blk_src(g)
                if g >= 2:
                    sy.wait_ge(S_['hh'], 8 * (g - 1))
                sy.dma_start(xblk[g % 2][:, :].rearrange("p (c t) -> p c t", c=NDC),
                             src[:, off:off + TB].rearrange("(c p) t -> p c t", c=NDC)
                             ).then_inc(S_['xd'], 16)
            if rank == 1:
                sy.wait_ge(S_['eo'], 16 * KO[NGA])
                sy.wait_ge(S_['vo'], 16 * VO[NGA])
                sy.dma_start(dbg1, KT_d[0:128, :]).then_inc(S_['out'], 16)
                sy.dma_start(dbg2.rearrange("p (a d) -> p a d", d=D),
                             V_d[0:8 * 128, :].rearrange("(a p) d -> p a d", p=128)
                             ).then_inc(S_['out'], 16)
                sy.dma_start(dbg3, QT_d[0:128, :]).then_inc(S_['out'], 16)
                sy.wait_ge(S_['out'], 48)

        @block.gpsimd
        def _(gp_):
            gp_.memset(ones_b[:, :], 1.0)
            gp_.memset(onesr_b[:, :], 1.0)
            gp_.memset(onesr_f[:, :], 1.0)
            gp_.memset(ones_f[:, :], 1.0)
            gp_.memset(sel[:, :], 0.0)
            gp_.memset(sel[0:1, 0:64], 1.0)
            gp_.memset(selk[:, :], 0.0)
            gp_.memset(selk[:, 0:1], 1.0)
            gp_.memset(sel[64:65, 64:128], 1.0)
            gp_.memset(eps_ln[:, :], 1e-5)
            gp_.memset(eps_rms[:, :], 1e-6)
            gp_.memset(eps_t[:, :], 1e-8)
            gp_.memset(eps_f[:, :], 1e-11)
            gp_.memset(scr1[:, :], 0.0)
            gp_.memset(scrV[:, :], 0.0).then_inc(S_['ld'], 1)
            gp_.dma_start(wqt[:, :].rearrange("p (c e) -> p c e", c=NDC), wq_d.rearrange("(c p) e -> p c e", c=NDC)
                          ).then_inc(S_['wl'], 16)
            gp_.dma_start(wkt[:, :].rearrange("p (c e) -> p c e", c=NDC), wk_d.rearrange("(c p) e -> p c e", c=NDC)
                          ).then_inc(S_['wl'], 16)
            gp_.dma_start(wvt[:, :].rearrange("p (c e) -> p c e", c=NDC), wv_d.rearrange("(c p) e -> p c e", c=NDC)
                          ).then_inc(S_['wl'], 16)
            for g in range(NGA):
                kind = blk_kind(g)
                for u in range(8):
                    gp_.wait_ge(S_['ev'], UB[g] + u + 1)
                    if kind == 'p1':
                        dst = KT_d[u * 128:(u + 1) * 128, g * TB:(g + 1) * TB]
                    else:
                        b = g - GP1B
                        dst = QT_d[u * 128:(u + 1) * 128, b * TB:(b + 1) * TB]
                    gp_.dma_start(dst, evK[u % 2][:, :]).then_inc(S_['eo'], 16)
                if kind == 'p1':
                    for i in range(8):
                        gp_.wait_ge(S_['ev'], UB[g] + 8 + i + 1)
                        gp_.dma_start(V_d[g * TB + tc128(i): g * TB + tc128(i) + 128,
                                          (i % 2) * 512:((i % 2) + 1) * 512],
                                      evV[i % 2][:, :]).then_inc(S_['vo'], 16)

        @block.vector
        def _(v):
            v.wait_ge(S_['ld'], 49)
            v.wait_ge(S_['wl'], 48)
            v.tensor_copy(nsrow_b[:, :], nsrow_f[:, :])
            stats_dve_pre(v, env1, 0)

            def rows_chain(g2):
                kind2 = blk_kind(g2)
                stats_dve_rows1(v, env1, g2)
                stats_dve_rows2(v, env1, g2)
                stats_dve_rows3(v, env1, g2, kind2)
                stats_dve_rows4(v, env1, g2)
            rows_chain(0)
            stats_dve_pre(v, env1, 1)
            for g in range(NGA):
                kind = blk_kind(g)
                stats_dve_h(v, env1, g, kind)
                nu = n_units(g)
                if g + 1 < NGA:
                    g2 = g + 1
                    hooks = {
                        nu // 8: (lambda g2=g2: stats_dve_rows1(v, env1, g2)),
                        nu // 4: (lambda g2=g2: stats_dve_rows2(v, env1, g2)),
                        3 * nu // 8: (lambda g2=g2: stats_dve_rows3(v, env1, g2,
                                                                    blk_kind(g2))),
                        nu // 2: (lambda g2=g2: stats_dve_rows4(v, env1, g2)),
                        3 * nu // 4: (lambda g2=g2: (stats_dve_pre(v, env1, g2 + 1)
                                                     if g2 + 1 < NGA else None)),
                    }
                else:
                    hooks = None
                stats_dve_evac(v, env1, g, kind, evK=evK, evV=evV, hooks=hooks)

        @block.scalar
        def _(sc):
            sc.wait_ge(S_['ld'], 1)
            for g in range(NGA):
                kind = blk_kind(g)
                stats_act_rows1(sc, env1, g)
                stats_act_rows2(sc, env1, g, kind)
                stats_act_rows3(sc, env1, g, kind)

        @block.tensor
        def _(t):
            t.wait_ge(S_['ld'], 49)
            t.wait_ge(S_['wl'], 48)
            for g in range(NGA):
                if g >= 1:
                    stats_pe_work_bcast(t, env1, g - 1)
                stats_pe_stats(t, env1, g)
                if g >= 1:
                    stats_pe_work(t, env1, g - 1, blk_kind(g - 1), wkt=wkt, wvt=wvt,
                                  wqt=wqt, split=True)
            stats_pe_work(t, env1, NGA - 1, blk_kind(NGA - 1), wkt=wkt, wvt=wvt, wqt=wqt)

    if rank < 2:
        ctx.close()
        return nc
    with ExitStack() as sc2:
        t2 = lambda *a: sc2.enter_context(nc.sbuf_tensor(*a))
        Kb = [t2("Kb%d" % i, [128, S], BF16) for i in range(2)]
        Vb = [t2("Vb%d" % i, [128, S], BF16) for i in range(2)]
        Qb = [t2("Qb%d" % i, [128, QOWN], BF16) for i in range(2)]
        NV = t2("NV", [128, NPAIR * QOWN], BF16)
        ptl = {0: [t2("pA%d" % i, [128, 1024], BF16) for i in range(2)],
               1: [t2("pB%d" % i, [128, 1024], BF16) for i in range(2)]}
        sums = t2("sums", [128, QT], F32)
        sums2 = t2("sums2", [128, QT], F32)
        sumsT = [sums, sums2]
        recs = t2("recs", [128, QT], F32)
        recs2 = t2("recs2", [128, QT], F32)
        recsT = [recs, recs2]
        recsP3 = [t2("recsP3%d" % i, [65, QT], F32) for i in range(2)]
        ksq = t2("ksq", [128, 2048], BF16)
        kqc = t2("kqc", [128, 24], F32)
        biasA = t2("biasA", [128, 1], F32)
        biasB = t2("biasB", [128, 1], F32)
        cneg = t2("cneg", [1, 2], BF16)
        taur = t2("taur", [128, 1], F32)
        tau_up = t2("tau_up", [128, 1], F32)
        tau_dn = t2("tau_dn", [128, 1], F32)
        twk = [t2("twk%d" % i, [128, 1024], F32) for i in range(3)]
        TWS = None
        tabs = t2("tabs", [128, 1024], F32)
        tmf = t2("tmf", [128, 1024], BF16)
        tout = [t2("tout%d" % i, [128, 1024], BF16) for i in range(2)]
        tacc = t2("tacc", [128, 64], F32)
        wot = t2("wot", [128, NDC * D], BF16)

        slots = [PS[0], PS[1], PS[2]]

        def tern_src(kind, k):
            if kind in ('u1', 'u2'):
                dc, q4 = k // 4, k % 4
                return wupT_d[dc * 128:(dc + 1) * 128, q4 * 1024:(q4 + 1) * 1024], 1024
            return wdnT_d[k * 128:(k + 1) * 128, :], 1024

        @block.sync
        def _(sy):
            for j in range(NPAIR):
                if j == 0:
                    sy.wait_ge(S_['eo'], 16 * KO[NGA])
                    sy.wait_ge(S_['vo'], 16 * VO[NGA])
                if j >= 2:
                    sy.wait_ge(S_['av'], NQT * NCP * (j - 1))
                p2s = S_['p2'] if j % 2 == 0 else S_['p2b']
                sy.dma_start(Kb[j % 2][:, :], KT_d[j * 128:(j + 1) * 128, :]
                             ).then_inc(p2s, 16)
                sy.dma_start(Qb[j % 2][:, :], QT_d[j * 128:(j + 1) * 128, :]
                             ).then_inc(p2s, 16)
                for i in range(8):
                    sy.dma_start(Vb[j % 2][:, i * 1024:(i + 1) * 1024]
                                 .rearrange("p (c e) -> p c e", c=8),
                                 V_d[i * 1024:(i + 1) * 1024, j * 128:(j + 1) * 128]
                                 .rearrange("(c p) e -> p c e", p=128)).then_inc(p2s, 16)

        @block.gpsimd
        def _(gp_):
            gp_.wait_ge(S_['eo'], 16 * KO[NGA])
            gp_.wait_ge(S_['vo'], 16 * VO[NGA])
            gp_.wait_ge(S_['ev'], UB[NGA])
            gp_.memset(recs[:, :], 0.0)
            gp_.memset(recs2[:, :], 0.0)
            gp_.memset(recsP3[0][:, :], 0.0)
            gp_.memset(recsP3[1][:, :], 0.0)
            gp_.memset(tacc[:, :], 0.0).then_inc(S_['tv'], 1)
            gp_.dma_start(wot[:, :].rearrange("p (c e) -> p c e", c=NDC), wo_d.rearrange("(c p) e -> p c e", c=NDC)
                          ).then_inc(S_['woc'], 16)
            def tern_out(gp_, i, kind, k):
                gp_.wait_ge(S_['tv'], 1 + i + 1)
                if kind == 'u2':
                    dc, q4 = k // 4, k % 4
                    r0 = dc * NFC + q4 * 8
                    dst = WUPt_d[r0:r0 + 8, :]
                else:
                    r0 = k * NDC
                    dst = WDNt_d[r0:r0 + 8, :]
                gp_.dma_start(dst.rearrange("f (p e) -> p f e", p=128),
                              tout[i % 2][:, 0:1024].rearrange("p (f e) -> p f e", f=8)
                              ).then_inc(S_['to'], 16)

            pend_out = [None]
            TWS = [S_['tw'], S_['tw1'], S_['tw2']]
            for i, (kind, k) in enumerate(TERN_UNITS):
                if i >= 3:
                    gp_.wait_ge(S_['tv'], 1 + i - 2)
                src, w = tern_src(kind, k)
                gp_.dma_start(twk[i % 3][:, 0:w], src).then_inc(TWS[i % 3], 16)
                if pend_out[0] is not None:
                    tern_out(gp_, *pend_out[0])
                    pend_out[0] = None
                if kind in ('u2', 'd2'):
                    pend_out[0] = (i, kind, k)
            tern_out(gp_, *pend_out[0])
            pend_out[0] = None

        def _avblock(t, j, qg, prev, u, first, drain=False, need_cb=False):
            t.wait_ge(S_['ex'], u if drain else u - 2)
            if first:
                t.wait_ge(S_['tl'], 2 * qg)
                if need_cb:
                    t.wait_ge(S_['cb'], j + 1)
            mm = None
            for ci in (0, 1):
                c = 2 * prev + ci
                fst = first and ci == 0
                lst = drain and ci == 1
                t.matmul(p_nv[0:64, :], Vb[j % 2][:, c * 128: c * 128 + 64],
                         ptl[0][prev % 2][:, ci * 512:(ci + 1) * 512],
                         start=fst, stop=lst, skip_group_check=True, tile_position=(0, 0))
                t.matmul(p_nv[64:128, :], Vb[j % 2][:, c * 128 + 64: c * 128 + 128],
                         ptl[1][prev % 2][:, ci * 512:(ci + 1) * 512],
                         start=fst, stop=lst, skip_group_check=True, tile_position=(0, 64))
                t.matmul(p_sum[0:1, :], ones_b[:, :],
                         ptl[0][prev % 2][:, ci * 512:(ci + 1) * 512],
                         start=fst, stop=lst, skip_group_check=True, tile_position=(0, 0))
                mm = t.matmul(p_sum[64:65, :], ones_b[:, :],
                              ptl[1][prev % 2][:, ci * 512:(ci + 1) * 512],
                              start=fst, stop=lst, skip_group_check=True, tile_position=(0, 64))
            mm.then_inc(S_['av'], 1)

        @block.tensor
        def _(t):
            u = 0
            t.wait_ge(S_['tv'], 1)
            for j in range(NPAIR):
                jq0 = j * NQT
                t.wait_ge(S_['p2'] if j % 2 == 0 else S_['p2b'], 160 * (j // 2 + 1))
                t.wait_ge(S_['tl'], 2 * jq0)
                for nm, pj in (('up', 2), ('dn', 4)):
                    if j == pj:
                        e = 0 if nm == 'up' else 1
                        t.wait_ge(S_['tau'], 5 * e + 1)
                        t.matmul(p_nv[0:1, 0:1], ones_f[:, :], taur[:, :],
                                 start=True, stop=True,
                                 skip_group_check=True).then_inc(S_['tau'], 1)
                        t.wait_ge(S_['tau'], 5 * e + 3)
                        t.matmul(p_nv[:, 2:3], onesr_f[:, :], kqc[0:1, 22:23],
                                 start=True, stop=True,
                                 skip_group_check=True).then_inc(S_['tau'], 1)
                        t.wait_ge(S_['tau'], 5 * e + 5)
                slK = slots[(128 * j + 2) % 3]
                fbK = [p_nv, p_sum, slK[:, 0:512], slK[:, 512:1024]]
                for f in range(20):
                    if f < 16:
                        qb, cg = f // 4, f % 4
                        t.wait_ge(S_['ksq'], 5 * j + qb + 1)
                        srcq = ksq
                    else:
                        cg = f - 16
                        t.wait_ge(S_['ksq'], 5 * j + 5)
                        srcq = ksq
                    if f >= 4:
                        t.wait_ge(S_['knr'], 11 * j + (f - 4) // 2 + 1)
                    bank = fbK[f % 4]
                    t.matmul(bank[0:64, :], selk[0:64, :], srcq[0:64, cg * 512:(cg + 1) * 512],
                             start=True, stop=True, skip_group_check=True, tile_position=(0, 0))
                    t.matmul(bank[64:128, :], selk[64:128, :],
                             srcq[64:128, cg * 512:(cg + 1) * 512],
                             start=True, stop=True, skip_group_check=True,
                             tile_position=(64, 64)).then_inc(S_['kn'], 1)
                t.wait_ge(S_['knr'], 11 * j + 11)
                t.matmul(p_nv[:, 0:1], onesr_b[:, :], cneg[0:1, 0:1],
                         start=True, stop=True, skip_group_check=True)
                t.matmul(p_nv[:, 1:2], onesr_b[:, :], cneg[0:1, 1:2],
                         start=True, stop=True, skip_group_check=True).then_inc(S_['kn'], 1)
                t.wait_ge(S_['knr'], 11 * j + 10)
                for qtr in range(NQT):
                    qg = jq0 + qtr
                    qc = qtr * QT
                    for cp in range(NCP):
                        for h_ in (0, 1):
                            slot = slots[(u + h_) % 3]
                            if u + h_ >= 3:
                                t.wait_ge(S_['ex'], u + h_ - 2)
                            base = 64 * h_
                            mm = None
                            for ci in (0, 1):
                                c = 2 * cp + ci
                                mm = t.matmul(slot[:, ci * 512:(ci + 1) * 512],
                                              Kb[j % 2][base:base + 64, c * 128:(c + 1) * 128],
                                              Qb[j % 2][base:base + 64, qc:qc + QT],
                                              start=True, stop=True, skip_group_check=True,
                                              tile_position=(base, 0))
                            mm.then_inc(S_['sc'], 1)
                        u += 2
                        if cp >= 1:
                            _avblock(t, j, qg, cp - 1, u, first=(cp == 1),
                                     need_cb=(qtr == 0))
                    _avblock(t, j, qg, NCP - 1, u, first=False, drain=True)

        def recs_store(eng, qs):
            js, qcs = qs // NQT, (qs % NQT) * QT
            eng.wait_ge(S_['bc'], qs + 1)
            eng.dma_start(recs_d[0:2, js * QOWN + qcs: js * QOWN + qcs + QT],
                          recsT[qs % 2][0:65:64, 0:QT]).then_inc(S_['rc'], 16)

        @block.scalar
        def _(sc):
            u = 0
            for j in range(NPAIR):
                sc.wait_ge(S_['cb'], j + 1)
                for qtr in range(NQT):
                    qg = j * NQT + qtr
                    for cp in range(NCP):
                        if cp == 12 and qg >= 1:
                            recs_store(sc, qg - 1)
                        for h_ in (0, 1):
                            sc.wait_ge(S_['sc'], u + 1)
                            guard = qg * NCP + cp - 1
                            if guard > 0:
                                sc.wait_ge(S_['av'], guard)
                            bias = biasA if h_ == 0 else biasB
                            sc.activation(ptl[h_][cp % 2][:, :], slots[u % 3][:, :], AF.Exp,
                                          bias=bias[:, 0:1], scale=1.0).then_inc(S_['ex'], 1)
                            u += 1



        @block.vector
        def _(v):
            pend_rec = [None]

            def emit_recs(v):
                if pend_rec[0] is None:
                    return
                qg = pend_rec[0]
                pend_rec[0] = None
                if qg >= 2:
                    v.wait_ge(S_['rc'], 16 * (qg - 1))
                v.reciprocal(recsT[qg % 2][0:1, :], sumsT[qg % 2][0:1, :])
                v.reciprocal(recsT[qg % 2][64:65, :], sumsT[qg % 2][64:65, :])
                padV(v, 'bc')
            for j in range(NPAIR):
                jq0 = j * NQT
                for nm, pj in (('up', 2), ('dn', 4)):
                    if j == pj:
                        e = 0 if nm == 'up' else 1
                        cols = (0, 32) if nm == 'up' else (32, 64)
                        v.wait_ge(S_['tv'], 1 + (32 if nm == 'up' else 64))
                        v.reduce_sum(taur[:, 0:1], tacc[:, cols[0]:cols[1]],
                                     axis=mybir.AxisListType.X)
                        padV(v, 'tau')
                        v.wait_ge(S_['tau'], 5 * e + 2)
                        v.tensor_single_scalar(kqc[0:1, 22:23], p_nv[0:1, 0:1], TAUF, ALU.mult)
                        padV(v, 'tau')
                        v.wait_ge(S_['tau'], 5 * e + 4)
                        dst = tau_up if nm == 'up' else tau_dn
                        v.tensor_copy(dst[:, :], p_nv[:, 2:3])
                        padV(v, 'tau')
                slK = slots[(128 * j + 2) % 3]
                fbK = [p_nv, p_sum, slK[:, 0:512], slK[:, 512:1024]]
                v.wait_ge(S_['p2'] if j % 2 == 0 else S_['p2b'], 160 * (j // 2 + 1))
                fb2 = [PS[3][:, 0:1024], slK[:, 0:1024]]
                for qb in range(4):
                    if qb >= 1:
                        v.wait_ge(S_['kn'], 21 * j + 4 * qb)
                    v.tensor_tensor(ksq[:, :], Kb[j % 2][:, qb * 2048:(qb + 1) * 2048],
                                    Kb[j % 2][:, qb * 2048:(qb + 1) * 2048],
                                    ALU.mult).then_inc(S_['ksq'], 1)
                    for r in range(2 * qb, 2 * qb + 2):
                        v.wait_ge(S_['kn'], 21 * j + 2 * r + 2)
                        v.reduce_max(kqc[:, r:r + 1], fb2[r % 2][:, :],
                                     axis=mybir.AxisListType.X).then_inc(S_['knr'], 1)
                v.wait_ge(S_['kn'], 21 * j + 16)
                v.tensor_tensor(ksq[:, :], Qb[j % 2][:, :], Qb[j % 2][:, :],
                                ALU.mult).then_inc(S_['ksq'], 1)
                for r in range(8, 10):
                    v.wait_ge(S_['kn'], 21 * j + 2 * r + 2)
                    v.reduce_max(kqc[:, r:r + 1], fb2[r % 2][:, :],
                                 axis=mybir.AxisListType.X).then_inc(S_['knr'], 1)
                v.wait_ge(S_['knr'], 11 * j + 10)
                v.reduce_max(kqc[:, 20:21], kqc[:, 0:8], axis=mybir.AxisListType.X)
                v.reduce_max(kqc[:, 21:22], kqc[:, 8:10], axis=mybir.AxisListType.X)
                v.tensor_tensor(kqc[:, 22:23], kqc[:, 20:21], kqc[:, 21:22], ALU.add)
                v.tensor_single_scalar(kqc[:, 22:23], kqc[:, 22:23], 0.5, ALU.mult)
                v.tensor_single_scalar(cneg[0:1, 0:1], kqc[0:1, 22:23], -1.0, ALU.mult)
                v.tensor_single_scalar(cneg[0:1, 1:2], kqc[64:65, 22:23], -1.0, ALU.mult)
                padV(v, 'knr')
                if j >= 1:
                    v.wait_ge(S_['ex'], 256 * j)
                v.wait_ge(S_['kn'], 21 * j + 21)
                v.tensor_copy(biasA[:, :], p_nv[:, 0:1])
                v.tensor_copy(biasB[:, :], p_nv[:, 1:2])
                padV(v, 'cb')
                emit_recs(v)
                i0 = sum(len(TERN_SCHED[q]) for q in range(j))
                tu = TERN_SCHED[j]
                for qtr in range(NQT):
                    qg = jq0 + qtr
                    qc = qtr * QT
                    emit_recs(v)
                    for ii in range(4 * qtr, 4 * qtr + 4):
                        if ii >= len(tu):
                            break
                        kind, k = tu[ii]
                        i = i0 + ii
                        tws = [S_['tw'], S_['tw1'], S_['tw2']][i % 3]
                        v.wait_ge(tws, 16 * (i // 3 + 1))
                        w = 1024
                        if kind in ('u1', 'd1'):
                            col = k if kind == 'u1' else 32 + k
                            v.reduce_sum(tacc[:, col:col + 1], twk[i % 3][:, 0:w],
                                         axis=mybir.AxisListType.X,
                                         apply_absolute_value=True).then_inc(S_['tv'], 1)
                        else:
                            tau_t = tau_up if kind == 'u2' else tau_dn
                            if TERN_OUT_BEFORE[i] > 0:
                                v.wait_ge(S_['to'], 16 * TERN_OUT_BEFORE[i])
                            v.scalar_tensor_tensor(tabs[:, 0:w], twk[i % 3][:, 0:w], -1.0,
                                                   twk[i % 3][:, 0:w], ALU.mult, ALU.max)
                            v.tensor_single_scalar(tabs[:, 0:w], tabs[:, 0:w],
                                                   tau_t[:, 0:1], ALU.subtract)
                            v.tensor_single_scalar(tmf[:, 0:w], tabs[:, 0:w], 0.0, ALU.is_gt)
                            v.tensor_single_scalar(tabs[:, 0:w], twk[i % 3][:, 0:w], 0.0,
                                                   ALU.is_gt)
                            v.tensor_scalar(tabs[:, 0:w], tabs[:, 0:w], 2.0, -1.0,
                                            ALU.mult, ALU.add)
                            v.tensor_tensor(tout[i % 2][:, 0:w], tabs[:, 0:w], tmf[:, 0:w],
                                            ALU.mult).then_inc(S_['tv'], 1)
                    v.wait_ge(S_['av'], (qg + 1) * NCP)
                    v.tensor_copy(sumsT[qg % 2][:, :], p_sum).then_inc(S_['tl'], 1)
                    v.tensor_copy(NV[:, j * QOWN + qc: j * QOWN + qc + QT],
                                  p_nv).then_inc(S_['tl'], 1)
                    pend_rec[0] = qg

        @block.vector
        def _(v):
            qg = NPAIR * NQT - 1
            v.wait_ge(S_['rc'], 16 * (qg - 1))
            v.reciprocal(recsT[qg % 2][0:1, :], sumsT[qg % 2][0:1, :])
            v.reciprocal(recsT[qg % 2][64:65, :], sumsT[qg % 2][64:65, :])
            padV(v, 'bc')

        if rank == 2:
            @block.sync
            def _(sy):
                sy.wait_ge(S_['tl'], 2 * NPAIR * NQT)
                sy.wait_ge(S_['to'], 16 * N_TERN_OUT)
                sy.dma_start(dbg4, NV[:, :]).then_inc(S_['out'], 16)
                sy.dma_start(dbg5[:, 0:1], tau_up[:, :], allow_slow_non_contiguous=True).then_inc(S_['out'], 16)
                sy.dma_start(dbg5[:, 1:2], tau_dn[:, :], allow_slow_non_contiguous=True).then_inc(S_['out'], 16)
                sy.dma_start(dbg5[:, 16:80], tacc[:, :]).then_inc(S_['out'], 16)
                sy.dma_start(dbg3[:, 0:1024], tout[0][:, 0:1024]).then_inc(S_['out'], 16)
                sy.dma_start(dbg1.rearrange("p (a q) -> p a q", a=64),
                             WUPt_d[0:64, :].rearrange("a (p q) -> p a q", p=128)
                             ).then_inc(S_['out'], 16)
                sy.dma_start(dbg5[:, 2048:3072], twk[0][:, :]).then_inc(S_['out'], 16)
                sy.dma_start(dbg5[:, 4096:6144],
                             wupT_d[0:128, 0:2048]).then_inc(S_['out'], 16)
                sy.wait_ge(S_['out'], 128)

        p3slots = [PS[0][:, 0:512], PS[0][:, 512:1024], PS[1][:, 0:512], PS[1][:, 512:1024]]

        if rank >= 3:

            nb3 = [PS[2][:, 0:512], PS[2][:, 512:1024], PS[3][:, 0:512], PS[3][:, 512:1024]]

            @block.sync
            def _(sy):
                sy.wait_ge(S_['rc'], 16 * (NPAIR * NQT - 1))
                for qg in range(4):
                    for jj in range(NPAIR):
                        bi = qg * NPAIR + jj
                        if jj == NPAIR - 1 and qg == 3:
                            continue
                        if bi >= 2:
                            sy.wait_ge(S_['nn'], bi - 1)
                        sy.dma_start(recsP3[bi % 2][0:65:64, :],
                                     recs_d[0:2, jj * QOWN + qg * QT: jj * QOWN + (qg + 1) * QT]
                                     ).then_inc(S_['rl'], 16)

            @block.tensor
            def _(t):
                t.wait_ge(S_['tl'], 2 * NPAIR * NQT)
                t.wait_ge(S_['woc'], 16)
                def p3bcast(t, qg2, jj):
                    bi = qg2 * NPAIR + jj
                    if bi >= 4:
                        t.wait_ge(S_['p3n'], bi - 3)
                    if bi == NPAIR * 4 - 1:
                        t.wait_ge(S_['bc'], NPAIR * NQT)
                        src_ = recsT[(NPAIR * 4 - 1) % 2][0:65, :]
                    else:
                        t.wait_ge(S_['rl'], 16 * min(bi + 2, NPAIR * 4 - 1))
                        src_ = recsP3[bi % 2][0:65, :]
                    t.matmul(nb3[bi % 4], sel[0:65, :], src_,
                             start=True, stop=True,
                             skip_group_check=True).then_inc(S_['nn'], 1)
                for jj in range(NPAIR):
                    p3bcast(t, 0, jj)
                for qg in range(4):
                    for dc in range(NDC):
                        idx = qg * NDC + dc
                        if idx >= 4:
                            t.wait_ge(S_['p3e'], idx - 3)
                        t.wait_ge(S_['p3n'], NPAIR * (qg + 1))
                        mm = None
                        for hc in range(NDC):
                            mm = t.matmul(p3slots[idx % 4],
                                          wot[:, hc * D + dc * 128: hc * D + (dc + 1) * 128],
                                          NV[:, hc * QOWN + qg * 512: hc * QOWN + (qg + 1) * 512],
                                          start=(hc == 0), stop=(hc == NDC - 1),
                                          skip_group_check=True)
                        mm.then_inc(S_['p3'], 1)
                        if qg < 3:
                            p3bcast(t, qg + 1, dc)

            @block.vector
            def _(v):
                def p3tt(v, qg2, jj):
                    bi = qg2 * NPAIR + jj
                    v.wait_ge(S_['nn'], bi + 1)
                    v.tensor_tensor(NV[:, jj * QOWN + qg2 * 512: jj * QOWN + (qg2 + 1) * 512],
                                    NV[:, jj * QOWN + qg2 * 512: jj * QOWN + (qg2 + 1) * 512],
                                    nb3[bi % 4], ALU.mult).then_inc(S_['p3n'], 1)
                for jj in range(NPAIR):
                    p3tt(v, 0, jj)
                for qg in range(4):
                    for dc in range(NDC):
                        idx = qg * NDC + dc
                        v.wait_ge(S_['p3'], idx + 1)
                        v.tensor_scalar(ST[:, dc * QOWN + qg * 512: dc * QOWN + (qg + 1) * 512],
                                        p3slots[idx % 4], bout_c[:, dc:dc + 1], None,
                                        ALU.add).then_inc(S_['p3e'], 1)
                        if qg < 3:
                            p3tt(v, qg + 1, dc)

    if rank == 3:
        @block.sync
        def _(sy):
            sy.wait_ge(S_['p3e'], 32)
            sy.dma_start(dbg4, ST[:, :]).then_inc(S_['out'], 16)
            sy.wait_ge(S_['out'], 16)
    if rank < 4:
        ctx.close()
        return nc
    with ExitStack() as sc4:
        t4 = lambda *a: sc4.enter_context(nc.sbuf_tensor(*a))
        xblk4s = t4("xblk4", [128, NDC * TB], F32)
        xblk4 = [xblk4s, xblk4s]
        xbf4 = t4("xbf4", [128, NDC * TB], BF16)
        xsq4 = t4("xsq4", [128, NDC * TB], BF16)
        rowA4 = t4("rowA4", [1, 6 * TB], F32)
        rowB4 = t4("rowB4", [1, 4 * TB], F32)
        rowC4 = t4("rowC4", [1, 4 * TB], BF16)
        HF = t4("HF", [128, NDC * QOWN], BF16)
        mid = t4("mid", [128, NFC * 512], BF16)
        Wub = [t4("Wub%d" % i, [128, NDC * 128], BF16) for i in range(4)]
        Wdb = [t4("Wdb%d" % i, [128, NFC * 128], BF16) for i in range(3)]
        WD3 = [3, 3, 2]
        a3s = t4("a3s", [128, 512], F32)
        gtile = t4("gtile", [128, 512], F32)
        gsil = t4("gsil", [128, 512], BF16)
        xfin = [t4("xfin%d" % i, [128, 512], F32) for i in range(2)]
        a3hist = None
        yout = [t4("yout", [128, 512], F32)]
        yout = [yout[0], yout[0]]
        hsc4 = t4("hsc4", [128, TB], BF16)
        env4 = dict(xblk=xblk4, xbf=xbf4, xsq=xsq4, rowA=rowA4, rowB=rowB4, rowC=rowC4,
                    hdst=HF, hT=None, hsc=hsc4, xd4base=GP4)

        NTB4 = 4

        def wo_count(tb, item, i):
            base = tb * (NFC + 5 * NDC)
            if item == 'wub':
                return 16 * (base + i + 1)
            if item == 'wdb':
                return 16 * (base + NFC + 5 * i + 4)
            return 16 * (base + NFC + 5 * i + 5)

        @block.sync
        def _(sy):
            sy.wait_ge(S_['p3e'], 32)
            for g in range(GP4, NG):
                off = (g - GP4) * TB
                for cc in range(NDC):
                    if g >= GP4 + 1:
                        sy.wait_ge(S_['hh'], 8 * (g - 1) + cc + 1)
                    sy.dma_start(xblk4[g % 2][:, cc * TB:(cc + 1) * TB],
                                 xTo[cc * 128:(cc + 1) * 128, off:off + TB]
                                 ).then_inc(S_['xd4'], 16)
            if not p4_full:
                sy.wait_ge(S_['hh'], 8 * NG)
                sy.dma_start(dbg4, ST[:, :]).then_inc(S_['out'], 16)
                sy.wait_ge(S_['out'], 16)
            for tb in range(NTB4 if p4_full else 0):
                if tb == 1 and rank == 5:
                    sy.wait_ge(S_['fa4'], 32)
                    sy.dma_start(dbgM, mid[:, :]).then_inc(S_['dmp'], 16)
                    sy.wait_ge(S_['dmp'], 16)
                for fc in range(NFC):
                    if fc >= 4:
                        sy.wait_ge(S_['f4'], tb * F4_PER + fc - 3)
                    elif tb >= 1:
                        sy.wait_ge(S_['f4'], (tb - 1) * F4_PER + 29 + fc)
                    if tb == 0 and fc == 0:
                        sy.wait_ge(S_['to'], 16 * N_TERN_OUT)
                    sy.dma_start(Wub[fc % 4][:, :].rearrange("p (c e) -> p c e", c=NDC),
                                 WUPt_d[fc::NFC, :].rearrange("c (p e) -> p c e", p=128)
                                 ).then_inc(S_['wu%d' % (fc % 4)], 16)
                for dc in range(NDC):
                    if dc >= 3:
                        sy.wait_ge(S_['f4'], tb * F4_PER + 31 + dc)
                    elif tb >= 1:
                        sy.wait_ge(S_['f4'], (tb - 1) * F4_PER + 39 + dc)
                    for k4 in range(4):
                        sy.dma_start(Wdb[dc % 3][:, k4 * 1024:(k4 + 1) * 1024]
                                     .rearrange("p (c e) -> p c e", c=8),
                                     WDNt_d[dc::NDC, :][k4 * 8:(k4 + 1) * 8, :]
                                     .rearrange("c (p e) -> p c e", p=128)
                                     ).then_inc(S_['wd%d' % (dc % 3)], 16)
                    if tb * NDC + dc - 2 >= 0:
                        sy.wait_ge(S_['fy'], tb * NDC + dc - 1)
                    sy.dma_start(xfin[dc % 2][:, :],
                                 xTo[dc * 128:(dc + 1) * 128, tb * 512:(tb + 1) * 512]
                                 ).then_inc(S_['wx'], 16)
            if p4_full:
                sy.wait_ge(S_['out'], 16 * NTB4 * NDC)
            if rank == 5:
                sy.dma_start(dbg1[:, 0:1024].rearrange("p (a q) -> p a q", a=8),
                             WUPt_d[0:8, :].rearrange("a (p q) -> p a q", p=128)
                             ).then_inc(S_['out'], 16)
                sy.dma_start(dbg2[:, 0:1024].rearrange("p (a q) -> p a q", a=8),
                             WDNt_d[0:8, :].rearrange("a (p q) -> p a q", p=128)
                             ).then_inc(S_['out'], 16)
                sy.dma_start(dbg4, ST[:, :]).then_inc(S_['out'], 16)
                sy.dma_start(dbg5[:, 0:512], a3s[:, :]).then_inc(S_['out'], 16)

                sy.dma_start(dbg5[0:1, 1024:1536], rowA4[0:1, 512:1024],
                             allow_slow_non_contiguous=True).then_inc(S_['out'], 16)
                sy.dma_start(dbg5[0:1, 1536:2048], rowA4[0:1, 1024:1536],
                             allow_slow_non_contiguous=True).then_inc(S_['out'], 16)
                sy.dma_start(dbg5[:, 512:1024], gtile[:, :]).then_inc(S_['out'], 16)
                sy.dma_start(dbg3[:, 1024:1536], mid[:, 0:512]).then_inc(S_['out'], 16)
                sy.dma_start(dbgH, HF[:, :]).then_inc(S_['out'], 16)
                sy.wait_ge(S_['out'], 16 * (NTB4 * NDC + 9))

        def mid_sq_view(fc):
            return xsq4[:, (fc % NDC) * 512:(fc % NDC) * 512 + 512]

        def _summ(t, tb, fc):
            t.wait_ge(S_['fe4'], tb * FE4_PER + fc + 1)
            t.matmul(PS[3][0:1, 0:512], ones_b[:, :], mid_sq_view(fc),
                     start=(fc == 0), stop=(fc == NFC - 1),
                     skip_group_check=True).then_inc(S_['fs'], 1)

        @block.gpsimd
        def _(gp_):
            for tb in range(NTB4 if p4_full else 0):
                for dc in range(NDC):
                    gp_.wait_ge(S_['fy'], tb * NDC + dc + 1)
                    gp_.dma_start(yT[dc * 128:(dc + 1) * 128, tb * 512:(tb + 1) * 512],
                                  yout[0][:, :]).then_inc(S_['out'], 16)

        @block.tensor
        def _(t):
            for g in range(GP4, NG):
                if g > GP4:
                    stats_pe_work(t, env4, g - 1, 'p4')
                stats_pe_stats(t, env4, g)
            stats_pe_work(t, env4, NG - 1, 'p4')
            for tb in range(NTB4 if p4_full else 0):
                for fc in range(NFC):
                    t.wait_ge(S_['wu%d' % (fc % 4)], 16 * (8 * tb + fc // 4 + 1))
                    if tb == 0 and fc == 0:
                        t.wait_ge(S_['hh'], 8 * NG)
                        t.wait_ge(S_['sa'], 3 * (NG - 1) + 1)
                    if fc >= 3:
                        t.wait_ge(S_['fa4'], tb * FA4_PER + fc - 2)
                    elif tb >= 1:
                        t.wait_ge(S_['fe4'], tb * FE4_PER)
                    mm = None
                    for dc in range(NDC):
                        mm = t.matmul(PS[fc % 3][:, 0:512],
                                      Wub[fc % 4][:, dc * 128:(dc + 1) * 128],
                                      HF[:, dc * QOWN + tb * 512: dc * QOWN + tb * 512 + 512],
                                      start=(dc == 0), stop=(dc == NDC - 1),
                                      skip_group_check=True)
                    mm.then_inc(S_['f4'], 1)
                    if fc >= 2:
                        _summ(t, tb, fc - 2)
                _summ(t, tb, NFC - 2)
                _summ(t, tb, NFC - 1)
                t.wait_ge(S_['fe4'], tb * FE4_PER + 33)
                t.matmul(PS[3][:, 512:1024], onesr_b[:, :], rowC4[0:1, 0:512],
                         start=True, stop=False, skip_group_check=True)
                t.matmul(PS[3][:, 512:1024], onesr_b[:, :], rowC4[0:1, 512:1024],
                         start=False, stop=True, skip_group_check=True).then_inc(S_['f4'], 1)
                for dc in range(NDC):
                    t.wait_ge(S_['fa4'], tb * FA4_PER + 32)
                    if dc == 0:
                        t.wait_ge(S_['fe4'], tb * FE4_PER + 32)
                    t.wait_ge(S_['wd%d' % (dc % 3)],
                              16 * 4 * (WD3[dc % 3] * tb + dc // 3 + 1))
                    if dc >= 2:
                        t.wait_ge(S_['fe4'], tb * FE4_PER + 35 + dc - 2)
                    mm = None
                    for fc in range(NFC):
                        mm = t.matmul(PS[dc % 2][:, 0:512],
                                      Wdb[dc % 3][:, fc * 128:(fc + 1) * 128],
                                      mid[:, fc * 512:(fc + 1) * 512],
                                      start=(fc == 0), stop=(fc == NFC - 1),
                                      skip_group_check=True)
                    mm.then_inc(S_['f4'], 1)

        @block.scalar
        def _(sc):
            for g in range(GP4, NG):
                stats_act_rows1(sc, env4, g)
                stats_act_rows2(sc, env4, g, 'p4')
                stats_act_rows3(sc, env4, g, 'p4')
            for tb in range(NTB4 if p4_full else 0):
                for fc in range(NFC):
                    sc.wait_ge(S_['f4'], tb * F4_PER + fc + 1)
                    sc.activation(mid[:, fc * 512:(fc + 1) * 512], PS[fc % 3][:, 0:512],
                                  AF.Gelu).then_inc(S_['fa4'], 1)
                sc.wait_ge(S_['fs'], (tb + 1) * FS_PER)
                sc.activation(rowA4[0:1, 0:512], PS[3][0:1, 0:512], AF.Ln, scale=1.0 / FF)
                sc.activation(rowA4[0:1, 1024:1536], rowA4[0:1, 0:512], AF.Exp, scale=0.5)
                sc.activation(rowA4[0:1, 1536:2048], rowA4[0:1, 1024:1536], AF.Identity,
                              bias=eps_t[0:1, 0:1])
                sc.activation(rowA4[0:1, 0:512], rowA4[0:1, 1536:2048], AF.Ln)
                sc.activation(rowA4[0:1, 1024:1536], rowA4[0:1, 0:512], AF.Exp,
                              scale=-1.0)
                padA(sc, 'fa4')
                for dc in range(NDC):
                    sc.wait_ge(S_['fe4'], tb * FE4_PER + 35 + dc)
                    sc.activation(gsil[:, :], gtile[:, :], AF.Silu)
                    padA(sc, 'fa4')

        @block.vector
        def _(v):
            stats_dve_pre(v, env4, GP4)
            for g in range(GP4, NG):
                stats_dve_rows1(v, env4, g)
                stats_dve_rows2(v, env4, g)
                stats_dve_rows3(v, env4, g, 'p4')
                stats_dve_rows4(v, env4, g)
                stats_dve_h(v, env4, g, 'p4')
                if g + 1 < NG:
                    stats_dve_pre(v, env4, g + 1)
            for tb in range(NTB4 if p4_full else 0):
                for fc in range(NFC):
                    v.wait_ge(S_['fa4'], tb * FA4_PER + fc + 1)
                    if fc >= NDC:
                        v.wait_ge(S_['fs'], tb * FS_PER + fc - NDC + 1)
                    v.tensor_tensor(mid_sq_view(fc), mid[:, fc * 512:(fc + 1) * 512],
                                    mid[:, fc * 512:(fc + 1) * 512],
                                    ALU.mult).then_inc(S_['fe4'], 1)
                v.wait_ge(S_['fa4'], tb * FA4_PER + 33)
                v.tensor_copy(rowC4[0:1, 0:512], rowA4[0:1, 1024:1536])
                v.tensor_tensor(rowC4[0:1, 512:1024], rowA4[0:1, 1024:1536], rowC4[0:1, 0:512],
                                ALU.subtract)
                padV(v, 'fe4')
                v.wait_ge(S_['f4'], tb * F4_PER + 33)

                v.tensor_copy(a3s[:, :], PS[3][:, 512:1024]).then_inc(S_['fe4'], 1)
                for dc in range(NDC):
                    v.wait_ge(S_['f4'], tb * F4_PER + 33 + dc + 1)
                    v.tensor_tensor(gtile[:, :], PS[dc % 2][:, 0:512], a3s[:, :],
                                    ALU.mult)
                    padV(v, 'fe4')
                    v.wait_ge(S_['fa4'], tb * FA4_PER + 34 + dc)
                    v.wait_ge(S_['wx'], 16 * (tb * NDC + dc + 1))
                    yi = tb * NDC + dc
                    if yi >= 1:
                        v.wait_ge(S_['out'], 16 * yi)
                    v.tensor_tensor(gsil[:, :],
                                    ST[:, dc * QOWN + tb * 512: dc * QOWN + (tb + 1) * 512],
                                    gsil[:, :], ALU.mult)
                    v.tensor_tensor(yout[0][:, :], xfin[dc % 2][:, :], gsil[:, :],
                                    ALU.add)
                    padV(v, 'fy')
    ctx.close()
    return nc


def kernel(L0_attn_x, L0_attn_norm_scale, L0_attn_wq, L0_attn_wk, L0_attn_wv,
           L0_attn_w_out, L0_attn_b_out, L0_ffn_w_up, L0_ffn_w_down):
    x = np.asarray(L0_attn_x, dtype=np.float32)
    wq = np.ascontiguousarray(np.asarray(L0_attn_wq, np.float32).reshape(D, D))
    wk = np.ascontiguousarray(np.asarray(L0_attn_wk, np.float32).reshape(D, D))
    wv = np.ascontiguousarray(np.asarray(L0_attn_wv, np.float32).reshape(D, D))
    wo = np.ascontiguousarray(np.asarray(L0_attn_w_out, np.float32).reshape(D, D))
    nsrow = np.ascontiguousarray(np.asarray(L0_attn_norm_scale, np.float32).reshape(1, D))
    boutc = np.ascontiguousarray(np.asarray(L0_attn_b_out, np.float32).reshape(NDC, 128).T)
    wupT = np.ascontiguousarray(np.asarray(L0_ffn_w_up, np.float32).T)
    wdnT = np.ascontiguousarray(np.asarray(L0_ffn_w_down, np.float32).T)
    xTb = [np.ascontiguousarray(x[b].T) for b in range(2)]

    nc = build_program()
    in_maps = []
    for c in range(8):
        b, qi = c // 4, c % 4
        xTo = np.ascontiguousarray(x[b, qi * QOWN:(qi + 1) * QOWN, :].T)
        in_maps.append({"xT": xTb[b], "xTo": xTo, "wq": wq, "wk": wk, "wv": wv, "wo": wo,
                        "nsr": nsrow, "boc": boutc, "wupT": wupT, "wdnT": wdnT})
    res = run_bass_kernel_spmd(nc, in_maps, core_ids=list(range(8)))
    y = np.empty((2, S, D), dtype=np.float32)
    for c in range(8):
        b, qi = c // 4, c % 4
        y[b, qi * QOWN:(qi + 1) * QOWN, :] = res.results[c]["yT"].T
    return y


# version 21
# speedup vs baseline: 1.2117x; 1.2117x over previous
import numpy as np
from contextlib import ExitStack

import concourse.bass as bass
import concourse.mybir as mybir
from concourse.bass_utils import run_bass_kernel_spmd

F32 = mybir.dt.float32
BF16 = mybir.dt.bfloat16
AF = mybir.ActivationFunctionType
ALU = mybir.AluOpType

D, S, QOWN = 1024, 8192, 2048
NDC, NPAIR, NCH, NCP, NQT, QT = 8, 8, 64, 32, 4, 512
FF, NFC, TB = 4096, 32, 512
NB1 = S // TB
NB2 = QOWN // TB
GP1B, GP4 = NB1, NB1
NG = GP4 + NB2
NGA = GP4
RT2 = 0.35355339059327373
INV_D = 1.0 / D
TAUF = 0.5 / (D * FF)

TERN_SCHED = {
    0: [('u1', k) for k in range(16)],
    1: [('u1', k) for k in range(16, 32)],
    2: [('d1', k) for k in range(16)],
    3: [('d1', k) for k in range(16, 32)],
    4: [('u2', k) for k in range(16)],
    5: [('u2', k) for k in range(16, 32)],
    6: [('d2', k) for k in range(16)],
    7: [('d2', k) for k in range(16, 32)],
}
TERN_UNITS = [u for p in range(8) for u in TERN_SCHED[p]]


def tern_out_dmas(kind):
    return 1 if kind in ('u2', 'd2') else 0


TERN_OUT_BEFORE = [0]
for _u in TERN_UNITS:
    TERN_OUT_BEFORE.append(TERN_OUT_BEFORE[-1] + tern_out_dmas(_u[0]))
N_TERN_OUT = TERN_OUT_BEFORE[-1]

F4_PER = 41
FA4_PER = 41
FE4_PER = 42
FS_PER = 32


def blk_kind(g):
    return 'p1' if g < GP1B else 'p4'


def n_units(g):
    if blk_kind(g) == 'p4':
        return 0
    return 24 if g < NB2 else 16


UB = [0] * (NG + 1)
KO = [0] * (NG + 1)
VO = [0] * (NG + 1)
for g in range(NG):
    UB[g + 1] = UB[g] + n_units(g)
    KO[g + 1] = KO[g] + (0 if blk_kind(g) == 'p4' else (16 if g < NB2 else 8))
    VO[g + 1] = VO[g] + (8 if blk_kind(g) == 'p1' else 0)


def tc128(i):
    return (i // 2) * 128


def build_program(stop='full'):
    rank = {'p1': 1, 'p2': 2, 'p3': 3, 'full': 4, 'dbg': 5, 'p4s': 6}[stop]
    p4_full = rank != 6
    nc = bass.Bass("TRN2", target_bir_lowering=False)

    xT = nc.dram_tensor("xT", [D, S], F32, kind="ExternalInput").ap()
    xTo = nc.dram_tensor("xTo", [D, QOWN], F32, kind="ExternalInput").ap()
    wq_d = nc.dram_tensor("wq", [D, D], F32, kind="ExternalInput").ap()
    wk_d = nc.dram_tensor("wk", [D, D], F32, kind="ExternalInput").ap()
    wv_d = nc.dram_tensor("wv", [D, D], F32, kind="ExternalInput").ap()
    wo_d = nc.dram_tensor("wo", [D, D], F32, kind="ExternalInput").ap()
    nsr = nc.dram_tensor("nsr", [1, D], F32, kind="ExternalInput").ap()
    boc = nc.dram_tensor("boc", [128, NDC], F32, kind="ExternalInput").ap()
    wupT_d = nc.dram_tensor("wupT", [D, FF], F32, kind="ExternalInput").ap()
    wdnT_d = nc.dram_tensor("wdnT", [FF, D], F32, kind="ExternalInput").ap()
    yT = nc.dram_tensor("yT", [D, QOWN], F32, kind="ExternalOutput").ap()
    dbg1 = nc.dram_tensor("dbg1", [128, S], BF16, kind="ExternalOutput").ap()
    dbg2 = nc.dram_tensor("dbg2", [128, S], BF16, kind="ExternalOutput").ap()
    dbg3 = nc.dram_tensor("dbg3", [128, QOWN], BF16, kind="ExternalOutput").ap()
    dbg4 = nc.dram_tensor("dbg4", [128, NPAIR * QOWN], BF16, kind="ExternalOutput").ap()
    dbg5 = nc.dram_tensor("dbg5", [128, 6144], F32, kind="ExternalOutput").ap()
    dbgH = nc.dram_tensor("dbgH", [128, 16384], BF16, kind="ExternalOutput").ap()
    dbgM = nc.dram_tensor("dbgM", [128, 16384], BF16, kind="ExternalOutput").ap()

    KT_d = nc.dram_tensor("KT_d", [D, S], BF16).ap()
    V_d = nc.dram_tensor("V_d", [S, D], BF16).ap()
    QT_d = nc.dram_tensor("QT_d", [D, QOWN], BF16).ap()
    WUPt_d = nc.dram_tensor("WUPt_d", [NDC * NFC, 128 * 128], BF16).ap()
    WDNt_d = nc.dram_tensor("WDNt_d", [NFC * NDC, 128 * 128], BF16).ap()
    recs_d = nc.dram_tensor("recs_d", [2, 8 * 2048], F32).ap()

    ctx = ExitStack()
    ten = lambda *a: ctx.enter_context(nc.sbuf_tensor(*a))
    psu = lambda *a: ctx.enter_context(nc.psum_tensor(*a))
    S_ = {}
    for n in ("ld wl woc xd xq stm sa sb ab pbc hh kv ev eo vo p2 ksq kn knr cb tau sc ex av tl "
              "bc tw tv to wo p3 p3e f4 fs fa4 fe4 fy out dmp wd wx rc rl nn p3n p2b wu0 wu1 wu2 wu3 wd0 wd1 wd2 xd4 tw1 tw2 tw3").split():
        S_[n] = ctx.enter_context(nc.semaphore("s_" + n))

    ones_b = ten("ones_b", [128, 1], BF16)
    onesr_b = ten("onesr_b", [1, 128], BF16)
    onesr_f = ten("onesr_f", [1, 128], F32)
    ones_f = ten("ones_f", [128, 1], F32)
    sel = ten("sel", [128, 128], F32)
    selk = ten("selk", [128, 64], BF16)
    nsrow_b = ten("nsrow_b", [1, D], BF16)
    nscol = ten("nscol", [128, NDC], F32)
    nsrow_f = ten("nsrow_f", [1, D], F32)
    bout_c = ten("bout_c", [128, NDC], F32)
    eps_ln = ten("eps_ln", [1, 1], F32)
    scr1 = ten("scr1", [1, 1], F32)
    scrV = ten("scrV", [1, 1], F32)
    eps_rms = ten("eps_rms", [1, 1], F32)
    eps_t = ten("eps_t", [1, 1], F32)
    eps_f = ten("eps_f", [1, 1], F32)
    ST = ten("ST", [128, NDC * QOWN], BF16)

    PS = [psu("ps%d" % i, [128, 1024], F32) for i in range(4)]
    pA = [PS[0][:, 0:512], PS[1][:, 0:512]]
    pB = [PS[0][:, 512:1024], PS[1][:, 512:1024]]
    pKV = [PS[2][:, 0:512], PS[2][:, 512:1024]]
    pS1 = PS[3][0:1, 0:512]
    pS2 = PS[3][0:1, 512:1024]
    p_nv = PS[3][:, 0:512]
    p_sum = PS[3][:, 512:1024]

    block = ctx.enter_context(nc.Block())

    def padA(sc, sem, n=1):
        sc.activation(scr1[0:1, 0:1], scr1[0:1, 0:1], AF.Identity).then_inc(S_[sem], n)

    def padV(v, sem, n=1):
        v.tensor_copy(scrV[0:1, 0:1], scrV[0:1, 0:1]).then_inc(S_[sem], n)

    def RS(env, w, i):
        return env[w][0:1, i * TB:(i + 1) * TB]

    def stats_pe_stats(t, env, g):
        t.wait_ge(S_['xq'], g + 1)
        if g >= 1:
            t.wait_ge(S_['sa'], 3 * (g - 1) + 1)
        for c in range(NDC):
            t.matmul(pS1, ones_b[:, :], env['xbf'][:, c * TB:(c + 1) * TB],
                     start=(c == 0), stop=(c == NDC - 1), skip_group_check=True)
        mm = None
        for c in range(NDC):
            mm = t.matmul(pS2, ones_b[:, :], env['xsq'][:, c * TB:(c + 1) * TB],
                          start=(c == 0), stop=(c == NDC - 1), skip_group_check=True)
        mm.then_inc(S_['stm'], 1)

    def stats_dve_pre(v, env, g):
        if 'xd4base' in env:
            v.wait_ge(S_['xd4'], 128 * (g - env['xd4base'] + 1))
        else:
            v.wait_ge(S_['xd'], 16 * (g + 1))
        v.tensor_copy(env['xbf'][:, :], env['xblk'][g % 2][:, :])
        v.tensor_tensor(env['xsq'][:, :], env['xbf'][:, :], env['xbf'][:, :],
                        ALU.mult).then_inc(S_['xq'], 1)

    def stats_dve_rows1(v, env, g, n=1):
        v.wait_ge(S_['stm'], g + 1)
        v.tensor_single_scalar(RS(env, 'rowA', 0), pS1, INV_D, ALU.mult)
        v.tensor_single_scalar(RS(env, 'rowA', 5), pS2, INV_D, ALU.mult)
        v.tensor_tensor(RS(env, 'rowB', 0), RS(env, 'rowA', 0), RS(env, 'rowA', 0), ALU.mult)
        v.tensor_tensor(RS(env, 'rowA', 1), RS(env, 'rowA', 5), RS(env, 'rowB', 0),
                        ALU.subtract)
        padV(v, 'sa', n)

    def stats_act_fused(sc, env, g):
        sc.wait_ge(S_['sa'], 3 * g + 3)
        sc.activation(RS(env, 'rowA', 2), RS(env, 'rowA', 1), AF.Ln,
                      scale=1.0 + 1e-6, bias=eps_f[0:1, 0:1])
        sc.activation(RS(env, 'rowA', 4), RS(env, 'rowA', 2), AF.Exp, scale=-0.5)
        padA(sc, 'sb', 3)

    def stats_act_rows1(sc, env, g):
        sc.wait_ge(S_['sa'], 3 * g + 1)
        sc.activation(RS(env, 'rowA', 2), RS(env, 'rowA', 1), AF.Ln, bias=eps_ln[0:1, 0:1])
        sc.activation(RS(env, 'rowB', 1), RS(env, 'rowA', 2), AF.Exp, scale=-1.0)
        padA(sc, 'sb')

    def stats_dve_rows2(v, env, g):
        v.wait_ge(S_['sb'], 3 * g + 1)
        v.tensor_tensor(RS(env, 'rowA', 3), RS(env, 'rowA', 1), RS(env, 'rowB', 1),
                        ALU.mult)
        padV(v, 'sa')

    def stats_act_rows2(sc, env, g, kind):
        sc.wait_ge(S_['sa'], 3 * g + 2)
        if kind != 'p4':
            sc.activation(RS(env, 'rowB', 0), RS(env, 'rowA', 3), AF.Ln,
                          bias=eps_rms[0:1, 0:1])
            padA(sc, 'sb')
        else:
            sc.activation(RS(env, 'rowB', 0), RS(env, 'rowA', 3), AF.Ln)
            sc.activation(RS(env, 'rowB', 1), RS(env, 'rowB', 0), AF.Exp, scale=0.5)
            sc.activation(RS(env, 'rowB', 3), RS(env, 'rowA', 2), AF.Exp, scale=0.5)
            padA(sc, 'sb')

    def stats_dve_rows3(v, env, g, kind):
        v.wait_ge(S_['sb'], 3 * g + 2)
        if kind != 'p4':
            v.tensor_tensor(RS(env, 'rowB', 0), RS(env, 'rowA', 2), RS(env, 'rowB', 0),
                            ALU.add)
            padV(v, 'sa')
        else:
            v.tensor_single_scalar(RS(env, 'rowB', 0), RS(env, 'rowB', 1), 1e-8, ALU.add)
            v.tensor_tensor(RS(env, 'rowB', 0), RS(env, 'rowB', 3), RS(env, 'rowB', 0),
                            ALU.mult)
            padV(v, 'sa')

    def stats_act_rows3(sc, env, g, kind):
        sc.wait_ge(S_['sa'], 3 * g + 3)
        if kind != 'p4':
            sc.activation(RS(env, 'rowA', 4), RS(env, 'rowB', 0), AF.Exp, scale=-0.5)
            padA(sc, 'sb')
        else:
            sc.activation(RS(env, 'rowB', 3), RS(env, 'rowB', 0), AF.Ln)
            sc.activation(RS(env, 'rowA', 4), RS(env, 'rowB', 3), AF.Exp, scale=-1.0)
            padA(sc, 'sb')

    def stats_dve_rows4(v, env, g):
        v.wait_ge(S_['sb'], 3 * g + 3)
        v.tensor_tensor(RS(env, 'rowB', 2), RS(env, 'rowA', 0), RS(env, 'rowA', 4), ALU.mult)
        v.tensor_copy(RS(env, 'rowC', 0), RS(env, 'rowA', 4))
        v.tensor_tensor(RS(env, 'rowC', 1), RS(env, 'rowA', 4), RS(env, 'rowC', 0), ALU.subtract)
        v.tensor_scalar(RS(env, 'rowC', 2), RS(env, 'rowB', 2), -1.0, None, ALU.mult)
        v.scalar_tensor_tensor(RS(env, 'rowC', 3), RS(env, 'rowB', 2), -1.0,
                               RS(env, 'rowC', 2), ALU.mult, ALU.subtract)
        padV(v, 'ab')

    def stats_pe_work_bcast(t, env, g):
        t.wait_ge(S_['ab'], g + 1)
        if g >= 2:
            t.wait_ge(S_['hh'], 8 * (g - 1))
        ls = onesr_b[0:1, 0:128]
        t.matmul(pA[g % 2], ls, RS(env, 'rowC', 0), start=True, stop=False,
                 skip_group_check=True)
        t.matmul(pA[g % 2], ls, RS(env, 'rowC', 1), start=False, stop=True,
                 skip_group_check=True)
        t.matmul(pB[g % 2], ls, RS(env, 'rowC', 2), start=True, stop=False,
                 skip_group_check=True)
        t.matmul(pB[g % 2], ls, RS(env, 'rowC', 3), start=False, stop=True,
                 skip_group_check=True).then_inc(S_['pbc'], 1)

    def stats_pe_work(t, env, g, kind, wkt=None, wvt=None, wqt=None, split=False):
        if not split:
            stats_pe_work_bcast(t, env, g)
        nu = n_units(g)
        if nu == 0:
            return
        t.wait_ge(S_['hh'], 8 * (g + 1))
        for u in range(nu):
            if UB[g] + u - 1 >= 1:
                t.wait_ge(S_['ev'], UB[g] + u - 1)
            mm = None
            if u < 8:
                for dc in range(NDC):
                    mm = t.matmul(pKV[u % 2], wkt[:, dc * D + u * 128: dc * D + (u + 1) * 128],
                                  env['hT'][g % 2][:, dc * TB:(dc + 1) * TB],
                                  start=(dc == 0), stop=(dc == NDC - 1), skip_group_check=True)
            elif u < 16:
                i = u - 8
                tc, half = i // 2, i % 2
                for dc in range(NDC):
                    mm = t.matmul(pKV[u % 2],
                                  env['hT'][g % 2][:, dc * TB + tc * 128: dc * TB + (tc + 1) * 128],
                                  wvt[:, dc * D + half * 512: dc * D + (half + 1) * 512],
                                  start=(dc == 0), stop=(dc == NDC - 1), skip_group_check=True)
            else:
                uq = u - 16
                for dc in range(NDC):
                    mm = t.matmul(pKV[u % 2], wqt[:, dc * D + uq * 128: dc * D + (uq + 1) * 128],
                                  env['hT'][g % 2][:, dc * TB:(dc + 1) * TB],
                                  start=(dc == 0), stop=(dc == NDC - 1), skip_group_check=True)
            mm.then_inc(S_['kv'], 1)

    def stats_dve_h(v, env, g, kind):
        for c in range(NDC):
            if c == 0:
                v.wait_ge(S_['pbc'], g + 1)
            if kind == 'p4':
                v.tensor_tensor(env['hsc'][:, :],
                                env['xblk'][g % 2][:, c * TB:(c + 1) * TB], pA[g % 2],
                                ALU.mult)
                boff = (g - GP4) * TB
                dst = env['hdst'][:, c * QOWN + boff: c * QOWN + boff + TB]
                v.tensor_tensor(dst, env['hsc'][:, :], pB[g % 2],
                                ALU.add).then_inc(S_['hh'], 1)
            else:
                v.scalar_tensor_tensor(env['hsc'][:, :],
                                       env['xblk'][g % 2][:, c * TB:(c + 1) * TB],
                                       nscol[:, c:c + 1], pA[g % 2], ALU.mult, ALU.mult)
                dst = env['hT'][g % 2][:, c * TB:(c + 1) * TB]
                v.scalar_tensor_tensor(dst, pB[g % 2], nscol[:, c:c + 1],
                                       env['hsc'][:, :],
                                       ALU.mult, ALU.add).then_inc(S_['hh'], 1)

    def stats_dve_evac(v, env, g, kind, evK=None, evV=None, hooks=None):
        for u in range(n_units(g)):
            if hooks and u in hooks:
                hooks[u]()
            v.wait_ge(S_['kv'], UB[g] + u + 1)
            if 8 <= u < 16:
                i = u - 8
                if VO[g] + i - 2 >= 0:
                    v.wait_ge(S_['vo'], 16 * (VO[g] + i - 1))
                v.tensor_copy(evV[i % 2][:, :], pKV[u % 2]).then_inc(S_['ev'], 1)
            else:
                ko = u if u < 8 else 8 + (u - 16)
                if KO[g] + ko - 2 >= 0:
                    v.wait_ge(S_['eo'], 16 * (KO[g] + ko - 1))
                v.tensor_single_scalar(evK[u % 2][:, :], pKV[u % 2], RT2,
                                       ALU.mult).then_inc(S_['ev'], 1)

    with ExitStack() as sc1:
        t1 = lambda *a: sc1.enter_context(nc.sbuf_tensor(*a))
        xblk = [t1("xblk%d" % i, [128, NDC * TB], F32) for i in range(2)]
        xbf = t1("xbf", [128, NDC * TB], BF16)
        xsq = t1("xsq", [128, NDC * TB], BF16)
        hT = [t1("hT%d" % i, [128, NDC * TB], BF16) for i in range(2)]
        evK = [t1("evK%d" % i, [128, TB], BF16) for i in range(2)]
        evV = [t1("evV%d" % i, [128, TB], BF16) for i in range(2)]
        wqt = t1("wqt", [128, NDC * D], BF16)
        wkt = t1("wkt", [128, NDC * D], BF16)
        wvt = t1("wvt", [128, NDC * D], BF16)
        rowA = t1("rowA", [1, 6 * TB], F32)
        rowB = t1("rowB", [1, 4 * TB], F32)
        rowC = t1("rowC", [1, 4 * TB], BF16)
        hsc = t1("hsc", [128, TB], BF16)
        env1 = dict(xblk=xblk, xbf=xbf, xsq=xsq, hT=hT, rowA=rowA, rowB=rowB, rowC=rowC,
                    hsc=hsc)

        def blk_src(g):
            return xT, g * TB

        @block.sync
        def _(sy):
            sy.dma_start(nsrow_f[:, :], nsr).then_inc(S_['ld'], 16)
            sy.dma_start(nscol[:, :], nsr.rearrange("a (c p) -> p (a c)", p=128),
                         allow_slow_non_contiguous=True).then_inc(S_['ld'], 16)
            sy.dma_start(bout_c[:, :], boc).then_inc(S_['ld'], 16)
            for g in range(NGA):
                src, off = blk_src(g)
                if g >= 2:
                    sy.wait_ge(S_['hh'], 8 * (g - 1))
                sy.dma_start(xblk[g % 2][:, :].rearrange("p (c t) -> p c t", c=NDC),
                             src[:, off:off + TB].rearrange("(c p) t -> p c t", c=NDC)
                             ).then_inc(S_['xd'], 16)
            if rank == 1:
                sy.wait_ge(S_['eo'], 16 * KO[NGA])
                sy.wait_ge(S_['vo'], 16 * VO[NGA])
                sy.dma_start(dbg1, KT_d[0:128, :]).then_inc(S_['out'], 16)
                sy.dma_start(dbg2.rearrange("p (a d) -> p a d", d=D),
                             V_d[0:8 * 128, :].rearrange("(a p) d -> p a d", p=128)
                             ).then_inc(S_['out'], 16)
                sy.dma_start(dbg3, QT_d[0:128, :]).then_inc(S_['out'], 16)
                sy.wait_ge(S_['out'], 48)

        @block.gpsimd
        def _(gp_):
            gp_.memset(ones_b[:, :], 1.0)
            gp_.memset(onesr_b[:, :], 1.0)
            gp_.memset(onesr_f[:, :], 1.0)
            gp_.memset(ones_f[:, :], 1.0)
            gp_.memset(sel[:, :], 0.0)
            gp_.memset(sel[0:1, 0:64], 1.0)
            gp_.memset(selk[:, :], 0.0)
            gp_.memset(selk[:, 0:1], 1.0)
            gp_.memset(sel[64:65, 64:128], 1.0)
            gp_.memset(eps_ln[:, :], 1e-5)
            gp_.memset(eps_rms[:, :], 1e-6)
            gp_.memset(eps_t[:, :], 1e-8)
            gp_.memset(eps_f[:, :], 1e-11)
            gp_.memset(scr1[:, :], 0.0)
            gp_.memset(scrV[:, :], 0.0).then_inc(S_['ld'], 1)
            gp_.dma_start(wqt[:, :].rearrange("p (c e) -> p c e", c=NDC), wq_d.rearrange("(c p) e -> p c e", c=NDC)
                          ).then_inc(S_['wl'], 16)
            gp_.dma_start(wkt[:, :].rearrange("p (c e) -> p c e", c=NDC), wk_d.rearrange("(c p) e -> p c e", c=NDC)
                          ).then_inc(S_['wl'], 16)
            gp_.dma_start(wvt[:, :].rearrange("p (c e) -> p c e", c=NDC), wv_d.rearrange("(c p) e -> p c e", c=NDC)
                          ).then_inc(S_['wl'], 16)
            for g in range(NGA):
                for u in range(8):
                    gp_.wait_ge(S_['ev'], UB[g] + u + 1)
                    gp_.dma_start(KT_d[u * 128:(u + 1) * 128, g * TB:(g + 1) * TB],
                                  evK[u % 2][:, :]).then_inc(S_['eo'], 16)
                for i in range(8):
                    gp_.wait_ge(S_['ev'], UB[g] + 8 + i + 1)
                    gp_.dma_start(V_d[g * TB + tc128(i): g * TB + tc128(i) + 128,
                                      (i % 2) * 512:((i % 2) + 1) * 512],
                                  evV[i % 2][:, :]).then_inc(S_['vo'], 16)
                if n_units(g) > 16:
                    for u in range(16, 24):
                        gp_.wait_ge(S_['ev'], UB[g] + u + 1)
                        gp_.dma_start(QT_d[(u - 16) * 128:(u - 15) * 128,
                                           g * TB:(g + 1) * TB],
                                      evK[u % 2][:, :]).then_inc(S_['eo'], 16)

        @block.vector
        def _(v):
            v.wait_ge(S_['ld'], 49)
            v.wait_ge(S_['wl'], 48)
            v.tensor_copy(nsrow_b[:, :], nsrow_f[:, :])
            stats_dve_pre(v, env1, 0)

            def rows_chain(g2):
                kind2 = blk_kind(g2)
                stats_dve_rows1(v, env1, g2)
                stats_dve_rows2(v, env1, g2)
                stats_dve_rows3(v, env1, g2, kind2)
                stats_dve_rows4(v, env1, g2)
            rows_chain(0)
            stats_dve_pre(v, env1, 1)
            for g in range(NGA):
                kind = blk_kind(g)
                stats_dve_h(v, env1, g, kind)
                nu = n_units(g)
                if g + 1 < NGA:
                    g2 = g + 1
                    hooks = {
                        nu // 8: (lambda g2=g2: stats_dve_rows1(v, env1, g2)),
                        nu // 4: (lambda g2=g2: stats_dve_rows2(v, env1, g2)),
                        3 * nu // 8: (lambda g2=g2: stats_dve_rows3(v, env1, g2,
                                                                    blk_kind(g2))),
                        nu // 2: (lambda g2=g2: stats_dve_rows4(v, env1, g2)),
                        3 * nu // 4: (lambda g2=g2: (stats_dve_pre(v, env1, g2 + 1)
                                                     if g2 + 1 < NGA else None)),
                    }
                else:
                    hooks = None
                stats_dve_evac(v, env1, g, kind, evK=evK, evV=evV, hooks=hooks)

        @block.scalar
        def _(sc):
            sc.wait_ge(S_['ld'], 1)
            for g in range(NGA):
                kind = blk_kind(g)
                stats_act_rows1(sc, env1, g)
                stats_act_rows2(sc, env1, g, kind)
                stats_act_rows3(sc, env1, g, kind)

        @block.tensor
        def _(t):
            t.wait_ge(S_['ld'], 49)
            t.wait_ge(S_['wl'], 48)
            for g in range(NGA):
                if g >= 1:
                    stats_pe_work_bcast(t, env1, g - 1)
                stats_pe_stats(t, env1, g)
                if g >= 1:
                    stats_pe_work(t, env1, g - 1, blk_kind(g - 1), wkt=wkt, wvt=wvt,
                                  wqt=wqt, split=True)
            stats_pe_work(t, env1, NGA - 1, blk_kind(NGA - 1), wkt=wkt, wvt=wvt, wqt=wqt)

    if rank < 2:
        ctx.close()
        return nc
    with ExitStack() as sc2:
        t2 = lambda *a: sc2.enter_context(nc.sbuf_tensor(*a))
        Kb = [t2("Kb%d" % i, [128, S], BF16) for i in range(2)]
        Vb = [t2("Vb%d" % i, [128, S], BF16) for i in range(2)]
        Qb = [t2("Qb%d" % i, [128, QOWN], BF16) for i in range(2)]
        NV = t2("NV", [128, NPAIR * QOWN], BF16)
        ptl = {0: [t2("pA%d" % i, [128, 1024], BF16) for i in range(2)],
               1: [t2("pB%d" % i, [128, 1024], BF16) for i in range(2)]}
        sums = t2("sums", [128, QT], F32)
        sums2 = t2("sums2", [128, QT], F32)
        sumsT = [sums, sums2]
        recs = t2("recs", [128, QT], F32)
        recs2 = t2("recs2", [128, QT], F32)
        recsT = [recs, recs2]
        recsP3 = [t2("recsP3%d" % i, [65, QT], F32) for i in range(2)]
        ksq = t2("ksq", [128, 2048], BF16)
        kqc = t2("kqc", [128, 24], F32)
        biasA = t2("biasA", [128, 1], F32)
        biasB = t2("biasB", [128, 1], F32)
        cneg = t2("cneg", [1, 2], BF16)
        taur = t2("taur", [128, 1], F32)
        tau_up = t2("tau_up", [128, 1], F32)
        tau_dn = t2("tau_dn", [128, 1], F32)
        twk = [t2("twk%d" % i, [128, 1024], F32) for i in range(3)]
        TWS = None
        tabs = t2("tabs", [128, 1024], F32)
        tmf = t2("tmf", [128, 1024], BF16)
        tout = [t2("tout%d" % i, [128, 1024], BF16) for i in range(2)]
        tacc = t2("tacc", [128, 64], F32)
        wot = t2("wot", [128, NDC * D], BF16)

        slots = [PS[0], PS[1], PS[2]]

        def tern_src(kind, k):
            if kind in ('u1', 'u2'):
                dc, q4 = k // 4, k % 4
                return wupT_d[dc * 128:(dc + 1) * 128, q4 * 1024:(q4 + 1) * 1024], 1024
            return wdnT_d[k * 128:(k + 1) * 128, :], 1024

        @block.sync
        def _(sy):
            for j in range(NPAIR):
                if j == 0:
                    sy.wait_ge(S_['eo'], 16 * KO[NGA])
                    sy.wait_ge(S_['vo'], 16 * VO[NGA])
                if j >= 2:
                    sy.wait_ge(S_['av'], NQT * NCP * (j - 1))
                p2s = S_['p2'] if j % 2 == 0 else S_['p2b']
                sy.dma_start(Kb[j % 2][:, :], KT_d[j * 128:(j + 1) * 128, :]
                             ).then_inc(p2s, 16)
                sy.dma_start(Qb[j % 2][:, :], QT_d[j * 128:(j + 1) * 128, :]
                             ).then_inc(p2s, 16)
                for i in range(8):
                    sy.dma_start(Vb[j % 2][:, i * 1024:(i + 1) * 1024]
                                 .rearrange("p (c e) -> p c e", c=8),
                                 V_d[i * 1024:(i + 1) * 1024, j * 128:(j + 1) * 128]
                                 .rearrange("(c p) e -> p c e", p=128)).then_inc(p2s, 16)

        @block.gpsimd
        def _(gp_):
            gp_.wait_ge(S_['eo'], 16 * KO[NGA])
            gp_.wait_ge(S_['vo'], 16 * VO[NGA])
            gp_.wait_ge(S_['ev'], UB[NGA])
            gp_.memset(recs[:, :], 0.0)
            gp_.memset(recs2[:, :], 0.0)
            gp_.memset(recsP3[0][:, :], 0.0)
            gp_.memset(recsP3[1][:, :], 0.0)
            gp_.memset(tacc[:, :], 0.0).then_inc(S_['tv'], 1)
            gp_.dma_start(wot[:, :].rearrange("p (c e) -> p c e", c=NDC), wo_d.rearrange("(c p) e -> p c e", c=NDC)
                          ).then_inc(S_['woc'], 16)
            def tern_out(gp_, i, kind, k):
                gp_.wait_ge(S_['tv'], 1 + i + 1)
                if kind == 'u2':
                    dc, q4 = k // 4, k % 4
                    r0 = dc * NFC + q4 * 8
                    dst = WUPt_d[r0:r0 + 8, :]
                else:
                    r0 = k * NDC
                    dst = WDNt_d[r0:r0 + 8, :]
                gp_.dma_start(dst.rearrange("f (p e) -> p f e", p=128),
                              tout[i % 2][:, 0:1024].rearrange("p (f e) -> p f e", f=8)
                              ).then_inc(S_['to'], 16)

            pend_out = [None]
            TWS = [S_['tw'], S_['tw1'], S_['tw2']]
            for i, (kind, k) in enumerate(TERN_UNITS):
                if i >= 3:
                    gp_.wait_ge(S_['tv'], 1 + i - 2)
                src, w = tern_src(kind, k)
                gp_.dma_start(twk[i % 3][:, 0:w], src).then_inc(TWS[i % 3], 16)
                if pend_out[0] is not None:
                    tern_out(gp_, *pend_out[0])
                    pend_out[0] = None
                if kind in ('u2', 'd2'):
                    pend_out[0] = (i, kind, k)
            tern_out(gp_, *pend_out[0])
            pend_out[0] = None

        def _avblock(t, j, qg, prev, u, first, drain=False, need_cb=False):
            t.wait_ge(S_['ex'], u if drain else u - 2)
            if first:
                t.wait_ge(S_['tl'], 2 * qg)
                if need_cb:
                    t.wait_ge(S_['cb'], j + 1)
            mm = None
            for ci in (0, 1):
                c = 2 * prev + ci
                fst = first and ci == 0
                lst = drain and ci == 1
                t.matmul(p_nv[0:64, :], Vb[j % 2][:, c * 128: c * 128 + 64],
                         ptl[0][prev % 2][:, ci * 512:(ci + 1) * 512],
                         start=fst, stop=lst, skip_group_check=True, tile_position=(0, 0))
                t.matmul(p_nv[64:128, :], Vb[j % 2][:, c * 128 + 64: c * 128 + 128],
                         ptl[1][prev % 2][:, ci * 512:(ci + 1) * 512],
                         start=fst, stop=lst, skip_group_check=True, tile_position=(0, 64))
                t.matmul(p_sum[0:1, :], ones_b[:, :],
                         ptl[0][prev % 2][:, ci * 512:(ci + 1) * 512],
                         start=fst, stop=lst, skip_group_check=True, tile_position=(0, 0))
                mm = t.matmul(p_sum[64:65, :], ones_b[:, :],
                              ptl[1][prev % 2][:, ci * 512:(ci + 1) * 512],
                              start=fst, stop=lst, skip_group_check=True, tile_position=(0, 64))
            mm.then_inc(S_['av'], 1)

        @block.tensor
        def _(t):
            u = 0
            t.wait_ge(S_['tv'], 1)
            for j in range(NPAIR):
                jq0 = j * NQT
                t.wait_ge(S_['p2'] if j % 2 == 0 else S_['p2b'], 160 * (j // 2 + 1))
                t.wait_ge(S_['tl'], 2 * jq0)
                for nm, pj in (('up', 2), ('dn', 4)):
                    if j == pj:
                        e = 0 if nm == 'up' else 1
                        t.wait_ge(S_['tau'], 5 * e + 1)
                        t.matmul(p_nv[0:1, 0:1], ones_f[:, :], taur[:, :],
                                 start=True, stop=True,
                                 skip_group_check=True).then_inc(S_['tau'], 1)
                        t.wait_ge(S_['tau'], 5 * e + 3)
                        t.matmul(p_nv[:, 2:3], onesr_f[:, :], kqc[0:1, 22:23],
                                 start=True, stop=True,
                                 skip_group_check=True).then_inc(S_['tau'], 1)
                        t.wait_ge(S_['tau'], 5 * e + 5)
                slK = slots[(128 * j + 2) % 3]
                fbK = [p_nv, p_sum, slK[:, 0:512], slK[:, 512:1024]]
                for f in range(20):
                    if f < 16:
                        qb, cg = f // 4, f % 4
                        t.wait_ge(S_['ksq'], 5 * j + qb + 1)
                        srcq = ksq
                    else:
                        cg = f - 16
                        t.wait_ge(S_['ksq'], 5 * j + 5)
                        srcq = ksq
                    if f >= 4:
                        t.wait_ge(S_['knr'], 11 * j + (f - 4) // 2 + 1)
                    bank = fbK[f % 4]
                    t.matmul(bank[0:64, :], selk[0:64, :], srcq[0:64, cg * 512:(cg + 1) * 512],
                             start=True, stop=True, skip_group_check=True, tile_position=(0, 0))
                    t.matmul(bank[64:128, :], selk[64:128, :],
                             srcq[64:128, cg * 512:(cg + 1) * 512],
                             start=True, stop=True, skip_group_check=True,
                             tile_position=(64, 64)).then_inc(S_['kn'], 1)
                t.wait_ge(S_['knr'], 11 * j + 11)
                t.matmul(p_nv[:, 0:1], onesr_b[:, :], cneg[0:1, 0:1],
                         start=True, stop=True, skip_group_check=True)
                t.matmul(p_nv[:, 1:2], onesr_b[:, :], cneg[0:1, 1:2],
                         start=True, stop=True, skip_group_check=True).then_inc(S_['kn'], 1)
                t.wait_ge(S_['knr'], 11 * j + 10)
                for qtr in range(NQT):
                    qg = jq0 + qtr
                    qc = qtr * QT
                    for cp in range(NCP):
                        for h_ in (0, 1):
                            slot = slots[(u + h_) % 3]
                            if u + h_ >= 3:
                                t.wait_ge(S_['ex'], u + h_ - 2)
                            base = 64 * h_
                            mm = None
                            for ci in (0, 1):
                                c = 2 * cp + ci
                                mm = t.matmul(slot[:, ci * 512:(ci + 1) * 512],
                                              Kb[j % 2][base:base + 64, c * 128:(c + 1) * 128],
                                              Qb[j % 2][base:base + 64, qc:qc + QT],
                                              start=True, stop=True, skip_group_check=True,
                                              tile_position=(base, 0))
                            mm.then_inc(S_['sc'], 1)
                        u += 2
                        if cp >= 1:
                            _avblock(t, j, qg, cp - 1, u, first=(cp == 1),
                                     need_cb=(qtr == 0))
                    _avblock(t, j, qg, NCP - 1, u, first=False, drain=True)

        def recs_store(eng, qs):
            js, qcs = qs // NQT, (qs % NQT) * QT
            eng.wait_ge(S_['bc'], qs + 1)
            eng.dma_start(recs_d[0:2, js * QOWN + qcs: js * QOWN + qcs + QT],
                          recsT[qs % 2][0:65:64, 0:QT]).then_inc(S_['rc'], 16)

        @block.scalar
        def _(sc):
            u = 0
            for j in range(NPAIR):
                sc.wait_ge(S_['cb'], j + 1)
                for qtr in range(NQT):
                    qg = j * NQT + qtr
                    for cp in range(NCP):
                        if cp == 12 and qg >= 1:
                            recs_store(sc, qg - 1)
                        for h_ in (0, 1):
                            sc.wait_ge(S_['sc'], u + 1)
                            guard = qg * NCP + cp - 1
                            if guard > 0:
                                sc.wait_ge(S_['av'], guard)
                            bias = biasA if h_ == 0 else biasB
                            sc.activation(ptl[h_][cp % 2][:, :], slots[u % 3][:, :], AF.Exp,
                                          bias=bias[:, 0:1], scale=1.0).then_inc(S_['ex'], 1)
                            u += 1



        @block.vector
        def _(v):
            pend_rec = [None]

            def emit_recs(v):
                if pend_rec[0] is None:
                    return
                qg = pend_rec[0]
                pend_rec[0] = None
                if qg >= 2:
                    v.wait_ge(S_['rc'], 16 * (qg - 1))
                v.reciprocal(recsT[qg % 2][0:1, :], sumsT[qg % 2][0:1, :])
                v.reciprocal(recsT[qg % 2][64:65, :], sumsT[qg % 2][64:65, :])
                padV(v, 'bc')
            for j in range(NPAIR):
                jq0 = j * NQT
                for nm, pj in (('up', 2), ('dn', 4)):
                    if j == pj:
                        e = 0 if nm == 'up' else 1
                        cols = (0, 32) if nm == 'up' else (32, 64)
                        v.wait_ge(S_['tv'], 1 + (32 if nm == 'up' else 64))
                        v.reduce_sum(taur[:, 0:1], tacc[:, cols[0]:cols[1]],
                                     axis=mybir.AxisListType.X)
                        padV(v, 'tau')
                        v.wait_ge(S_['tau'], 5 * e + 2)
                        v.tensor_single_scalar(kqc[0:1, 22:23], p_nv[0:1, 0:1], TAUF, ALU.mult)
                        padV(v, 'tau')
                        v.wait_ge(S_['tau'], 5 * e + 4)
                        dst = tau_up if nm == 'up' else tau_dn
                        v.tensor_copy(dst[:, :], p_nv[:, 2:3])
                        padV(v, 'tau')
                slK = slots[(128 * j + 2) % 3]
                fbK = [p_nv, p_sum, slK[:, 0:512], slK[:, 512:1024]]
                v.wait_ge(S_['p2'] if j % 2 == 0 else S_['p2b'], 160 * (j // 2 + 1))
                fb2 = [PS[3][:, 0:1024], slK[:, 0:1024]]
                for qb in range(4):
                    if qb >= 1:
                        v.wait_ge(S_['kn'], 21 * j + 4 * qb)
                    v.tensor_tensor(ksq[:, :], Kb[j % 2][:, qb * 2048:(qb + 1) * 2048],
                                    Kb[j % 2][:, qb * 2048:(qb + 1) * 2048],
                                    ALU.mult).then_inc(S_['ksq'], 1)
                    for r in range(2 * qb, 2 * qb + 2):
                        v.wait_ge(S_['kn'], 21 * j + 2 * r + 2)
                        v.reduce_max(kqc[:, r:r + 1], fb2[r % 2][:, :],
                                     axis=mybir.AxisListType.X).then_inc(S_['knr'], 1)
                v.wait_ge(S_['kn'], 21 * j + 16)
                v.tensor_tensor(ksq[:, :], Qb[j % 2][:, :], Qb[j % 2][:, :],
                                ALU.mult).then_inc(S_['ksq'], 1)
                for r in range(8, 10):
                    v.wait_ge(S_['kn'], 21 * j + 2 * r + 2)
                    v.reduce_max(kqc[:, r:r + 1], fb2[r % 2][:, :],
                                 axis=mybir.AxisListType.X).then_inc(S_['knr'], 1)
                v.wait_ge(S_['knr'], 11 * j + 10)
                v.reduce_max(kqc[:, 20:21], kqc[:, 0:8], axis=mybir.AxisListType.X)
                v.reduce_max(kqc[:, 21:22], kqc[:, 8:10], axis=mybir.AxisListType.X)
                v.tensor_tensor(kqc[:, 22:23], kqc[:, 20:21], kqc[:, 21:22], ALU.add)
                v.tensor_single_scalar(kqc[:, 22:23], kqc[:, 22:23], 0.5, ALU.mult)
                v.tensor_single_scalar(cneg[0:1, 0:1], kqc[0:1, 22:23], -1.0, ALU.mult)
                v.tensor_single_scalar(cneg[0:1, 1:2], kqc[64:65, 22:23], -1.0, ALU.mult)
                padV(v, 'knr')
                if j >= 1:
                    v.wait_ge(S_['ex'], 256 * j)
                v.wait_ge(S_['kn'], 21 * j + 21)
                v.tensor_copy(biasA[:, :], p_nv[:, 0:1])
                v.tensor_copy(biasB[:, :], p_nv[:, 1:2])
                padV(v, 'cb')
                emit_recs(v)
                i0 = sum(len(TERN_SCHED[q]) for q in range(j))
                tu = TERN_SCHED[j]
                for qtr in range(NQT):
                    qg = jq0 + qtr
                    qc = qtr * QT
                    emit_recs(v)
                    for ii in range(4 * qtr, 4 * qtr + 4):
                        if ii >= len(tu):
                            break
                        kind, k = tu[ii]
                        i = i0 + ii
                        tws = [S_['tw'], S_['tw1'], S_['tw2']][i % 3]
                        v.wait_ge(tws, 16 * (i // 3 + 1))
                        w = 1024
                        if kind in ('u1', 'd1'):
                            col = k if kind == 'u1' else 32 + k
                            v.reduce_sum(tacc[:, col:col + 1], twk[i % 3][:, 0:w],
                                         axis=mybir.AxisListType.X,
                                         apply_absolute_value=True).then_inc(S_['tv'], 1)
                        else:
                            tau_t = tau_up if kind == 'u2' else tau_dn
                            if TERN_OUT_BEFORE[i] > 0:
                                v.wait_ge(S_['to'], 16 * TERN_OUT_BEFORE[i])
                            v.scalar_tensor_tensor(tabs[:, 0:w], twk[i % 3][:, 0:w], -1.0,
                                                   twk[i % 3][:, 0:w], ALU.mult, ALU.max)
                            v.tensor_single_scalar(tabs[:, 0:w], tabs[:, 0:w],
                                                   tau_t[:, 0:1], ALU.subtract)
                            v.tensor_single_scalar(tmf[:, 0:w], tabs[:, 0:w], 0.0, ALU.is_gt)
                            v.tensor_single_scalar(tabs[:, 0:w], twk[i % 3][:, 0:w], 0.0,
                                                   ALU.is_gt)
                            v.tensor_scalar(tabs[:, 0:w], tabs[:, 0:w], 2.0, -1.0,
                                            ALU.mult, ALU.add)
                            v.tensor_tensor(tout[i % 2][:, 0:w], tabs[:, 0:w], tmf[:, 0:w],
                                            ALU.mult).then_inc(S_['tv'], 1)
                    v.wait_ge(S_['av'], (qg + 1) * NCP)
                    v.tensor_copy(sumsT[qg % 2][:, :], p_sum).then_inc(S_['tl'], 1)
                    v.tensor_copy(NV[:, j * QOWN + qc: j * QOWN + qc + QT],
                                  p_nv).then_inc(S_['tl'], 1)
                    pend_rec[0] = qg

        @block.vector
        def _(v):
            qg = NPAIR * NQT - 1
            v.wait_ge(S_['rc'], 16 * (qg - 1))
            v.reciprocal(recsT[qg % 2][0:1, :], sumsT[qg % 2][0:1, :])
            v.reciprocal(recsT[qg % 2][64:65, :], sumsT[qg % 2][64:65, :])
            padV(v, 'bc')

        if rank == 2:
            @block.sync
            def _(sy):
                sy.wait_ge(S_['tl'], 2 * NPAIR * NQT)
                sy.wait_ge(S_['to'], 16 * N_TERN_OUT)
                sy.dma_start(dbg4, NV[:, :]).then_inc(S_['out'], 16)
                sy.dma_start(dbg5[:, 0:1], tau_up[:, :], allow_slow_non_contiguous=True).then_inc(S_['out'], 16)
                sy.dma_start(dbg5[:, 1:2], tau_dn[:, :], allow_slow_non_contiguous=True).then_inc(S_['out'], 16)
                sy.dma_start(dbg5[:, 16:80], tacc[:, :]).then_inc(S_['out'], 16)
                sy.dma_start(dbg3[:, 0:1024], tout[0][:, 0:1024]).then_inc(S_['out'], 16)
                sy.dma_start(dbg1.rearrange("p (a q) -> p a q", a=64),
                             WUPt_d[0:64, :].rearrange("a (p q) -> p a q", p=128)
                             ).then_inc(S_['out'], 16)
                sy.dma_start(dbg5[:, 2048:3072], twk[0][:, :]).then_inc(S_['out'], 16)
                sy.dma_start(dbg5[:, 4096:6144],
                             wupT_d[0:128, 0:2048]).then_inc(S_['out'], 16)
                sy.wait_ge(S_['out'], 128)

        p3slots = [PS[0][:, 0:512], PS[0][:, 512:1024], PS[1][:, 0:512], PS[1][:, 512:1024]]

        if rank >= 3:

            nb3 = [PS[2][:, 0:512], PS[2][:, 512:1024], PS[3][:, 0:512], PS[3][:, 512:1024]]

            @block.sync
            def _(sy):
                sy.wait_ge(S_['rc'], 16 * (NPAIR * NQT - 1))
                for qg in range(4):
                    for jj in range(NPAIR):
                        bi = qg * NPAIR + jj
                        if jj == NPAIR - 1 and qg == 3:
                            continue
                        if bi >= 2:
                            sy.wait_ge(S_['nn'], bi - 1)
                        sy.dma_start(recsP3[bi % 2][0:65:64, :],
                                     recs_d[0:2, jj * QOWN + qg * QT: jj * QOWN + (qg + 1) * QT]
                                     ).then_inc(S_['rl'], 16)

            @block.tensor
            def _(t):
                t.wait_ge(S_['tl'], 2 * NPAIR * NQT)
                t.wait_ge(S_['woc'], 16)
                def p3bcast(t, qg2, jj):
                    bi = qg2 * NPAIR + jj
                    if bi >= 4:
                        t.wait_ge(S_['p3n'], bi - 3)
                    if bi == NPAIR * 4 - 1:
                        t.wait_ge(S_['bc'], NPAIR * NQT)
                        src_ = recsT[(NPAIR * 4 - 1) % 2][0:65, :]
                    else:
                        t.wait_ge(S_['rl'], 16 * min(bi + 2, NPAIR * 4 - 1))
                        src_ = recsP3[bi % 2][0:65, :]
                    t.matmul(nb3[bi % 4], sel[0:65, :], src_,
                             start=True, stop=True,
                             skip_group_check=True).then_inc(S_['nn'], 1)
                for jj in range(NPAIR):
                    p3bcast(t, 0, jj)
                for qg in range(4):
                    for dc in range(NDC):
                        idx = qg * NDC + dc
                        if idx >= 4:
                            t.wait_ge(S_['p3e'], idx - 3)
                        t.wait_ge(S_['p3n'], NPAIR * (qg + 1))
                        mm = None
                        for hc in range(NDC):
                            mm = t.matmul(p3slots[idx % 4],
                                          wot[:, hc * D + dc * 128: hc * D + (dc + 1) * 128],
                                          NV[:, hc * QOWN + qg * 512: hc * QOWN + (qg + 1) * 512],
                                          start=(hc == 0), stop=(hc == NDC - 1),
                                          skip_group_check=True)
                        mm.then_inc(S_['p3'], 1)
                        if qg < 3:
                            p3bcast(t, qg + 1, dc)

            @block.vector
            def _(v):
                def p3tt(v, qg2, jj):
                    bi = qg2 * NPAIR + jj
                    v.wait_ge(S_['nn'], bi + 1)
                    v.tensor_tensor(NV[:, jj * QOWN + qg2 * 512: jj * QOWN + (qg2 + 1) * 512],
                                    NV[:, jj * QOWN + qg2 * 512: jj * QOWN + (qg2 + 1) * 512],
                                    nb3[bi % 4], ALU.mult).then_inc(S_['p3n'], 1)
                for jj in range(NPAIR):
                    p3tt(v, 0, jj)
                for qg in range(4):
                    for dc in range(NDC):
                        idx = qg * NDC + dc
                        v.wait_ge(S_['p3'], idx + 1)
                        v.tensor_scalar(ST[:, dc * QOWN + qg * 512: dc * QOWN + (qg + 1) * 512],
                                        p3slots[idx % 4], bout_c[:, dc:dc + 1], None,
                                        ALU.add).then_inc(S_['p3e'], 1)
                        if qg < 3:
                            p3tt(v, qg + 1, dc)

    if rank == 3:
        @block.sync
        def _(sy):
            sy.wait_ge(S_['p3e'], 32)
            sy.dma_start(dbg4, ST[:, :]).then_inc(S_['out'], 16)
            sy.wait_ge(S_['out'], 16)
    if rank < 4:
        ctx.close()
        return nc
    with ExitStack() as sc4:
        t4 = lambda *a: sc4.enter_context(nc.sbuf_tensor(*a))
        xblk4s = t4("xblk4", [128, NDC * TB], F32)
        xblk4 = [xblk4s, xblk4s]
        xbf4 = t4("xbf4", [128, NDC * TB], BF16)
        xsq4 = t4("xsq4", [128, NDC * TB], BF16)
        rowA4 = t4("rowA4", [1, 6 * TB], F32)
        rowB4 = t4("rowB4", [1, 4 * TB], F32)
        rowC4 = t4("rowC4", [1, 4 * TB], BF16)
        HF = t4("HF", [128, NDC * QOWN], BF16)
        mid = t4("mid", [128, NFC * 512], BF16)
        Wub = [t4("Wub%d" % i, [128, NDC * 128], BF16) for i in range(4)]
        Wdb = [t4("Wdb%d" % i, [128, NFC * 128], BF16) for i in range(3)]
        WD3 = [3, 3, 2]
        a3s = t4("a3s", [128, 512], F32)
        gtile = t4("gtile", [128, 512], F32)
        gsil = t4("gsil", [128, 512], BF16)
        xfin = [t4("xfin%d" % i, [128, 512], F32) for i in range(2)]
        a3hist = None
        yout = [t4("yout", [128, 512], F32)]
        yout = [yout[0], yout[0]]
        hsc4 = t4("hsc4", [128, TB], BF16)
        env4 = dict(xblk=xblk4, xbf=xbf4, xsq=xsq4, rowA=rowA4, rowB=rowB4, rowC=rowC4,
                    hdst=HF, hT=None, hsc=hsc4, xd4base=GP4)

        NTB4 = 4

        def wo_count(tb, item, i):
            base = tb * (NFC + 5 * NDC)
            if item == 'wub':
                return 16 * (base + i + 1)
            if item == 'wdb':
                return 16 * (base + NFC + 5 * i + 4)
            return 16 * (base + NFC + 5 * i + 5)

        @block.sync
        def _(sy):
            sy.wait_ge(S_['p3e'], 32)
            for g in range(GP4, NG):
                off = (g - GP4) * TB
                for cc in range(NDC):
                    if g >= GP4 + 1:
                        sy.wait_ge(S_['hh'], 8 * (g - 1) + cc + 1)
                    sy.dma_start(xblk4[g % 2][:, cc * TB:(cc + 1) * TB],
                                 xTo[cc * 128:(cc + 1) * 128, off:off + TB]
                                 ).then_inc(S_['xd4'], 16)
            if not p4_full:
                sy.wait_ge(S_['hh'], 8 * NG)
                sy.dma_start(dbg4, ST[:, :]).then_inc(S_['out'], 16)
                sy.wait_ge(S_['out'], 16)
            for tb in range(NTB4 if p4_full else 0):
                if tb == 1 and rank == 5:
                    sy.wait_ge(S_['fa4'], 32)
                    sy.dma_start(dbgM, mid[:, :]).then_inc(S_['dmp'], 16)
                    sy.wait_ge(S_['dmp'], 16)
                for fc in range(NFC):
                    if fc >= 4:
                        sy.wait_ge(S_['f4'], tb * F4_PER + fc - 3)
                    elif tb >= 1:
                        sy.wait_ge(S_['f4'], (tb - 1) * F4_PER + 29 + fc)
                    if tb == 0 and fc == 0:
                        sy.wait_ge(S_['to'], 16 * N_TERN_OUT)
                    sy.dma_start(Wub[fc % 4][:, :].rearrange("p (c e) -> p c e", c=NDC),
                                 WUPt_d[fc::NFC, :].rearrange("c (p e) -> p c e", p=128)
                                 ).then_inc(S_['wu%d' % (fc % 4)], 16)
                for dc in range(NDC):
                    if dc >= 3:
                        sy.wait_ge(S_['f4'], tb * F4_PER + 31 + dc)
                    elif tb >= 1:
                        sy.wait_ge(S_['f4'], (tb - 1) * F4_PER + 39 + dc)
                    for k4 in range(4):
                        sy.dma_start(Wdb[dc % 3][:, k4 * 1024:(k4 + 1) * 1024]
                                     .rearrange("p (c e) -> p c e", c=8),
                                     WDNt_d[dc::NDC, :][k4 * 8:(k4 + 1) * 8, :]
                                     .rearrange("c (p e) -> p c e", p=128)
                                     ).then_inc(S_['wd%d' % (dc % 3)], 16)
                    if tb * NDC + dc - 2 >= 0:
                        sy.wait_ge(S_['fy'], tb * NDC + dc - 1)
                    sy.dma_start(xfin[dc % 2][:, :],
                                 xTo[dc * 128:(dc + 1) * 128, tb * 512:(tb + 1) * 512]
                                 ).then_inc(S_['wx'], 16)
            if p4_full:
                sy.wait_ge(S_['out'], 16 * NTB4 * NDC)
            if rank == 5:
                sy.dma_start(dbg1[:, 0:1024].rearrange("p (a q) -> p a q", a=8),
                             WUPt_d[0:8, :].rearrange("a (p q) -> p a q", p=128)
                             ).then_inc(S_['out'], 16)
                sy.dma_start(dbg2[:, 0:1024].rearrange("p (a q) -> p a q", a=8),
                             WDNt_d[0:8, :].rearrange("a (p q) -> p a q", p=128)
                             ).then_inc(S_['out'], 16)
                sy.dma_start(dbg4, ST[:, :]).then_inc(S_['out'], 16)
                sy.dma_start(dbg5[:, 0:512], a3s[:, :]).then_inc(S_['out'], 16)

                sy.dma_start(dbg5[0:1, 1024:1536], rowA4[0:1, 512:1024],
                             allow_slow_non_contiguous=True).then_inc(S_['out'], 16)
                sy.dma_start(dbg5[0:1, 1536:2048], rowA4[0:1, 1024:1536],
                             allow_slow_non_contiguous=True).then_inc(S_['out'], 16)
                sy.dma_start(dbg5[:, 512:1024], gtile[:, :]).then_inc(S_['out'], 16)
                sy.dma_start(dbg3[:, 1024:1536], mid[:, 0:512]).then_inc(S_['out'], 16)
                sy.dma_start(dbgH, HF[:, :]).then_inc(S_['out'], 16)
                sy.wait_ge(S_['out'], 16 * (NTB4 * NDC + 9))

        def mid_sq_view(fc):
            return xsq4[:, (fc % NDC) * 512:(fc % NDC) * 512 + 512]

        def _summ(t, tb, fc):
            t.wait_ge(S_['fe4'], tb * FE4_PER + fc + 1)
            t.matmul(PS[3][0:1, 0:512], ones_b[:, :], mid_sq_view(fc),
                     start=(fc == 0), stop=(fc == NFC - 1),
                     skip_group_check=True).then_inc(S_['fs'], 1)

        @block.gpsimd
        def _(gp_):
            for tb in range(NTB4 if p4_full else 0):
                for dc in range(NDC):
                    gp_.wait_ge(S_['fy'], tb * NDC + dc + 1)
                    gp_.dma_start(yT[dc * 128:(dc + 1) * 128, tb * 512:(tb + 1) * 512],
                                  yout[0][:, :]).then_inc(S_['out'], 16)

        @block.tensor
        def _(t):
            for g in range(GP4, NG):
                if g > GP4:
                    stats_pe_work(t, env4, g - 1, 'p4')
                stats_pe_stats(t, env4, g)
            stats_pe_work(t, env4, NG - 1, 'p4')
            for tb in range(NTB4 if p4_full else 0):
                for fc in range(NFC):
                    t.wait_ge(S_['wu%d' % (fc % 4)], 16 * (8 * tb + fc // 4 + 1))
                    if tb == 0 and fc == 0:
                        t.wait_ge(S_['hh'], 8 * NG)
                        t.wait_ge(S_['sa'], 3 * (NG - 1) + 1)
                    if fc >= 3:
                        t.wait_ge(S_['fa4'], tb * FA4_PER + fc - 2)
                    elif tb >= 1:
                        t.wait_ge(S_['fe4'], tb * FE4_PER)
                    mm = None
                    for dc in range(NDC):
                        mm = t.matmul(PS[fc % 3][:, 0:512],
                                      Wub[fc % 4][:, dc * 128:(dc + 1) * 128],
                                      HF[:, dc * QOWN + tb * 512: dc * QOWN + tb * 512 + 512],
                                      start=(dc == 0), stop=(dc == NDC - 1),
                                      skip_group_check=True)
                    mm.then_inc(S_['f4'], 1)
                    if fc >= 2:
                        _summ(t, tb, fc - 2)
                _summ(t, tb, NFC - 2)
                _summ(t, tb, NFC - 1)
                t.wait_ge(S_['fe4'], tb * FE4_PER + 33)
                t.matmul(PS[3][:, 512:1024], onesr_b[:, :], rowC4[0:1, 0:512],
                         start=True, stop=False, skip_group_check=True)
                t.matmul(PS[3][:, 512:1024], onesr_b[:, :], rowC4[0:1, 512:1024],
                         start=False, stop=True, skip_group_check=True).then_inc(S_['f4'], 1)
                for dc in range(NDC):
                    t.wait_ge(S_['fa4'], tb * FA4_PER + 32)
                    if dc == 0:
                        t.wait_ge(S_['fe4'], tb * FE4_PER + 32)
                    t.wait_ge(S_['wd%d' % (dc % 3)],
                              16 * 4 * (WD3[dc % 3] * tb + dc // 3 + 1))
                    if dc >= 2:
                        t.wait_ge(S_['fe4'], tb * FE4_PER + 35 + dc - 2)
                    mm = None
                    for fc in range(NFC):
                        mm = t.matmul(PS[dc % 2][:, 0:512],
                                      Wdb[dc % 3][:, fc * 128:(fc + 1) * 128],
                                      mid[:, fc * 512:(fc + 1) * 512],
                                      start=(fc == 0), stop=(fc == NFC - 1),
                                      skip_group_check=True)
                    mm.then_inc(S_['f4'], 1)

        @block.scalar
        def _(sc):
            for g in range(GP4, NG):
                stats_act_rows1(sc, env4, g)
                stats_act_rows2(sc, env4, g, 'p4')
                stats_act_rows3(sc, env4, g, 'p4')
            for tb in range(NTB4 if p4_full else 0):
                for fc in range(NFC):
                    sc.wait_ge(S_['f4'], tb * F4_PER + fc + 1)
                    sc.activation(mid[:, fc * 512:(fc + 1) * 512], PS[fc % 3][:, 0:512],
                                  AF.Gelu).then_inc(S_['fa4'], 1)
                sc.wait_ge(S_['fs'], (tb + 1) * FS_PER)
                sc.activation(rowA4[0:1, 0:512], PS[3][0:1, 0:512], AF.Ln, scale=1.0 / FF)
                sc.activation(rowA4[0:1, 1024:1536], rowA4[0:1, 0:512], AF.Exp, scale=0.5)
                sc.activation(rowA4[0:1, 1536:2048], rowA4[0:1, 1024:1536], AF.Identity,
                              bias=eps_t[0:1, 0:1])
                sc.activation(rowA4[0:1, 0:512], rowA4[0:1, 1536:2048], AF.Ln)
                sc.activation(rowA4[0:1, 1024:1536], rowA4[0:1, 0:512], AF.Exp,
                              scale=-1.0)
                padA(sc, 'fa4')
                for dc in range(NDC):
                    sc.wait_ge(S_['fe4'], tb * FE4_PER + 35 + dc)
                    sc.activation(gsil[:, :], gtile[:, :], AF.Silu)
                    padA(sc, 'fa4')

        @block.vector
        def _(v):
            stats_dve_pre(v, env4, GP4)
            for g in range(GP4, NG):
                stats_dve_rows1(v, env4, g)
                stats_dve_rows2(v, env4, g)
                stats_dve_rows3(v, env4, g, 'p4')
                stats_dve_rows4(v, env4, g)
                stats_dve_h(v, env4, g, 'p4')
                if g + 1 < NG:
                    stats_dve_pre(v, env4, g + 1)
            for tb in range(NTB4 if p4_full else 0):
                for fc in range(NFC):
                    v.wait_ge(S_['fa4'], tb * FA4_PER + fc + 1)
                    if fc >= NDC:
                        v.wait_ge(S_['fs'], tb * FS_PER + fc - NDC + 1)
                    v.tensor_tensor(mid_sq_view(fc), mid[:, fc * 512:(fc + 1) * 512],
                                    mid[:, fc * 512:(fc + 1) * 512],
                                    ALU.mult).then_inc(S_['fe4'], 1)
                v.wait_ge(S_['fa4'], tb * FA4_PER + 33)
                v.tensor_copy(rowC4[0:1, 0:512], rowA4[0:1, 1024:1536])
                v.tensor_tensor(rowC4[0:1, 512:1024], rowA4[0:1, 1024:1536], rowC4[0:1, 0:512],
                                ALU.subtract)
                padV(v, 'fe4')
                v.wait_ge(S_['f4'], tb * F4_PER + 33)

                v.tensor_copy(a3s[:, :], PS[3][:, 512:1024]).then_inc(S_['fe4'], 1)
                for dc in range(NDC):
                    v.wait_ge(S_['f4'], tb * F4_PER + 33 + dc + 1)
                    v.tensor_tensor(gtile[:, :], PS[dc % 2][:, 0:512], a3s[:, :],
                                    ALU.mult)
                    padV(v, 'fe4')
                    v.wait_ge(S_['fa4'], tb * FA4_PER + 34 + dc)
                    v.wait_ge(S_['wx'], 16 * (tb * NDC + dc + 1))
                    yi = tb * NDC + dc
                    if yi >= 1:
                        v.wait_ge(S_['out'], 16 * yi)
                    v.tensor_tensor(gsil[:, :],
                                    ST[:, dc * QOWN + tb * 512: dc * QOWN + (tb + 1) * 512],
                                    gsil[:, :], ALU.mult)
                    v.tensor_tensor(yout[0][:, :], xfin[dc % 2][:, :], gsil[:, :],
                                    ALU.add)
                    padV(v, 'fy')
    ctx.close()
    return nc


def kernel(L0_attn_x, L0_attn_norm_scale, L0_attn_wq, L0_attn_wk, L0_attn_wv,
           L0_attn_w_out, L0_attn_b_out, L0_ffn_w_up, L0_ffn_w_down):
    x = np.asarray(L0_attn_x, dtype=np.float32)
    wq = np.ascontiguousarray(np.asarray(L0_attn_wq, np.float32).reshape(D, D))
    wk = np.ascontiguousarray(np.asarray(L0_attn_wk, np.float32).reshape(D, D))
    wv = np.ascontiguousarray(np.asarray(L0_attn_wv, np.float32).reshape(D, D))
    wo = np.ascontiguousarray(np.asarray(L0_attn_w_out, np.float32).reshape(D, D))
    nsrow = np.ascontiguousarray(np.asarray(L0_attn_norm_scale, np.float32).reshape(1, D))
    boutc = np.ascontiguousarray(np.asarray(L0_attn_b_out, np.float32).reshape(NDC, 128).T)
    wupT = np.ascontiguousarray(np.asarray(L0_ffn_w_up, np.float32).T)
    wdnT = np.ascontiguousarray(np.asarray(L0_ffn_w_down, np.float32).T)
    xTb = [np.ascontiguousarray(x[b].T) for b in range(2)]

    nc = build_program()
    in_maps = []
    for c in range(8):
        b, qi = c // 4, c % 4
        xTr = np.ascontiguousarray(np.roll(xTb[b], -qi * QOWN, axis=1))
        xTo = np.ascontiguousarray(x[b, qi * QOWN:(qi + 1) * QOWN, :].T)
        in_maps.append({"xT": xTr, "xTo": xTo, "wq": wq, "wk": wk, "wv": wv, "wo": wo,
                        "nsr": nsrow, "boc": boutc, "wupT": wupT, "wdnT": wdnT})
    res = run_bass_kernel_spmd(nc, in_maps, core_ids=list(range(8)))
    y = np.empty((2, S, D), dtype=np.float32)
    for c in range(8):
        b, qi = c // 4, c % 4
        y[b, qi * QOWN:(qi + 1) * QOWN, :] = res.results[c]["yT"].T
    return y


# version 22
# speedup vs baseline: 1.2177x; 1.0049x over previous
import numpy as np
from contextlib import ExitStack

import concourse.bass as bass
import concourse.mybir as mybir
from concourse.bass_utils import run_bass_kernel_spmd

F32 = mybir.dt.float32
BF16 = mybir.dt.bfloat16
AF = mybir.ActivationFunctionType
ALU = mybir.AluOpType

D, S, QOWN = 1024, 8192, 2048
NDC, NPAIR, NCH, NCP, NQT, QT = 8, 8, 64, 32, 4, 512
FF, NFC, TB = 4096, 32, 512
NB1 = S // TB
NB2 = QOWN // TB
GP1B, GP4 = NB1, NB1
NG = GP4 + NB2
NGA = GP4
RT2 = 0.35355339059327373
INV_D = 1.0 / D
TAUF = 0.5 / (D * FF)

TERN_SCHED = {
    0: [('u1', k) for k in range(16)],
    1: [('u1', k) for k in range(16, 32)],
    2: [('d1', k) for k in range(16)],
    3: [('d1', k) for k in range(16, 32)],
    4: [('u2', k) for k in range(16)],
    5: [('u2', k) for k in range(16, 32)],
    6: [('d2', k) for k in range(16)],
    7: [('d2', k) for k in range(16, 32)],
}
TERN_UNITS = [u for p in range(8) for u in TERN_SCHED[p]]


def tern_out_dmas(kind):
    return 1 if kind in ('u2', 'd2') else 0


TERN_OUT_BEFORE = [0]
for _u in TERN_UNITS:
    TERN_OUT_BEFORE.append(TERN_OUT_BEFORE[-1] + tern_out_dmas(_u[0]))
N_TERN_OUT = TERN_OUT_BEFORE[-1]

F4_PER = 41
FA4_PER = 41
FE4_PER = 42
FS_PER = 32


def blk_kind(g):
    return 'p1' if g < GP1B else 'p4'


def n_units(g):
    if blk_kind(g) == 'p4':
        return 0
    return 24 if g < NB2 else 16


UB = [0] * (NG + 1)
KO = [0] * (NG + 1)
VO = [0] * (NG + 1)
for g in range(NG):
    UB[g + 1] = UB[g] + n_units(g)
    KO[g + 1] = KO[g] + (0 if blk_kind(g) == 'p4' else (16 if g < NB2 else 8))
    VO[g + 1] = VO[g] + (8 if blk_kind(g) == 'p1' else 0)


def tc128(i):
    return (i // 2) * 128


def build_program(stop='full'):
    rank = {'p1': 1, 'p2': 2, 'p3': 3, 'full': 4, 'dbg': 5, 'p4s': 6}[stop]
    p4_full = rank != 6
    nc = bass.Bass("TRN2", target_bir_lowering=False)

    xT = nc.dram_tensor("xT", [D, S], F32, kind="ExternalInput").ap()
    xTo = nc.dram_tensor("xTo", [D, QOWN], F32, kind="ExternalInput").ap()
    wq_d = nc.dram_tensor("wq", [D, D], F32, kind="ExternalInput").ap()
    wk_d = nc.dram_tensor("wk", [D, D], F32, kind="ExternalInput").ap()
    wv_d = nc.dram_tensor("wv", [D, D], F32, kind="ExternalInput").ap()
    wo_d = nc.dram_tensor("wo", [D, D], F32, kind="ExternalInput").ap()
    nsr = nc.dram_tensor("nsr", [1, D], F32, kind="ExternalInput").ap()
    boc = nc.dram_tensor("boc", [128, NDC], F32, kind="ExternalInput").ap()
    wupT_d = nc.dram_tensor("wupT", [D, FF], F32, kind="ExternalInput").ap()
    wdnT_d = nc.dram_tensor("wdnT", [FF, D], F32, kind="ExternalInput").ap()
    yT = nc.dram_tensor("yT", [D, QOWN], F32, kind="ExternalOutput").ap()
    dbg1 = nc.dram_tensor("dbg1", [128, S], BF16, kind="ExternalOutput").ap()
    dbg2 = nc.dram_tensor("dbg2", [128, S], BF16, kind="ExternalOutput").ap()
    dbg3 = nc.dram_tensor("dbg3", [128, QOWN], BF16, kind="ExternalOutput").ap()
    dbg4 = nc.dram_tensor("dbg4", [128, NPAIR * QOWN], BF16, kind="ExternalOutput").ap()
    dbg5 = nc.dram_tensor("dbg5", [128, 6144], F32, kind="ExternalOutput").ap()
    dbgH = nc.dram_tensor("dbgH", [128, 16384], BF16, kind="ExternalOutput").ap()
    dbgM = nc.dram_tensor("dbgM", [128, 16384], BF16, kind="ExternalOutput").ap()

    KT_d = nc.dram_tensor("KT_d", [D, S], BF16).ap()
    V_d = nc.dram_tensor("V_d", [S, D], BF16).ap()
    QT_d = nc.dram_tensor("QT_d", [D, QOWN], BF16).ap()
    WUPt_d = nc.dram_tensor("WUPt_d", [NDC * NFC, 128 * 128], BF16).ap()
    WDNt_d = nc.dram_tensor("WDNt_d", [NFC * NDC, 128 * 128], BF16).ap()
    recs_d = nc.dram_tensor("recs_d", [2, 8 * 2048], F32).ap()

    ctx = ExitStack()
    ten = lambda *a: ctx.enter_context(nc.sbuf_tensor(*a))
    psu = lambda *a: ctx.enter_context(nc.psum_tensor(*a))
    S_ = {}
    for n in ("ld wl woc xd xq stm sa sb ab pbc hh kv ev eo vo p2 ksq kn knr cb tau sc ex av tl "
              "bc tw tv to wo p3 p3e f4 fs fa4 fe4 fy out dmp wd wx rc rl nn p3n p2b wu0 wu1 wu2 wu3 wd0 wd1 wd2 xd4 tw1 tw2 tw3").split():
        S_[n] = ctx.enter_context(nc.semaphore("s_" + n))

    ones_b = ten("ones_b", [128, 1], BF16)
    onesr_b = ten("onesr_b", [1, 128], BF16)
    onesr_f = ten("onesr_f", [1, 128], F32)
    ones_f = ten("ones_f", [128, 1], F32)
    sel = ten("sel", [128, 128], F32)
    selk = ten("selk", [128, 64], BF16)
    nsrow_b = ten("nsrow_b", [1, D], BF16)
    nscol = ten("nscol", [128, NDC], F32)
    nsrow_f = ten("nsrow_f", [1, D], F32)
    bout_c = ten("bout_c", [128, NDC], F32)
    eps_ln = ten("eps_ln", [1, 1], F32)
    scr1 = ten("scr1", [1, 1], F32)
    scrV = ten("scrV", [1, 1], F32)
    eps_rms = ten("eps_rms", [1, 1], F32)
    eps_t = ten("eps_t", [1, 1], F32)
    eps_f = ten("eps_f", [1, 1], F32)
    ST = ten("ST", [128, NDC * QOWN], BF16)

    PS = [psu("ps%d" % i, [128, 1024], F32) for i in range(4)]
    pA = [PS[0][:, 0:512], PS[1][:, 0:512]]
    pB = [PS[0][:, 512:1024], PS[1][:, 512:1024]]
    pKV = [PS[2][:, 0:512], PS[2][:, 512:1024]]
    pS1 = PS[3][0:1, 0:512]
    pS2 = PS[3][0:1, 512:1024]
    p_nv = PS[3][:, 0:512]
    p_sum = PS[3][:, 512:1024]

    block = ctx.enter_context(nc.Block())

    def padA(sc, sem, n=1):
        sc.activation(scr1[0:1, 0:1], scr1[0:1, 0:1], AF.Identity).then_inc(S_[sem], n)

    def padV(v, sem, n=1):
        v.tensor_copy(scrV[0:1, 0:1], scrV[0:1, 0:1]).then_inc(S_[sem], n)

    def RS(env, w, i):
        return env[w][0:1, i * TB:(i + 1) * TB]

    def stats_pe_stats(t, env, g):
        t.wait_ge(S_['xq'], g + 1)
        if g >= 1:
            t.wait_ge(S_['sa'], 3 * (g - 1) + 1)
        for c in range(NDC):
            t.matmul(pS1, ones_b[:, :], env['xbf'][:, c * TB:(c + 1) * TB],
                     start=(c == 0), stop=(c == NDC - 1), skip_group_check=True)
        mm = None
        for c in range(NDC):
            mm = t.matmul(pS2, ones_b[:, :], env['xsq'][:, c * TB:(c + 1) * TB],
                          start=(c == 0), stop=(c == NDC - 1), skip_group_check=True)
        mm.then_inc(S_['stm'], 1)

    def stats_dve_pre(v, env, g):
        if 'xd4base' in env:
            v.wait_ge(S_['xd4'], 128 * (g - env['xd4base'] + 1))
        else:
            v.wait_ge(S_['xd'], 16 * (g + 1))
        v.tensor_copy(env['xbf'][:, :], env['xblk'][g % 2][:, :])
        v.tensor_tensor(env['xsq'][:, :], env['xbf'][:, :], env['xbf'][:, :],
                        ALU.mult).then_inc(S_['xq'], 1)

    def stats_dve_rows1(v, env, g, n=1):
        v.wait_ge(S_['stm'], g + 1)
        v.tensor_single_scalar(RS(env, 'rowA', 0), pS1, INV_D, ALU.mult)
        v.tensor_single_scalar(RS(env, 'rowA', 5), pS2, INV_D, ALU.mult)
        v.tensor_tensor(RS(env, 'rowB', 0), RS(env, 'rowA', 0), RS(env, 'rowA', 0), ALU.mult)
        v.tensor_tensor(RS(env, 'rowA', 1), RS(env, 'rowA', 5), RS(env, 'rowB', 0),
                        ALU.subtract)
        padV(v, 'sa', n)

    def stats_act_fused(sc, env, g):
        sc.wait_ge(S_['sa'], 3 * g + 3)
        sc.activation(RS(env, 'rowA', 2), RS(env, 'rowA', 1), AF.Ln,
                      scale=1.0 + 1e-6, bias=eps_f[0:1, 0:1])
        sc.activation(RS(env, 'rowA', 4), RS(env, 'rowA', 2), AF.Exp, scale=-0.5)
        padA(sc, 'sb', 3)

    def stats_act_rows1(sc, env, g):
        sc.wait_ge(S_['sa'], 3 * g + 1)
        sc.activation(RS(env, 'rowA', 2), RS(env, 'rowA', 1), AF.Ln, bias=eps_ln[0:1, 0:1])
        sc.activation(RS(env, 'rowB', 1), RS(env, 'rowA', 2), AF.Exp, scale=-1.0)
        padA(sc, 'sb')

    def stats_dve_rows2(v, env, g):
        v.wait_ge(S_['sb'], 3 * g + 1)
        v.tensor_tensor(RS(env, 'rowA', 3), RS(env, 'rowA', 1), RS(env, 'rowB', 1),
                        ALU.mult)
        padV(v, 'sa')

    def stats_act_rows2(sc, env, g, kind):
        sc.wait_ge(S_['sa'], 3 * g + 2)
        if kind != 'p4':
            sc.activation(RS(env, 'rowB', 0), RS(env, 'rowA', 3), AF.Ln,
                          bias=eps_rms[0:1, 0:1])
            padA(sc, 'sb')
        else:
            sc.activation(RS(env, 'rowB', 0), RS(env, 'rowA', 3), AF.Ln)
            sc.activation(RS(env, 'rowB', 1), RS(env, 'rowB', 0), AF.Exp, scale=0.5)
            sc.activation(RS(env, 'rowB', 3), RS(env, 'rowA', 2), AF.Exp, scale=0.5)
            padA(sc, 'sb')

    def stats_dve_rows3(v, env, g, kind):
        v.wait_ge(S_['sb'], 3 * g + 2)
        if kind != 'p4':
            v.tensor_tensor(RS(env, 'rowB', 0), RS(env, 'rowA', 2), RS(env, 'rowB', 0),
                            ALU.add)
            padV(v, 'sa')
        else:
            v.tensor_single_scalar(RS(env, 'rowB', 0), RS(env, 'rowB', 1), 1e-8, ALU.add)
            v.tensor_tensor(RS(env, 'rowB', 0), RS(env, 'rowB', 3), RS(env, 'rowB', 0),
                            ALU.mult)
            padV(v, 'sa')

    def stats_act_rows3(sc, env, g, kind):
        sc.wait_ge(S_['sa'], 3 * g + 3)
        if kind != 'p4':
            sc.activation(RS(env, 'rowA', 4), RS(env, 'rowB', 0), AF.Exp, scale=-0.5)
            padA(sc, 'sb')
        else:
            sc.activation(RS(env, 'rowB', 3), RS(env, 'rowB', 0), AF.Ln)
            sc.activation(RS(env, 'rowA', 4), RS(env, 'rowB', 3), AF.Exp, scale=-1.0)
            padA(sc, 'sb')

    def stats_dve_rows4(v, env, g):
        v.wait_ge(S_['sb'], 3 * g + 3)
        v.tensor_tensor(RS(env, 'rowB', 2), RS(env, 'rowA', 0), RS(env, 'rowA', 4), ALU.mult)
        v.tensor_copy(RS(env, 'rowC', 0), RS(env, 'rowA', 4))
        v.tensor_tensor(RS(env, 'rowC', 1), RS(env, 'rowA', 4), RS(env, 'rowC', 0), ALU.subtract)
        v.tensor_scalar(RS(env, 'rowC', 2), RS(env, 'rowB', 2), -1.0, None, ALU.mult)
        v.scalar_tensor_tensor(RS(env, 'rowC', 3), RS(env, 'rowB', 2), -1.0,
                               RS(env, 'rowC', 2), ALU.mult, ALU.subtract)
        padV(v, 'ab')

    def stats_pe_work_bcast(t, env, g):
        t.wait_ge(S_['ab'], g + 1)
        if g >= 2:
            t.wait_ge(S_['hh'], 8 * (g - 1))
        ls = onesr_b[0:1, 0:128]
        t.matmul(pA[g % 2], ls, RS(env, 'rowC', 0), start=True, stop=False,
                 skip_group_check=True)
        t.matmul(pA[g % 2], ls, RS(env, 'rowC', 1), start=False, stop=True,
                 skip_group_check=True)
        t.matmul(pB[g % 2], ls, RS(env, 'rowC', 2), start=True, stop=False,
                 skip_group_check=True)
        t.matmul(pB[g % 2], ls, RS(env, 'rowC', 3), start=False, stop=True,
                 skip_group_check=True).then_inc(S_['pbc'], 1)

    def stats_pe_work(t, env, g, kind, wkt=None, wvt=None, wqt=None, split=False):
        if not split:
            stats_pe_work_bcast(t, env, g)
        nu = n_units(g)
        if nu == 0:
            return
        t.wait_ge(S_['hh'], 8 * (g + 1))
        for u in range(nu):
            if UB[g] + u - 1 >= 1:
                t.wait_ge(S_['ev'], UB[g] + u - 1)
            mm = None
            if u < 8:
                for dc in range(NDC):
                    mm = t.matmul(pKV[u % 2], wkt[:, dc * D + u * 128: dc * D + (u + 1) * 128],
                                  env['hT'][g % 2][:, dc * TB:(dc + 1) * TB],
                                  start=(dc == 0), stop=(dc == NDC - 1), skip_group_check=True)
            elif u < 16:
                i = u - 8
                tc, half = i // 2, i % 2
                for dc in range(NDC):
                    mm = t.matmul(pKV[u % 2],
                                  env['hT'][g % 2][:, dc * TB + tc * 128: dc * TB + (tc + 1) * 128],
                                  wvt[:, dc * D + half * 512: dc * D + (half + 1) * 512],
                                  start=(dc == 0), stop=(dc == NDC - 1), skip_group_check=True)
            else:
                uq = u - 16
                for dc in range(NDC):
                    mm = t.matmul(pKV[u % 2], wqt[:, dc * D + uq * 128: dc * D + (uq + 1) * 128],
                                  env['hT'][g % 2][:, dc * TB:(dc + 1) * TB],
                                  start=(dc == 0), stop=(dc == NDC - 1), skip_group_check=True)
            mm.then_inc(S_['kv'], 1)

    def stats_dve_h(v, env, g, kind):
        for c in range(NDC):
            if c == 0:
                v.wait_ge(S_['pbc'], g + 1)
            if kind == 'p4':
                v.tensor_tensor(env['hsc'][:, :],
                                env['xblk'][g % 2][:, c * TB:(c + 1) * TB], pA[g % 2],
                                ALU.mult)
                boff = (g - GP4) * TB
                dst = env['hdst'][:, c * QOWN + boff: c * QOWN + boff + TB]
                v.tensor_tensor(dst, env['hsc'][:, :], pB[g % 2],
                                ALU.add).then_inc(S_['hh'], 1)
            else:
                v.scalar_tensor_tensor(env['hsc'][:, :],
                                       env['xblk'][g % 2][:, c * TB:(c + 1) * TB],
                                       nscol[:, c:c + 1], pA[g % 2], ALU.mult, ALU.mult)
                dst = env['hT'][g % 2][:, c * TB:(c + 1) * TB]
                v.scalar_tensor_tensor(dst, pB[g % 2], nscol[:, c:c + 1],
                                       env['hsc'][:, :],
                                       ALU.mult, ALU.add).then_inc(S_['hh'], 1)

    def stats_act_evac(sc, env, g, evK=None, evV=None, hooks=None):
        for u in range(n_units(g)):
            if hooks and u in hooks:
                hooks[u]()
            sc.wait_ge(S_['kv'], UB[g] + u + 1)
            if 8 <= u < 16:
                i = u - 8
                if VO[g] + i - 4 >= 0:
                    sc.wait_ge(S_['vo'], 16 * (VO[g] + i - 3))
                sc.activation(evV[i % 4][:, :], pKV[u % 2],
                              AF.Identity).then_inc(S_['ev'], 1)
            else:
                ko = u if u < 8 else 8 + (u - 16)
                if KO[g] + ko - 4 >= 0:
                    sc.wait_ge(S_['eo'], 16 * (KO[g] + ko - 3))
                sc.activation(evK[ko % 4][:, :], pKV[u % 2], AF.Identity,
                              scale=RT2).then_inc(S_['ev'], 1)

    with ExitStack() as sc1:
        t1 = lambda *a: sc1.enter_context(nc.sbuf_tensor(*a))
        xblk = [t1("xblk%d" % i, [128, NDC * TB], F32) for i in range(2)]
        xbf = t1("xbf", [128, NDC * TB], BF16)
        xsq = t1("xsq", [128, NDC * TB], BF16)
        hT = [t1("hT%d" % i, [128, NDC * TB], BF16) for i in range(2)]
        evK = [t1("evK%d" % i, [128, TB], BF16) for i in range(4)]
        evV = [t1("evV%d" % i, [128, TB], BF16) for i in range(4)]
        wqt = t1("wqt", [128, NDC * D], BF16)
        wkt = t1("wkt", [128, NDC * D], BF16)
        wvt = t1("wvt", [128, NDC * D], BF16)
        rowA = t1("rowA", [1, 6 * TB], F32)
        rowB = t1("rowB", [1, 4 * TB], F32)
        rowC = t1("rowC", [1, 4 * TB], BF16)
        hsc = t1("hsc", [128, TB], BF16)
        env1 = dict(xblk=xblk, xbf=xbf, xsq=xsq, hT=hT, rowA=rowA, rowB=rowB, rowC=rowC,
                    hsc=hsc)

        def blk_src(g):
            return xT, g * TB

        @block.sync
        def _(sy):
            sy.dma_start(nsrow_f[:, :], nsr).then_inc(S_['ld'], 16)
            sy.dma_start(nscol[:, :], nsr.rearrange("a (c p) -> p (a c)", p=128),
                         allow_slow_non_contiguous=True).then_inc(S_['ld'], 16)
            sy.dma_start(bout_c[:, :], boc).then_inc(S_['ld'], 16)
            for g in range(NGA):
                src, off = blk_src(g)
                if g >= 2:
                    sy.wait_ge(S_['hh'], 8 * (g - 1))
                sy.dma_start(xblk[g % 2][:, :].rearrange("p (c t) -> p c t", c=NDC),
                             src[:, off:off + TB].rearrange("(c p) t -> p c t", c=NDC)
                             ).then_inc(S_['xd'], 16)
            if rank == 1:
                sy.wait_ge(S_['eo'], 16 * KO[NGA])
                sy.wait_ge(S_['vo'], 16 * VO[NGA])
                sy.dma_start(dbg1, KT_d[0:128, :]).then_inc(S_['out'], 16)
                sy.dma_start(dbg2.rearrange("p (a d) -> p a d", d=D),
                             V_d[0:8 * 128, :].rearrange("(a p) d -> p a d", p=128)
                             ).then_inc(S_['out'], 16)
                sy.dma_start(dbg3, QT_d[0:128, :]).then_inc(S_['out'], 16)
                sy.wait_ge(S_['out'], 48)

        @block.gpsimd
        def _(gp_):
            gp_.memset(ones_b[:, :], 1.0)
            gp_.memset(onesr_b[:, :], 1.0)
            gp_.memset(onesr_f[:, :], 1.0)
            gp_.memset(ones_f[:, :], 1.0)
            gp_.memset(sel[:, :], 0.0)
            gp_.memset(sel[0:1, 0:64], 1.0)
            gp_.memset(selk[:, :], 0.0)
            gp_.memset(selk[:, 0:1], 1.0)
            gp_.memset(sel[64:65, 64:128], 1.0)
            gp_.memset(eps_ln[:, :], 1e-5)
            gp_.memset(eps_rms[:, :], 1e-6)
            gp_.memset(eps_t[:, :], 1e-8)
            gp_.memset(eps_f[:, :], 1e-11)
            gp_.memset(scr1[:, :], 0.0)
            gp_.memset(scrV[:, :], 0.0).then_inc(S_['ld'], 1)
            gp_.dma_start(wqt[:, :].rearrange("p (c e) -> p c e", c=NDC), wq_d.rearrange("(c p) e -> p c e", c=NDC)
                          ).then_inc(S_['wl'], 16)
            gp_.dma_start(wkt[:, :].rearrange("p (c e) -> p c e", c=NDC), wk_d.rearrange("(c p) e -> p c e", c=NDC)
                          ).then_inc(S_['wl'], 16)
            gp_.dma_start(wvt[:, :].rearrange("p (c e) -> p c e", c=NDC), wv_d.rearrange("(c p) e -> p c e", c=NDC)
                          ).then_inc(S_['wl'], 16)
            for g in range(NGA):
                for u in range(8):
                    gp_.wait_ge(S_['ev'], UB[g] + u + 2)
                    gp_.dma_start(KT_d[u * 128:(u + 1) * 128, g * TB:(g + 1) * TB],
                                  evK[u % 4][:, :]).then_inc(S_['eo'], 16)
                for i in range(8):
                    gp_.wait_ge(S_['ev'], UB[g] + 8 + i + 2)
                    gp_.dma_start(V_d[g * TB + tc128(i): g * TB + tc128(i) + 128,
                                      (i % 2) * 512:((i % 2) + 1) * 512],
                                  evV[i % 4][:, :]).then_inc(S_['vo'], 16)
                if n_units(g) > 16:
                    for u in range(16, 24):
                        gp_.wait_ge(S_['ev'], UB[g] + u + 2)
                        gp_.dma_start(QT_d[(u - 16) * 128:(u - 15) * 128,
                                           g * TB:(g + 1) * TB],
                                      evK[(8 + u - 16) % 4][:, :]).then_inc(S_['eo'], 16)

        @block.vector
        def _(v):
            v.wait_ge(S_['ld'], 49)
            v.wait_ge(S_['wl'], 48)
            v.tensor_copy(nsrow_b[:, :], nsrow_f[:, :])
            stats_dve_pre(v, env1, 0)

            def rows_chain(g2):
                kind2 = blk_kind(g2)
                stats_dve_rows1(v, env1, g2)
                stats_dve_rows2(v, env1, g2)
                stats_dve_rows3(v, env1, g2, kind2)
                stats_dve_rows4(v, env1, g2)
            rows_chain(0)
            stats_dve_pre(v, env1, 1)
            for g in range(NGA):
                kind = blk_kind(g)
                stats_dve_h(v, env1, g, kind)
                if g + 1 < NGA:
                    g2 = g + 1
                    stats_dve_rows1(v, env1, g2)
                    stats_dve_rows2(v, env1, g2)
                    stats_dve_rows3(v, env1, g2, blk_kind(g2))
                    stats_dve_rows4(v, env1, g2)
                    if g2 + 1 < NGA:
                        stats_dve_pre(v, env1, g2 + 1)

        @block.scalar
        def _(sc):
            sc.wait_ge(S_['ld'], 1)
            stats_act_rows1(sc, env1, 0)
            stats_act_rows2(sc, env1, 0, blk_kind(0))
            stats_act_rows3(sc, env1, 0, blk_kind(0))
            for g in range(NGA):
                nu = n_units(g)
                if g + 1 < NGA:
                    g2 = g + 1
                    hooks = {
                        5 * nu // 8: (lambda g2=g2: stats_act_rows1(sc, env1, g2)),
                        6 * nu // 8: (lambda g2=g2: stats_act_rows2(sc, env1, g2,
                                                                    blk_kind(g2))),
                        7 * nu // 8: (lambda g2=g2: stats_act_rows3(sc, env1, g2,
                                                                    blk_kind(g2))),
                    }
                else:
                    hooks = None
                stats_act_evac(sc, env1, g, evK=evK, evV=evV, hooks=hooks)
            padA(sc, 'ev')

        @block.tensor
        def _(t):
            t.wait_ge(S_['ld'], 49)
            t.wait_ge(S_['wl'], 48)
            for g in range(NGA):
                if g >= 1:
                    stats_pe_work_bcast(t, env1, g - 1)
                stats_pe_stats(t, env1, g)
                if g >= 1:
                    stats_pe_work(t, env1, g - 1, blk_kind(g - 1), wkt=wkt, wvt=wvt,
                                  wqt=wqt, split=True)
            stats_pe_work(t, env1, NGA - 1, blk_kind(NGA - 1), wkt=wkt, wvt=wvt, wqt=wqt)

    if rank < 2:
        ctx.close()
        return nc
    with ExitStack() as sc2:
        t2 = lambda *a: sc2.enter_context(nc.sbuf_tensor(*a))
        Kb = [t2("Kb%d" % i, [128, S], BF16) for i in range(2)]
        Vb = [t2("Vb%d" % i, [128, S], BF16) for i in range(2)]
        Qb = [t2("Qb%d" % i, [128, QOWN], BF16) for i in range(2)]
        NV = t2("NV", [128, NPAIR * QOWN], BF16)
        ptl = {0: [t2("pA%d" % i, [128, 1024], BF16) for i in range(2)],
               1: [t2("pB%d" % i, [128, 1024], BF16) for i in range(2)]}
        sums = t2("sums", [128, QT], F32)
        sums2 = t2("sums2", [128, QT], F32)
        sumsT = [sums, sums2]
        recs = t2("recs", [128, QT], F32)
        recs2 = t2("recs2", [128, QT], F32)
        recsT = [recs, recs2]
        recsP3 = [t2("recsP3%d" % i, [65, QT], F32) for i in range(2)]
        ksq = t2("ksq", [128, 2048], BF16)
        kqc = t2("kqc", [128, 24], F32)
        biasA = t2("biasA", [128, 1], F32)
        biasB = t2("biasB", [128, 1], F32)
        cneg = t2("cneg", [1, 2], BF16)
        taur = t2("taur", [128, 1], F32)
        tau_up = t2("tau_up", [128, 1], F32)
        tau_dn = t2("tau_dn", [128, 1], F32)
        twk = [t2("twk%d" % i, [128, 1024], F32) for i in range(3)]
        TWS = None
        tabs = t2("tabs", [128, 1024], F32)
        tmf = t2("tmf", [128, 1024], BF16)
        tout = [t2("tout%d" % i, [128, 1024], BF16) for i in range(2)]
        tacc = t2("tacc", [128, 64], F32)
        wot = t2("wot", [128, NDC * D], BF16)

        slots = [PS[0], PS[1], PS[2]]

        def tern_src(kind, k):
            if kind in ('u1', 'u2'):
                dc, q4 = k // 4, k % 4
                return wupT_d[dc * 128:(dc + 1) * 128, q4 * 1024:(q4 + 1) * 1024], 1024
            return wdnT_d[k * 128:(k + 1) * 128, :], 1024

        @block.sync
        def _(sy):
            for j in range(NPAIR):
                if j == 0:
                    sy.wait_ge(S_['eo'], 16 * KO[NGA])
                    sy.wait_ge(S_['vo'], 16 * VO[NGA])
                if j >= 2:
                    sy.wait_ge(S_['av'], NQT * NCP * (j - 1))
                p2s = S_['p2'] if j % 2 == 0 else S_['p2b']
                sy.dma_start(Kb[j % 2][:, :], KT_d[j * 128:(j + 1) * 128, :]
                             ).then_inc(p2s, 16)
                sy.dma_start(Qb[j % 2][:, :], QT_d[j * 128:(j + 1) * 128, :]
                             ).then_inc(p2s, 16)
                for i in range(8):
                    sy.dma_start(Vb[j % 2][:, i * 1024:(i + 1) * 1024]
                                 .rearrange("p (c e) -> p c e", c=8),
                                 V_d[i * 1024:(i + 1) * 1024, j * 128:(j + 1) * 128]
                                 .rearrange("(c p) e -> p c e", p=128)).then_inc(p2s, 16)

        @block.gpsimd
        def _(gp_):
            gp_.wait_ge(S_['eo'], 16 * KO[NGA])
            gp_.wait_ge(S_['vo'], 16 * VO[NGA])
            gp_.wait_ge(S_['ev'], UB[NGA])
            gp_.memset(recs[:, :], 0.0)
            gp_.memset(recs2[:, :], 0.0)
            gp_.memset(recsP3[0][:, :], 0.0)
            gp_.memset(recsP3[1][:, :], 0.0)
            gp_.memset(tacc[:, :], 0.0).then_inc(S_['tv'], 1)
            gp_.dma_start(wot[:, :].rearrange("p (c e) -> p c e", c=NDC), wo_d.rearrange("(c p) e -> p c e", c=NDC)
                          ).then_inc(S_['woc'], 16)
            def tern_out(gp_, i, kind, k):
                gp_.wait_ge(S_['tv'], 1 + i + 1)
                if kind == 'u2':
                    dc, q4 = k // 4, k % 4
                    r0 = dc * NFC + q4 * 8
                    dst = WUPt_d[r0:r0 + 8, :]
                else:
                    r0 = k * NDC
                    dst = WDNt_d[r0:r0 + 8, :]
                gp_.dma_start(dst.rearrange("f (p e) -> p f e", p=128),
                              tout[i % 2][:, 0:1024].rearrange("p (f e) -> p f e", f=8)
                              ).then_inc(S_['to'], 16)

            pend_out = [None]
            TWS = [S_['tw'], S_['tw1'], S_['tw2']]
            for i, (kind, k) in enumerate(TERN_UNITS):
                if i >= 3:
                    gp_.wait_ge(S_['tv'], 1 + i - 2)
                src, w = tern_src(kind, k)
                gp_.dma_start(twk[i % 3][:, 0:w], src).then_inc(TWS[i % 3], 16)
                if pend_out[0] is not None:
                    tern_out(gp_, *pend_out[0])
                    pend_out[0] = None
                if kind in ('u2', 'd2'):
                    pend_out[0] = (i, kind, k)
            tern_out(gp_, *pend_out[0])
            pend_out[0] = None

        def _avblock(t, j, qg, prev, u, first, drain=False, need_cb=False):
            t.wait_ge(S_['ex'], u if drain else u - 2)
            if first:
                t.wait_ge(S_['tl'], 2 * qg)
                if need_cb:
                    t.wait_ge(S_['cb'], j + 1)
            mm = None
            for ci in (0, 1):
                c = 2 * prev + ci
                fst = first and ci == 0
                lst = drain and ci == 1
                t.matmul(p_nv[0:64, :], Vb[j % 2][:, c * 128: c * 128 + 64],
                         ptl[0][prev % 2][:, ci * 512:(ci + 1) * 512],
                         start=fst, stop=lst, skip_group_check=True, tile_position=(0, 0))
                t.matmul(p_nv[64:128, :], Vb[j % 2][:, c * 128 + 64: c * 128 + 128],
                         ptl[1][prev % 2][:, ci * 512:(ci + 1) * 512],
                         start=fst, stop=lst, skip_group_check=True, tile_position=(0, 64))
                t.matmul(p_sum[0:1, :], ones_b[:, :],
                         ptl[0][prev % 2][:, ci * 512:(ci + 1) * 512],
                         start=fst, stop=lst, skip_group_check=True, tile_position=(0, 0))
                mm = t.matmul(p_sum[64:65, :], ones_b[:, :],
                              ptl[1][prev % 2][:, ci * 512:(ci + 1) * 512],
                              start=fst, stop=lst, skip_group_check=True, tile_position=(0, 64))
            mm.then_inc(S_['av'], 1)

        @block.tensor
        def _(t):
            u = 0
            t.wait_ge(S_['tv'], 1)
            for j in range(NPAIR):
                jq0 = j * NQT
                t.wait_ge(S_['p2'] if j % 2 == 0 else S_['p2b'], 160 * (j // 2 + 1))
                t.wait_ge(S_['tl'], 2 * jq0)
                for nm, pj in (('up', 2), ('dn', 4)):
                    if j == pj:
                        e = 0 if nm == 'up' else 1
                        t.wait_ge(S_['tau'], 5 * e + 1)
                        t.matmul(p_nv[0:1, 0:1], ones_f[:, :], taur[:, :],
                                 start=True, stop=True,
                                 skip_group_check=True).then_inc(S_['tau'], 1)
                        t.wait_ge(S_['tau'], 5 * e + 3)
                        t.matmul(p_nv[:, 2:3], onesr_f[:, :], kqc[0:1, 22:23],
                                 start=True, stop=True,
                                 skip_group_check=True).then_inc(S_['tau'], 1)
                        t.wait_ge(S_['tau'], 5 * e + 5)
                slK = slots[(128 * j + 2) % 3]
                fbK = [p_nv, p_sum, slK[:, 0:512], slK[:, 512:1024]]
                for f in range(20):
                    if f < 16:
                        qb, cg = f // 4, f % 4
                        t.wait_ge(S_['ksq'], 5 * j + qb + 1)
                        srcq = ksq
                    else:
                        cg = f - 16
                        t.wait_ge(S_['ksq'], 5 * j + 5)
                        srcq = ksq
                    if f >= 4:
                        t.wait_ge(S_['knr'], 11 * j + (f - 4) // 2 + 1)
                    bank = fbK[f % 4]
                    t.matmul(bank[0:64, :], selk[0:64, :], srcq[0:64, cg * 512:(cg + 1) * 512],
                             start=True, stop=True, skip_group_check=True, tile_position=(0, 0))
                    t.matmul(bank[64:128, :], selk[64:128, :],
                             srcq[64:128, cg * 512:(cg + 1) * 512],
                             start=True, stop=True, skip_group_check=True,
                             tile_position=(64, 64)).then_inc(S_['kn'], 1)
                t.wait_ge(S_['knr'], 11 * j + 11)
                t.matmul(p_nv[:, 0:1], onesr_b[:, :], cneg[0:1, 0:1],
                         start=True, stop=True, skip_group_check=True)
                t.matmul(p_nv[:, 1:2], onesr_b[:, :], cneg[0:1, 1:2],
                         start=True, stop=True, skip_group_check=True).then_inc(S_['kn'], 1)
                t.wait_ge(S_['knr'], 11 * j + 10)
                for qtr in range(NQT):
                    qg = jq0 + qtr
                    qc = qtr * QT
                    for cp in range(NCP):
                        for h_ in (0, 1):
                            slot = slots[(u + h_) % 3]
                            if u + h_ >= 3:
                                t.wait_ge(S_['ex'], u + h_ - 2)
                            base = 64 * h_
                            mm = None
                            for ci in (0, 1):
                                c = 2 * cp + ci
                                mm = t.matmul(slot[:, ci * 512:(ci + 1) * 512],
                                              Kb[j % 2][base:base + 64, c * 128:(c + 1) * 128],
                                              Qb[j % 2][base:base + 64, qc:qc + QT],
                                              start=True, stop=True, skip_group_check=True,
                                              tile_position=(base, 0))
                            mm.then_inc(S_['sc'], 1)
                        u += 2
                        if cp >= 1:
                            _avblock(t, j, qg, cp - 1, u, first=(cp == 1),
                                     need_cb=(qtr == 0))
                    _avblock(t, j, qg, NCP - 1, u, first=False, drain=True)

        def recs_store(eng, qs):
            js, qcs = qs // NQT, (qs % NQT) * QT
            eng.wait_ge(S_['bc'], qs + 1)
            eng.dma_start(recs_d[0:2, js * QOWN + qcs: js * QOWN + qcs + QT],
                          recsT[qs % 2][0:65:64, 0:QT]).then_inc(S_['rc'], 16)

        @block.scalar
        def _(sc):
            u = 0
            for j in range(NPAIR):
                sc.wait_ge(S_['cb'], j + 1)
                for qtr in range(NQT):
                    qg = j * NQT + qtr
                    for cp in range(NCP):
                        if cp == 12 and qg >= 1:
                            recs_store(sc, qg - 1)
                        for h_ in (0, 1):
                            sc.wait_ge(S_['sc'], u + 1)
                            guard = qg * NCP + cp - 1
                            if guard > 0:
                                sc.wait_ge(S_['av'], guard)
                            bias = biasA if h_ == 0 else biasB
                            sc.activation(ptl[h_][cp % 2][:, :], slots[u % 3][:, :], AF.Exp,
                                          bias=bias[:, 0:1], scale=1.0).then_inc(S_['ex'], 1)
                            u += 1



        @block.vector
        def _(v):
            pend_rec = [None]

            def emit_recs(v):
                if pend_rec[0] is None:
                    return
                qg = pend_rec[0]
                pend_rec[0] = None
                if qg >= 2:
                    v.wait_ge(S_['rc'], 16 * (qg - 1))
                v.reciprocal(recsT[qg % 2][0:1, :], sumsT[qg % 2][0:1, :])
                v.reciprocal(recsT[qg % 2][64:65, :], sumsT[qg % 2][64:65, :])
                padV(v, 'bc')
            for j in range(NPAIR):
                jq0 = j * NQT
                for nm, pj in (('up', 2), ('dn', 4)):
                    if j == pj:
                        e = 0 if nm == 'up' else 1
                        cols = (0, 32) if nm == 'up' else (32, 64)
                        v.wait_ge(S_['tv'], 1 + (32 if nm == 'up' else 64))
                        v.reduce_sum(taur[:, 0:1], tacc[:, cols[0]:cols[1]],
                                     axis=mybir.AxisListType.X)
                        padV(v, 'tau')
                        v.wait_ge(S_['tau'], 5 * e + 2)
                        v.tensor_single_scalar(kqc[0:1, 22:23], p_nv[0:1, 0:1], TAUF, ALU.mult)
                        padV(v, 'tau')
                        v.wait_ge(S_['tau'], 5 * e + 4)
                        dst = tau_up if nm == 'up' else tau_dn
                        v.tensor_copy(dst[:, :], p_nv[:, 2:3])
                        padV(v, 'tau')
                slK = slots[(128 * j + 2) % 3]
                fbK = [p_nv, p_sum, slK[:, 0:512], slK[:, 512:1024]]
                v.wait_ge(S_['p2'] if j % 2 == 0 else S_['p2b'], 160 * (j // 2 + 1))
                fb2 = [PS[3][:, 0:1024], slK[:, 0:1024]]
                for qb in range(4):
                    if qb >= 1:
                        v.wait_ge(S_['kn'], 21 * j + 4 * qb)
                    v.tensor_tensor(ksq[:, :], Kb[j % 2][:, qb * 2048:(qb + 1) * 2048],
                                    Kb[j % 2][:, qb * 2048:(qb + 1) * 2048],
                                    ALU.mult).then_inc(S_['ksq'], 1)
                    for r in range(2 * qb, 2 * qb + 2):
                        v.wait_ge(S_['kn'], 21 * j + 2 * r + 2)
                        v.reduce_max(kqc[:, r:r + 1], fb2[r % 2][:, :],
                                     axis=mybir.AxisListType.X).then_inc(S_['knr'], 1)
                v.wait_ge(S_['kn'], 21 * j + 16)
                v.tensor_tensor(ksq[:, :], Qb[j % 2][:, :], Qb[j % 2][:, :],
                                ALU.mult).then_inc(S_['ksq'], 1)
                for r in range(8, 10):
                    v.wait_ge(S_['kn'], 21 * j + 2 * r + 2)
                    v.reduce_max(kqc[:, r:r + 1], fb2[r % 2][:, :],
                                 axis=mybir.AxisListType.X).then_inc(S_['knr'], 1)
                v.wait_ge(S_['knr'], 11 * j + 10)
                v.reduce_max(kqc[:, 20:21], kqc[:, 0:8], axis=mybir.AxisListType.X)
                v.reduce_max(kqc[:, 21:22], kqc[:, 8:10], axis=mybir.AxisListType.X)
                v.tensor_tensor(kqc[:, 22:23], kqc[:, 20:21], kqc[:, 21:22], ALU.add)
                v.tensor_single_scalar(kqc[:, 22:23], kqc[:, 22:23], 0.5, ALU.mult)
                v.tensor_single_scalar(cneg[0:1, 0:1], kqc[0:1, 22:23], -1.0, ALU.mult)
                v.tensor_single_scalar(cneg[0:1, 1:2], kqc[64:65, 22:23], -1.0, ALU.mult)
                padV(v, 'knr')
                if j >= 1:
                    v.wait_ge(S_['ex'], 256 * j)
                v.wait_ge(S_['kn'], 21 * j + 21)
                v.tensor_copy(biasA[:, :], p_nv[:, 0:1])
                v.tensor_copy(biasB[:, :], p_nv[:, 1:2])
                padV(v, 'cb')
                emit_recs(v)
                i0 = sum(len(TERN_SCHED[q]) for q in range(j))
                tu = TERN_SCHED[j]
                for qtr in range(NQT):
                    qg = jq0 + qtr
                    qc = qtr * QT
                    emit_recs(v)
                    for ii in range(4 * qtr, 4 * qtr + 4):
                        if ii >= len(tu):
                            break
                        kind, k = tu[ii]
                        i = i0 + ii
                        tws = [S_['tw'], S_['tw1'], S_['tw2']][i % 3]
                        v.wait_ge(tws, 16 * (i // 3 + 1))
                        w = 1024
                        if kind in ('u1', 'd1'):
                            col = k if kind == 'u1' else 32 + k
                            v.reduce_sum(tacc[:, col:col + 1], twk[i % 3][:, 0:w],
                                         axis=mybir.AxisListType.X,
                                         apply_absolute_value=True).then_inc(S_['tv'], 1)
                        else:
                            tau_t = tau_up if kind == 'u2' else tau_dn
                            if TERN_OUT_BEFORE[i] > 0:
                                v.wait_ge(S_['to'], 16 * TERN_OUT_BEFORE[i])
                            v.scalar_tensor_tensor(tabs[:, 0:w], twk[i % 3][:, 0:w], -1.0,
                                                   twk[i % 3][:, 0:w], ALU.mult, ALU.max)
                            v.tensor_single_scalar(tabs[:, 0:w], tabs[:, 0:w],
                                                   tau_t[:, 0:1], ALU.subtract)
                            v.tensor_single_scalar(tmf[:, 0:w], tabs[:, 0:w], 0.0, ALU.is_gt)
                            v.tensor_single_scalar(tabs[:, 0:w], twk[i % 3][:, 0:w], 0.0,
                                                   ALU.is_gt)
                            v.tensor_scalar(tabs[:, 0:w], tabs[:, 0:w], 2.0, -1.0,
                                            ALU.mult, ALU.add)
                            v.tensor_tensor(tout[i % 2][:, 0:w], tabs[:, 0:w], tmf[:, 0:w],
                                            ALU.mult).then_inc(S_['tv'], 1)
                    v.wait_ge(S_['av'], (qg + 1) * NCP)
                    v.tensor_copy(sumsT[qg % 2][:, :], p_sum).then_inc(S_['tl'], 1)
                    v.tensor_copy(NV[:, j * QOWN + qc: j * QOWN + qc + QT],
                                  p_nv).then_inc(S_['tl'], 1)
                    pend_rec[0] = qg

        @block.vector
        def _(v):
            qg = NPAIR * NQT - 1
            v.wait_ge(S_['rc'], 16 * (qg - 1))
            v.reciprocal(recsT[qg % 2][0:1, :], sumsT[qg % 2][0:1, :])
            v.reciprocal(recsT[qg % 2][64:65, :], sumsT[qg % 2][64:65, :])
            padV(v, 'bc')

        if rank == 2:
            @block.sync
            def _(sy):
                sy.wait_ge(S_['tl'], 2 * NPAIR * NQT)
                sy.wait_ge(S_['to'], 16 * N_TERN_OUT)
                sy.dma_start(dbg4, NV[:, :]).then_inc(S_['out'], 16)
                sy.dma_start(dbg5[:, 0:1], tau_up[:, :], allow_slow_non_contiguous=True).then_inc(S_['out'], 16)
                sy.dma_start(dbg5[:, 1:2], tau_dn[:, :], allow_slow_non_contiguous=True).then_inc(S_['out'], 16)
                sy.dma_start(dbg5[:, 16:80], tacc[:, :]).then_inc(S_['out'], 16)
                sy.dma_start(dbg3[:, 0:1024], tout[0][:, 0:1024]).then_inc(S_['out'], 16)
                sy.dma_start(dbg1.rearrange("p (a q) -> p a q", a=64),
                             WUPt_d[0:64, :].rearrange("a (p q) -> p a q", p=128)
                             ).then_inc(S_['out'], 16)
                sy.dma_start(dbg5[:, 2048:3072], twk[0][:, :]).then_inc(S_['out'], 16)
                sy.dma_start(dbg5[:, 4096:6144],
                             wupT_d[0:128, 0:2048]).then_inc(S_['out'], 16)
                sy.wait_ge(S_['out'], 128)

        p3slots = [PS[0][:, 0:512], PS[0][:, 512:1024], PS[1][:, 0:512], PS[1][:, 512:1024]]

        if rank >= 3:

            nb3 = [PS[2][:, 0:512], PS[2][:, 512:1024], PS[3][:, 0:512], PS[3][:, 512:1024]]

            @block.sync
            def _(sy):
                sy.wait_ge(S_['rc'], 16 * (NPAIR * NQT - 1))
                for qg in range(4):
                    for jj in range(NPAIR):
                        bi = qg * NPAIR + jj
                        if jj == NPAIR - 1 and qg == 3:
                            continue
                        if bi >= 2:
                            sy.wait_ge(S_['nn'], bi - 1)
                        sy.dma_start(recsP3[bi % 2][0:65:64, :],
                                     recs_d[0:2, jj * QOWN + qg * QT: jj * QOWN + (qg + 1) * QT]
                                     ).then_inc(S_['rl'], 16)

            @block.tensor
            def _(t):
                t.wait_ge(S_['tl'], 2 * NPAIR * NQT)
                t.wait_ge(S_['woc'], 16)
                def p3bcast(t, qg2, jj):
                    bi = qg2 * NPAIR + jj
                    if bi >= 4:
                        t.wait_ge(S_['p3n'], bi - 3)
                    if bi == NPAIR * 4 - 1:
                        t.wait_ge(S_['bc'], NPAIR * NQT)
                        src_ = recsT[(NPAIR * 4 - 1) % 2][0:65, :]
                    else:
                        t.wait_ge(S_['rl'], 16 * min(bi + 2, NPAIR * 4 - 1))
                        src_ = recsP3[bi % 2][0:65, :]
                    t.matmul(nb3[bi % 4], sel[0:65, :], src_,
                             start=True, stop=True,
                             skip_group_check=True).then_inc(S_['nn'], 1)
                for jj in range(NPAIR):
                    p3bcast(t, 0, jj)
                for qg in range(4):
                    for dc in range(NDC):
                        idx = qg * NDC + dc
                        if idx >= 4:
                            t.wait_ge(S_['p3e'], idx - 3)
                        t.wait_ge(S_['p3n'], NPAIR * (qg + 1))
                        mm = None
                        for hc in range(NDC):
                            mm = t.matmul(p3slots[idx % 4],
                                          wot[:, hc * D + dc * 128: hc * D + (dc + 1) * 128],
                                          NV[:, hc * QOWN + qg * 512: hc * QOWN + (qg + 1) * 512],
                                          start=(hc == 0), stop=(hc == NDC - 1),
                                          skip_group_check=True)
                        mm.then_inc(S_['p3'], 1)
                        if qg < 3:
                            p3bcast(t, qg + 1, dc)

            @block.vector
            def _(v):
                def p3tt(v, qg2, jj):
                    bi = qg2 * NPAIR + jj
                    v.wait_ge(S_['nn'], bi + 1)
                    v.tensor_tensor(NV[:, jj * QOWN + qg2 * 512: jj * QOWN + (qg2 + 1) * 512],
                                    NV[:, jj * QOWN + qg2 * 512: jj * QOWN + (qg2 + 1) * 512],
                                    nb3[bi % 4], ALU.mult).then_inc(S_['p3n'], 1)
                for jj in range(NPAIR):
                    p3tt(v, 0, jj)
                for qg in range(4):
                    for dc in range(NDC):
                        idx = qg * NDC + dc
                        v.wait_ge(S_['p3'], idx + 1)
                        v.tensor_scalar(ST[:, dc * QOWN + qg * 512: dc * QOWN + (qg + 1) * 512],
                                        p3slots[idx % 4], bout_c[:, dc:dc + 1], None,
                                        ALU.add).then_inc(S_['p3e'], 1)
                        if qg < 3:
                            p3tt(v, qg + 1, dc)

    if rank == 3:
        @block.sync
        def _(sy):
            sy.wait_ge(S_['p3e'], 32)
            sy.dma_start(dbg4, ST[:, :]).then_inc(S_['out'], 16)
            sy.wait_ge(S_['out'], 16)
    if rank < 4:
        ctx.close()
        return nc
    with ExitStack() as sc4:
        t4 = lambda *a: sc4.enter_context(nc.sbuf_tensor(*a))
        xblk4s = t4("xblk4", [128, NDC * TB], F32)
        xblk4 = [xblk4s, xblk4s]
        xbf4 = t4("xbf4", [128, NDC * TB], BF16)
        xsq4 = t4("xsq4", [128, NDC * TB], BF16)
        rowA4 = t4("rowA4", [1, 6 * TB], F32)
        rowB4 = t4("rowB4", [1, 4 * TB], F32)
        rowC4 = t4("rowC4", [1, 4 * TB], BF16)
        HF = t4("HF", [128, NDC * QOWN], BF16)
        mid = t4("mid", [128, NFC * 512], BF16)
        Wub = [t4("Wub%d" % i, [128, NDC * 128], BF16) for i in range(4)]
        Wdb = [t4("Wdb%d" % i, [128, NFC * 128], BF16) for i in range(3)]
        WD3 = [3, 3, 2]
        a3s = t4("a3s", [128, 512], F32)
        gtile = t4("gtile", [128, 512], F32)
        gsil = t4("gsil", [128, 512], BF16)
        xfin = [t4("xfin%d" % i, [128, 512], F32) for i in range(2)]
        a3hist = None
        yout = [t4("yout", [128, 512], F32)]
        yout = [yout[0], yout[0]]
        hsc4 = t4("hsc4", [128, TB], BF16)
        env4 = dict(xblk=xblk4, xbf=xbf4, xsq=xsq4, rowA=rowA4, rowB=rowB4, rowC=rowC4,
                    hdst=HF, hT=None, hsc=hsc4, xd4base=GP4)

        NTB4 = 4

        def wo_count(tb, item, i):
            base = tb * (NFC + 5 * NDC)
            if item == 'wub':
                return 16 * (base + i + 1)
            if item == 'wdb':
                return 16 * (base + NFC + 5 * i + 4)
            return 16 * (base + NFC + 5 * i + 5)

        @block.sync
        def _(sy):
            sy.wait_ge(S_['p3e'], 32)
            for g in range(GP4, NG):
                off = (g - GP4) * TB
                for cc in range(NDC):
                    if g >= GP4 + 1:
                        sy.wait_ge(S_['hh'], 8 * (g - 1) + cc + 1)
                    sy.dma_start(xblk4[g % 2][:, cc * TB:(cc + 1) * TB],
                                 xTo[cc * 128:(cc + 1) * 128, off:off + TB]
                                 ).then_inc(S_['xd4'], 16)
            if not p4_full:
                sy.wait_ge(S_['hh'], 8 * NG)
                sy.dma_start(dbg4, ST[:, :]).then_inc(S_['out'], 16)
                sy.wait_ge(S_['out'], 16)
            for tb in range(NTB4 if p4_full else 0):
                if tb == 1 and rank == 5:
                    sy.wait_ge(S_['fa4'], 32)
                    sy.dma_start(dbgM, mid[:, :]).then_inc(S_['dmp'], 16)
                    sy.wait_ge(S_['dmp'], 16)
                for fc in range(NFC):
                    if fc >= 4:
                        sy.wait_ge(S_['f4'], tb * F4_PER + fc - 3)
                    elif tb >= 1:
                        sy.wait_ge(S_['f4'], (tb - 1) * F4_PER + 29 + fc)
                    if tb == 0 and fc == 0:
                        sy.wait_ge(S_['to'], 16 * N_TERN_OUT)
                    sy.dma_start(Wub[fc % 4][:, :].rearrange("p (c e) -> p c e", c=NDC),
                                 WUPt_d[fc::NFC, :].rearrange("c (p e) -> p c e", p=128)
                                 ).then_inc(S_['wu%d' % (fc % 4)], 16)
                for dc in range(NDC):
                    if dc >= 3:
                        sy.wait_ge(S_['f4'], tb * F4_PER + 31 + dc)
                    elif tb >= 1:
                        sy.wait_ge(S_['f4'], (tb - 1) * F4_PER + 39 + dc)
                    for k4 in range(4):
                        sy.dma_start(Wdb[dc % 3][:, k4 * 1024:(k4 + 1) * 1024]
                                     .rearrange("p (c e) -> p c e", c=8),
                                     WDNt_d[dc::NDC, :][k4 * 8:(k4 + 1) * 8, :]
                                     .rearrange("c (p e) -> p c e", p=128)
                                     ).then_inc(S_['wd%d' % (dc % 3)], 16)
                    if tb * NDC + dc - 2 >= 0:
                        sy.wait_ge(S_['fy'], tb * NDC + dc - 1)
                    sy.dma_start(xfin[dc % 2][:, :],
                                 xTo[dc * 128:(dc + 1) * 128, tb * 512:(tb + 1) * 512]
                                 ).then_inc(S_['wx'], 16)
            if p4_full:
                sy.wait_ge(S_['out'], 16 * NTB4 * NDC)
            if rank == 5:
                sy.dma_start(dbg1[:, 0:1024].rearrange("p (a q) -> p a q", a=8),
                             WUPt_d[0:8, :].rearrange("a (p q) -> p a q", p=128)
                             ).then_inc(S_['out'], 16)
                sy.dma_start(dbg2[:, 0:1024].rearrange("p (a q) -> p a q", a=8),
                             WDNt_d[0:8, :].rearrange("a (p q) -> p a q", p=128)
                             ).then_inc(S_['out'], 16)
                sy.dma_start(dbg4, ST[:, :]).then_inc(S_['out'], 16)
                sy.dma_start(dbg5[:, 0:512], a3s[:, :]).then_inc(S_['out'], 16)

                sy.dma_start(dbg5[0:1, 1024:1536], rowA4[0:1, 512:1024],
                             allow_slow_non_contiguous=True).then_inc(S_['out'], 16)
                sy.dma_start(dbg5[0:1, 1536:2048], rowA4[0:1, 1024:1536],
                             allow_slow_non_contiguous=True).then_inc(S_['out'], 16)
                sy.dma_start(dbg5[:, 512:1024], gtile[:, :]).then_inc(S_['out'], 16)
                sy.dma_start(dbg3[:, 1024:1536], mid[:, 0:512]).then_inc(S_['out'], 16)
                sy.dma_start(dbgH, HF[:, :]).then_inc(S_['out'], 16)
                sy.wait_ge(S_['out'], 16 * (NTB4 * NDC + 9))

        def mid_sq_view(fc):
            return xsq4[:, (fc % NDC) * 512:(fc % NDC) * 512 + 512]

        def _summ(t, tb, fc):
            t.wait_ge(S_['fe4'], tb * FE4_PER + fc + 1)
            t.matmul(PS[3][0:1, 0:512], ones_b[:, :], mid_sq_view(fc),
                     start=(fc == 0), stop=(fc == NFC - 1),
                     skip_group_check=True).then_inc(S_['fs'], 1)

        @block.gpsimd
        def _(gp_):
            for tb in range(NTB4 if p4_full else 0):
                for dc in range(NDC):
                    gp_.wait_ge(S_['fy'], tb * NDC + dc + 1)
                    gp_.dma_start(yT[dc * 128:(dc + 1) * 128, tb * 512:(tb + 1) * 512],
                                  yout[0][:, :]).then_inc(S_['out'], 16)

        @block.tensor
        def _(t):
            for g in range(GP4, NG):
                if g > GP4:
                    stats_pe_work(t, env4, g - 1, 'p4')
                stats_pe_stats(t, env4, g)
            stats_pe_work(t, env4, NG - 1, 'p4')
            for tb in range(NTB4 if p4_full else 0):
                for fc in range(NFC):
                    t.wait_ge(S_['wu%d' % (fc % 4)], 16 * (8 * tb + fc // 4 + 1))
                    if tb == 0 and fc == 0:
                        t.wait_ge(S_['hh'], 8 * NG)
                        t.wait_ge(S_['sa'], 3 * (NG - 1) + 1)
                    if fc >= 3:
                        t.wait_ge(S_['fa4'], tb * FA4_PER + fc - 2)
                    elif tb >= 1:
                        t.wait_ge(S_['fe4'], tb * FE4_PER)
                    mm = None
                    for dc in range(NDC):
                        mm = t.matmul(PS[fc % 3][:, 0:512],
                                      Wub[fc % 4][:, dc * 128:(dc + 1) * 128],
                                      HF[:, dc * QOWN + tb * 512: dc * QOWN + tb * 512 + 512],
                                      start=(dc == 0), stop=(dc == NDC - 1),
                                      skip_group_check=True)
                    mm.then_inc(S_['f4'], 1)
                    if fc >= 2:
                        _summ(t, tb, fc - 2)
                _summ(t, tb, NFC - 2)
                _summ(t, tb, NFC - 1)
                t.wait_ge(S_['fe4'], tb * FE4_PER + 33)
                t.matmul(PS[3][:, 512:1024], onesr_b[:, :], rowC4[0:1, 0:512],
                         start=True, stop=False, skip_group_check=True)
                t.matmul(PS[3][:, 512:1024], onesr_b[:, :], rowC4[0:1, 512:1024],
                         start=False, stop=True, skip_group_check=True).then_inc(S_['f4'], 1)
                for dc in range(NDC):
                    t.wait_ge(S_['fa4'], tb * FA4_PER + 32)
                    if dc == 0:
                        t.wait_ge(S_['fe4'], tb * FE4_PER + 32)
                    t.wait_ge(S_['wd%d' % (dc % 3)],
                              16 * 4 * (WD3[dc % 3] * tb + dc // 3 + 1))
                    if dc >= 2:
                        t.wait_ge(S_['fe4'], tb * FE4_PER + 35 + dc - 2)
                    mm = None
                    for fc in range(NFC):
                        mm = t.matmul(PS[dc % 2][:, 0:512],
                                      Wdb[dc % 3][:, fc * 128:(fc + 1) * 128],
                                      mid[:, fc * 512:(fc + 1) * 512],
                                      start=(fc == 0), stop=(fc == NFC - 1),
                                      skip_group_check=True)
                    mm.then_inc(S_['f4'], 1)

        @block.scalar
        def _(sc):
            for g in range(GP4, NG):
                stats_act_rows1(sc, env4, g)
                stats_act_rows2(sc, env4, g, 'p4')
                stats_act_rows3(sc, env4, g, 'p4')
            for tb in range(NTB4 if p4_full else 0):
                for fc in range(NFC):
                    sc.wait_ge(S_['f4'], tb * F4_PER + fc + 1)
                    sc.activation(mid[:, fc * 512:(fc + 1) * 512], PS[fc % 3][:, 0:512],
                                  AF.Gelu).then_inc(S_['fa4'], 1)
                sc.wait_ge(S_['fs'], (tb + 1) * FS_PER)
                sc.activation(rowA4[0:1, 0:512], PS[3][0:1, 0:512], AF.Ln, scale=1.0 / FF)
                sc.activation(rowA4[0:1, 1024:1536], rowA4[0:1, 0:512], AF.Exp, scale=0.5)
                sc.activation(rowA4[0:1, 1536:2048], rowA4[0:1, 1024:1536], AF.Identity,
                              bias=eps_t[0:1, 0:1])
                sc.activation(rowA4[0:1, 0:512], rowA4[0:1, 1536:2048], AF.Ln)
                sc.activation(rowA4[0:1, 1024:1536], rowA4[0:1, 0:512], AF.Exp,
                              scale=-1.0)
                padA(sc, 'fa4')
                for dc in range(NDC):
                    sc.wait_ge(S_['fe4'], tb * FE4_PER + 35 + dc)
                    sc.activation(gsil[:, :], gtile[:, :], AF.Silu)
                    padA(sc, 'fa4')

        @block.vector
        def _(v):
            stats_dve_pre(v, env4, GP4)
            for g in range(GP4, NG):
                stats_dve_rows1(v, env4, g)
                stats_dve_rows2(v, env4, g)
                stats_dve_rows3(v, env4, g, 'p4')
                stats_dve_rows4(v, env4, g)
                stats_dve_h(v, env4, g, 'p4')
                if g + 1 < NG:
                    stats_dve_pre(v, env4, g + 1)
            for tb in range(NTB4 if p4_full else 0):
                for fc in range(NFC):
                    v.wait_ge(S_['fa4'], tb * FA4_PER + fc + 1)
                    if fc >= NDC:
                        v.wait_ge(S_['fs'], tb * FS_PER + fc - NDC + 1)
                    v.tensor_tensor(mid_sq_view(fc), mid[:, fc * 512:(fc + 1) * 512],
                                    mid[:, fc * 512:(fc + 1) * 512],
                                    ALU.mult).then_inc(S_['fe4'], 1)
                v.wait_ge(S_['fa4'], tb * FA4_PER + 33)
                v.tensor_copy(rowC4[0:1, 0:512], rowA4[0:1, 1024:1536])
                v.tensor_tensor(rowC4[0:1, 512:1024], rowA4[0:1, 1024:1536], rowC4[0:1, 0:512],
                                ALU.subtract)
                padV(v, 'fe4')
                v.wait_ge(S_['f4'], tb * F4_PER + 33)

                v.tensor_copy(a3s[:, :], PS[3][:, 512:1024]).then_inc(S_['fe4'], 1)
                for dc in range(NDC):
                    v.wait_ge(S_['f4'], tb * F4_PER + 33 + dc + 1)
                    v.tensor_tensor(gtile[:, :], PS[dc % 2][:, 0:512], a3s[:, :],
                                    ALU.mult)
                    padV(v, 'fe4')
                    v.wait_ge(S_['fa4'], tb * FA4_PER + 34 + dc)
                    v.wait_ge(S_['wx'], 16 * (tb * NDC + dc + 1))
                    yi = tb * NDC + dc
                    if yi >= 1:
                        v.wait_ge(S_['out'], 16 * yi)
                    v.tensor_tensor(gsil[:, :],
                                    ST[:, dc * QOWN + tb * 512: dc * QOWN + (tb + 1) * 512],
                                    gsil[:, :], ALU.mult)
                    v.tensor_tensor(yout[0][:, :], xfin[dc % 2][:, :], gsil[:, :],
                                    ALU.add)
                    padV(v, 'fy')
    ctx.close()
    return nc


def kernel(L0_attn_x, L0_attn_norm_scale, L0_attn_wq, L0_attn_wk, L0_attn_wv,
           L0_attn_w_out, L0_attn_b_out, L0_ffn_w_up, L0_ffn_w_down):
    x = np.asarray(L0_attn_x, dtype=np.float32)
    wq = np.ascontiguousarray(np.asarray(L0_attn_wq, np.float32).reshape(D, D))
    wk = np.ascontiguousarray(np.asarray(L0_attn_wk, np.float32).reshape(D, D))
    wv = np.ascontiguousarray(np.asarray(L0_attn_wv, np.float32).reshape(D, D))
    wo = np.ascontiguousarray(np.asarray(L0_attn_w_out, np.float32).reshape(D, D))
    nsrow = np.ascontiguousarray(np.asarray(L0_attn_norm_scale, np.float32).reshape(1, D))
    boutc = np.ascontiguousarray(np.asarray(L0_attn_b_out, np.float32).reshape(NDC, 128).T)
    wupT = np.ascontiguousarray(np.asarray(L0_ffn_w_up, np.float32).T)
    wdnT = np.ascontiguousarray(np.asarray(L0_ffn_w_down, np.float32).T)
    xTb = [np.ascontiguousarray(x[b].T) for b in range(2)]

    nc = build_program()
    in_maps = []
    for c in range(8):
        b, qi = c // 4, c % 4
        xTr = np.ascontiguousarray(np.roll(xTb[b], -qi * QOWN, axis=1))
        xTo = np.ascontiguousarray(x[b, qi * QOWN:(qi + 1) * QOWN, :].T)
        in_maps.append({"xT": xTr, "xTo": xTo, "wq": wq, "wk": wk, "wv": wv, "wo": wo,
                        "nsr": nsrow, "boc": boutc, "wupT": wupT, "wdnT": wdnT})
    res = run_bass_kernel_spmd(nc, in_maps, core_ids=list(range(8)))
    y = np.empty((2, S, D), dtype=np.float32)
    for c in range(8):
        b, qi = c // 4, c % 4
        y[b, qi * QOWN:(qi + 1) * QOWN, :] = res.results[c]["yT"].T
    return y


# version 23
# speedup vs baseline: 1.2194x; 1.0014x over previous
import numpy as np
from contextlib import ExitStack

import concourse.bass as bass
import concourse.mybir as mybir
from concourse.bass_utils import run_bass_kernel_spmd

F32 = mybir.dt.float32
BF16 = mybir.dt.bfloat16
AF = mybir.ActivationFunctionType
ALU = mybir.AluOpType

D, S, QOWN = 1024, 8192, 2048
NDC, NPAIR, NCH, NCP, NQT, QT = 8, 8, 64, 32, 4, 512
FF, NFC, TB = 4096, 32, 512
NB1 = S // TB
NB2 = QOWN // TB
GP1B, GP4 = NB1, NB1
NG = GP4 + NB2
NGA = GP4
RT2 = 0.35355339059327373
INV_D = 1.0 / D
TAUF = 0.5 / (D * FF)

TERN_SCHED = {
    0: [('u1', k) for k in range(16)],
    1: [('u1', k) for k in range(16, 32)],
    2: [('d1', k) for k in range(16)],
    3: [('d1', k) for k in range(16, 32)],
    4: [('u2', k) for k in range(16)],
    5: [('u2', k) for k in range(16, 32)],
    6: [('d2', k) for k in range(16)],
    7: [('d2', k) for k in range(16, 32)],
}
TERN_UNITS = [u for p in range(8) for u in TERN_SCHED[p]]


def tern_out_dmas(kind):
    return 1 if kind in ('u2', 'd2') else 0


TERN_OUT_BEFORE = [0]
for _u in TERN_UNITS:
    TERN_OUT_BEFORE.append(TERN_OUT_BEFORE[-1] + tern_out_dmas(_u[0]))
N_TERN_OUT = TERN_OUT_BEFORE[-1]

F4_PER = 41
FA4_PER = 41
FE4_PER = 42
FS_PER = 32


def blk_kind(g):
    return 'p1' if g < GP1B else 'p4'


def n_units(g):
    if blk_kind(g) == 'p4':
        return 0
    return 24 if g < NB2 else 16


UB = [0] * (NG + 1)
KO = [0] * (NG + 1)
VO = [0] * (NG + 1)
for g in range(NG):
    UB[g + 1] = UB[g] + n_units(g)
    KO[g + 1] = KO[g] + (0 if blk_kind(g) == 'p4' else (16 if g < NB2 else 8))
    VO[g + 1] = VO[g] + (8 if blk_kind(g) == 'p1' else 0)


def tc128(i):
    return (i // 2) * 128


def build_program(stop='full'):
    rank = {'p1': 1, 'p2': 2, 'p3': 3, 'full': 4, 'dbg': 5, 'p4s': 6}[stop]
    p4_full = rank != 6
    nc = bass.Bass("TRN2", target_bir_lowering=False)

    xT = nc.dram_tensor("xT", [D, S], F32, kind="ExternalInput").ap()
    xTo = nc.dram_tensor("xTo", [D, QOWN], F32, kind="ExternalInput").ap()
    wq_d = nc.dram_tensor("wq", [D, D], F32, kind="ExternalInput").ap()
    wk_d = nc.dram_tensor("wk", [D, D], F32, kind="ExternalInput").ap()
    wv_d = nc.dram_tensor("wv", [D, D], F32, kind="ExternalInput").ap()
    wo_d = nc.dram_tensor("wo", [D, D], F32, kind="ExternalInput").ap()
    nsr = nc.dram_tensor("nsr", [1, D], F32, kind="ExternalInput").ap()
    boc = nc.dram_tensor("boc", [128, NDC], F32, kind="ExternalInput").ap()
    wupT_d = nc.dram_tensor("wupT", [D, FF], F32, kind="ExternalInput").ap()
    wdnT_d = nc.dram_tensor("wdnT", [FF, D], F32, kind="ExternalInput").ap()
    yT = nc.dram_tensor("yT", [D, QOWN], F32, kind="ExternalOutput").ap()
    dbg1 = nc.dram_tensor("dbg1", [128, S], BF16, kind="ExternalOutput").ap()
    dbg2 = nc.dram_tensor("dbg2", [128, S], BF16, kind="ExternalOutput").ap()
    dbg3 = nc.dram_tensor("dbg3", [128, QOWN], BF16, kind="ExternalOutput").ap()
    dbg4 = nc.dram_tensor("dbg4", [128, NPAIR * QOWN], BF16, kind="ExternalOutput").ap()
    dbg5 = nc.dram_tensor("dbg5", [128, 6144], F32, kind="ExternalOutput").ap()
    dbgH = nc.dram_tensor("dbgH", [128, 16384], BF16, kind="ExternalOutput").ap()
    dbgM = nc.dram_tensor("dbgM", [128, 16384], BF16, kind="ExternalOutput").ap()

    KT_d = nc.dram_tensor("KT_d", [D, S], BF16).ap()
    V_d = nc.dram_tensor("V_d", [S, D], BF16).ap()
    QT_d = nc.dram_tensor("QT_d", [D, QOWN], BF16).ap()
    WUPt_d = nc.dram_tensor("WUPt_d", [NDC * NFC, 128 * 128], BF16).ap()
    WDNt_d = nc.dram_tensor("WDNt_d", [NFC * NDC, 128 * 128], BF16).ap()
    recs_d = nc.dram_tensor("recs_d", [2, 8 * 2048], F32).ap()

    ctx = ExitStack()
    ten = lambda *a: ctx.enter_context(nc.sbuf_tensor(*a))
    psu = lambda *a: ctx.enter_context(nc.psum_tensor(*a))
    S_ = {}
    for n in ("ld wl woc xd xq stm sa sb ab pbc hh kv ev eo vo p2 ksq kn knr cb tau sc ex av tl "
              "bc tw tv to wo p3 p3e f4 fs fa4 fe4 fy out dmp wd wx rc rl nn p3n p2b wu0 wu1 wu2 wu3 wd0 wd1 wd2 xd4 tw1 tw2 tw3").split():
        S_[n] = ctx.enter_context(nc.semaphore("s_" + n))

    ones_b = ten("ones_b", [128, 1], BF16)
    onesr_b = ten("onesr_b", [1, 128], BF16)
    onesr_f = ten("onesr_f", [1, 128], F32)
    ones_f = ten("ones_f", [128, 1], F32)
    sel = ten("sel", [128, 128], F32)
    selk = ten("selk", [128, 64], BF16)
    nsrow_b = ten("nsrow_b", [1, D], BF16)
    nscol = ten("nscol", [128, NDC], F32)
    nsrow_f = ten("nsrow_f", [1, D], F32)
    bout_c = ten("bout_c", [128, NDC], F32)
    eps_ln = ten("eps_ln", [1, 1], F32)
    scr1 = ten("scr1", [1, 1], F32)
    scrV = ten("scrV", [1, 1], F32)
    eps_rms = ten("eps_rms", [1, 1], F32)
    eps_t = ten("eps_t", [1, 1], F32)
    eps_f = ten("eps_f", [1, 1], F32)
    ST = ten("ST", [128, NDC * QOWN], BF16)

    PS = [psu("ps%d" % i, [128, 1024], F32) for i in range(4)]
    pA = [PS[0][:, 0:512], PS[1][:, 0:512]]
    pB = [PS[0][:, 512:1024], PS[1][:, 512:1024]]
    pKV = [PS[2][:, 0:512], PS[2][:, 512:1024]]
    pS1 = PS[3][0:1, 0:512]
    pS2 = PS[3][0:1, 512:1024]
    p_nv = PS[3][:, 0:512]
    p_sum = PS[3][:, 512:1024]

    block = ctx.enter_context(nc.Block())

    def padA(sc, sem, n=1):
        sc.activation(scr1[0:1, 0:1], scr1[0:1, 0:1], AF.Identity).then_inc(S_[sem], n)

    def padV(v, sem, n=1):
        v.tensor_copy(scrV[0:1, 0:1], scrV[0:1, 0:1]).then_inc(S_[sem], n)

    def RS(env, w, i):
        return env[w][0:1, i * TB:(i + 1) * TB]

    def stats_pe_stats(t, env, g):
        t.wait_ge(S_['xq'], g + 1)
        if g >= 1:
            t.wait_ge(S_['sa'], 3 * (g - 1) + 1)
        for c in range(NDC):
            t.matmul(pS1, ones_b[:, :], env['xbf'][:, c * TB:(c + 1) * TB],
                     start=(c == 0), stop=(c == NDC - 1), skip_group_check=True)
        mm = None
        for c in range(NDC):
            mm = t.matmul(pS2, ones_b[:, :], env['xsq'][:, c * TB:(c + 1) * TB],
                          start=(c == 0), stop=(c == NDC - 1), skip_group_check=True)
        mm.then_inc(S_['stm'], 1)

    def stats_dve_pre(v, env, g):
        if 'xd4base' in env:
            v.wait_ge(S_['xd4'], 128 * (g - env['xd4base'] + 1))
        else:
            v.wait_ge(S_['xd'], 16 * (g + 1))
        v.tensor_copy(env['xbf'][:, :], env['xblk'][g % 2][:, :])
        v.tensor_tensor(env['xsq'][:, :], env['xbf'][:, :], env['xbf'][:, :],
                        ALU.mult).then_inc(S_['xq'], 1)

    def stats_dve_rows1(v, env, g, n=1):
        v.wait_ge(S_['stm'], g + 1)
        v.tensor_single_scalar(RS(env, 'rowA', 0), pS1, INV_D, ALU.mult)
        v.tensor_single_scalar(RS(env, 'rowA', 5), pS2, INV_D, ALU.mult)
        v.tensor_tensor(RS(env, 'rowB', 0), RS(env, 'rowA', 0), RS(env, 'rowA', 0), ALU.mult)
        v.tensor_tensor(RS(env, 'rowA', 1), RS(env, 'rowA', 5), RS(env, 'rowB', 0),
                        ALU.subtract)
        padV(v, 'sa', n)

    def stats_act_fused(sc, env, g):
        sc.wait_ge(S_['sa'], 3 * g + 3)
        sc.activation(RS(env, 'rowA', 2), RS(env, 'rowA', 1), AF.Ln,
                      scale=1.0 + 1e-6, bias=eps_f[0:1, 0:1])
        sc.activation(RS(env, 'rowA', 4), RS(env, 'rowA', 2), AF.Exp, scale=-0.5)
        padA(sc, 'sb', 3)

    def stats_act_rows1(sc, env, g):
        sc.wait_ge(S_['sa'], 3 * g + 1)
        sc.activation(RS(env, 'rowA', 2), RS(env, 'rowA', 1), AF.Ln, bias=eps_ln[0:1, 0:1])
        sc.activation(RS(env, 'rowB', 1), RS(env, 'rowA', 2), AF.Exp, scale=-1.0)
        padA(sc, 'sb')

    def stats_dve_rows2(v, env, g):
        v.wait_ge(S_['sb'], 3 * g + 1)
        v.tensor_tensor(RS(env, 'rowA', 3), RS(env, 'rowA', 1), RS(env, 'rowB', 1),
                        ALU.mult)
        padV(v, 'sa')

    def stats_act_rows2(sc, env, g, kind):
        sc.wait_ge(S_['sa'], 3 * g + 2)
        if kind != 'p4':
            sc.activation(RS(env, 'rowB', 0), RS(env, 'rowA', 3), AF.Ln,
                          bias=eps_rms[0:1, 0:1])
            padA(sc, 'sb')
        else:
            sc.activation(RS(env, 'rowB', 0), RS(env, 'rowA', 3), AF.Ln)
            sc.activation(RS(env, 'rowB', 1), RS(env, 'rowB', 0), AF.Exp, scale=0.5)
            sc.activation(RS(env, 'rowB', 3), RS(env, 'rowA', 2), AF.Exp, scale=0.5)
            padA(sc, 'sb')

    def stats_dve_rows3(v, env, g, kind):
        v.wait_ge(S_['sb'], 3 * g + 2)
        if kind != 'p4':
            v.tensor_tensor(RS(env, 'rowB', 0), RS(env, 'rowA', 2), RS(env, 'rowB', 0),
                            ALU.add)
            padV(v, 'sa')
        else:
            v.tensor_single_scalar(RS(env, 'rowB', 0), RS(env, 'rowB', 1), 1e-8, ALU.add)
            v.tensor_tensor(RS(env, 'rowB', 0), RS(env, 'rowB', 3), RS(env, 'rowB', 0),
                            ALU.mult)
            padV(v, 'sa')

    def stats_act_rows3(sc, env, g, kind):
        sc.wait_ge(S_['sa'], 3 * g + 3)
        if kind != 'p4':
            sc.activation(RS(env, 'rowA', 4), RS(env, 'rowB', 0), AF.Exp, scale=-0.5)
            padA(sc, 'sb')
        else:
            sc.activation(RS(env, 'rowB', 3), RS(env, 'rowB', 0), AF.Ln)
            sc.activation(RS(env, 'rowA', 4), RS(env, 'rowB', 3), AF.Exp, scale=-1.0)
            padA(sc, 'sb')

    def stats_dve_rows4(v, env, g):
        v.wait_ge(S_['sb'], 3 * g + 3)
        v.tensor_tensor(RS(env, 'rowB', 2), RS(env, 'rowA', 0), RS(env, 'rowA', 4), ALU.mult)
        v.tensor_copy(RS(env, 'rowC', 0), RS(env, 'rowA', 4))
        v.tensor_tensor(RS(env, 'rowC', 1), RS(env, 'rowA', 4), RS(env, 'rowC', 0), ALU.subtract)
        v.tensor_scalar(RS(env, 'rowC', 2), RS(env, 'rowB', 2), -1.0, None, ALU.mult)
        v.scalar_tensor_tensor(RS(env, 'rowC', 3), RS(env, 'rowB', 2), -1.0,
                               RS(env, 'rowC', 2), ALU.mult, ALU.subtract)
        padV(v, 'ab')

    def stats_pe_work_bcast(t, env, g):
        t.wait_ge(S_['ab'], g + 1)
        if g >= 2:
            t.wait_ge(S_['hh'], 8 * (g - 1))
        ls = onesr_b[0:1, 0:128]
        t.matmul(pA[g % 2], ls, RS(env, 'rowC', 0), start=True, stop=False,
                 skip_group_check=True)
        t.matmul(pA[g % 2], ls, RS(env, 'rowC', 1), start=False, stop=True,
                 skip_group_check=True)
        t.matmul(pB[g % 2], ls, RS(env, 'rowC', 2), start=True, stop=False,
                 skip_group_check=True)
        t.matmul(pB[g % 2], ls, RS(env, 'rowC', 3), start=False, stop=True,
                 skip_group_check=True).then_inc(S_['pbc'], 1)

    def stats_pe_work(t, env, g, kind, wkt=None, wvt=None, wqt=None, split=False):
        if not split:
            stats_pe_work_bcast(t, env, g)
        nu = n_units(g)
        if nu == 0:
            return
        for u in range(nu):
            if u == 1:
                t.wait_ge(S_['hh'], 8 * (g + 1))
            if UB[g] + u - 1 >= 1:
                t.wait_ge(S_['ev'], UB[g] + u - 1)
            mm = None
            if u < 8:
                for dc in range(NDC):
                    if u == 0:
                        t.wait_ge(S_['hh'], min(8 * g + dc + 5, 8 * (g + 1)))
                    mm = t.matmul(pKV[u % 2], wkt[:, dc * D + u * 128: dc * D + (u + 1) * 128],
                                  env['hT'][g % 2][:, dc * TB:(dc + 1) * TB],
                                  start=(dc == 0), stop=(dc == NDC - 1), skip_group_check=True)
            elif u < 16:
                i = u - 8
                tc, half = i // 2, i % 2
                for dc in range(NDC):
                    mm = t.matmul(pKV[u % 2],
                                  env['hT'][g % 2][:, dc * TB + tc * 128: dc * TB + (tc + 1) * 128],
                                  wvt[:, dc * D + half * 512: dc * D + (half + 1) * 512],
                                  start=(dc == 0), stop=(dc == NDC - 1), skip_group_check=True)
            else:
                uq = u - 16
                for dc in range(NDC):
                    mm = t.matmul(pKV[u % 2], wqt[:, dc * D + uq * 128: dc * D + (uq + 1) * 128],
                                  env['hT'][g % 2][:, dc * TB:(dc + 1) * TB],
                                  start=(dc == 0), stop=(dc == NDC - 1), skip_group_check=True)
            mm.then_inc(S_['kv'], 1)

    def stats_dve_h(v, env, g, kind):
        for c in range(NDC):
            if c == 0:
                v.wait_ge(S_['pbc'], g + 1)
            if kind == 'p4':
                v.tensor_tensor(env['hsc'][:, :],
                                env['xblk'][g % 2][:, c * TB:(c + 1) * TB], pA[g % 2],
                                ALU.mult)
                boff = (g - GP4) * TB
                dst = env['hdst'][:, c * QOWN + boff: c * QOWN + boff + TB]
                v.tensor_tensor(dst, env['hsc'][:, :], pB[g % 2],
                                ALU.add).then_inc(S_['hh'], 1)
            else:
                v.scalar_tensor_tensor(env['hsc'][:, :],
                                       env['xblk'][g % 2][:, c * TB:(c + 1) * TB],
                                       nscol[:, c:c + 1], pA[g % 2], ALU.mult, ALU.mult)
                dst = env['hT'][g % 2][:, c * TB:(c + 1) * TB]
                v.scalar_tensor_tensor(dst, pB[g % 2], nscol[:, c:c + 1],
                                       env['hsc'][:, :],
                                       ALU.mult, ALU.add).then_inc(S_['hh'], 1)

    def stats_act_evac(sc, env, g, evK=None, evV=None, hooks=None):
        for u in range(n_units(g)):
            if hooks and u in hooks:
                hooks[u]()
            sc.wait_ge(S_['kv'], UB[g] + u + 1)
            if 8 <= u < 16:
                i = u - 8
                if VO[g] + i - 4 >= 0:
                    sc.wait_ge(S_['vo'], 16 * (VO[g] + i - 3))
                sc.activation(evV[i % 4][:, :], pKV[u % 2],
                              AF.Identity).then_inc(S_['ev'], 1)
            else:
                ko = u if u < 8 else 8 + (u - 16)
                if KO[g] + ko - 4 >= 0:
                    sc.wait_ge(S_['eo'], 16 * (KO[g] + ko - 3))
                sc.activation(evK[ko % 4][:, :], pKV[u % 2], AF.Identity,
                              scale=RT2).then_inc(S_['ev'], 1)

    with ExitStack() as sc1:
        t1 = lambda *a: sc1.enter_context(nc.sbuf_tensor(*a))
        xblk = [t1("xblk%d" % i, [128, NDC * TB], F32) for i in range(2)]
        xbf = t1("xbf", [128, NDC * TB], BF16)
        xsq = t1("xsq", [128, NDC * TB], BF16)
        hT = [t1("hT%d" % i, [128, NDC * TB], BF16) for i in range(2)]
        evK = [t1("evK%d" % i, [128, TB], BF16) for i in range(4)]
        evV = [t1("evV%d" % i, [128, TB], BF16) for i in range(4)]
        wqt = t1("wqt", [128, NDC * D], BF16)
        wkt = t1("wkt", [128, NDC * D], BF16)
        wvt = t1("wvt", [128, NDC * D], BF16)
        rowA = t1("rowA", [1, 6 * TB], F32)
        rowB = t1("rowB", [1, 4 * TB], F32)
        rowC = t1("rowC", [1, 4 * TB], BF16)
        hsc = t1("hsc", [128, TB], BF16)
        env1 = dict(xblk=xblk, xbf=xbf, xsq=xsq, hT=hT, rowA=rowA, rowB=rowB, rowC=rowC,
                    hsc=hsc)

        def blk_src(g):
            return xT, g * TB

        @block.sync
        def _(sy):
            sy.dma_start(nsrow_f[:, :], nsr).then_inc(S_['ld'], 16)
            sy.dma_start(nscol[:, :], nsr.rearrange("a (c p) -> p (a c)", p=128),
                         allow_slow_non_contiguous=True).then_inc(S_['ld'], 16)
            sy.dma_start(bout_c[:, :], boc).then_inc(S_['ld'], 16)
            for g in range(NGA):
                src, off = blk_src(g)
                if g >= 2:
                    sy.wait_ge(S_['hh'], 8 * (g - 1))
                sy.dma_start(xblk[g % 2][:, :].rearrange("p (c t) -> p c t", c=NDC),
                             src[:, off:off + TB].rearrange("(c p) t -> p c t", c=NDC)
                             ).then_inc(S_['xd'], 16)
            if rank == 1:
                sy.wait_ge(S_['eo'], 16 * KO[NGA])
                sy.wait_ge(S_['vo'], 16 * VO[NGA])
                sy.dma_start(dbg1, KT_d[0:128, :]).then_inc(S_['out'], 16)
                sy.dma_start(dbg2.rearrange("p (a d) -> p a d", d=D),
                             V_d[0:8 * 128, :].rearrange("(a p) d -> p a d", p=128)
                             ).then_inc(S_['out'], 16)
                sy.dma_start(dbg3, QT_d[0:128, :]).then_inc(S_['out'], 16)
                sy.wait_ge(S_['out'], 48)

        @block.gpsimd
        def _(gp_):
            gp_.memset(ones_b[:, :], 1.0)
            gp_.memset(onesr_b[:, :], 1.0)
            gp_.memset(onesr_f[:, :], 1.0)
            gp_.memset(ones_f[:, :], 1.0)
            gp_.memset(sel[:, :], 0.0)
            gp_.memset(sel[0:1, 0:64], 1.0)
            gp_.memset(selk[:, :], 0.0)
            gp_.memset(selk[:, 0:1], 1.0)
            gp_.memset(sel[64:65, 64:128], 1.0)
            gp_.memset(eps_ln[:, :], 1e-5)
            gp_.memset(eps_rms[:, :], 1e-6)
            gp_.memset(eps_t[:, :], 1e-8)
            gp_.memset(eps_f[:, :], 1e-11)
            gp_.memset(scr1[:, :], 0.0)
            gp_.memset(scrV[:, :], 0.0).then_inc(S_['ld'], 1)
            gp_.dma_start(wqt[:, :].rearrange("p (c e) -> p c e", c=NDC), wq_d.rearrange("(c p) e -> p c e", c=NDC)
                          ).then_inc(S_['wl'], 16)
            gp_.dma_start(wkt[:, :].rearrange("p (c e) -> p c e", c=NDC), wk_d.rearrange("(c p) e -> p c e", c=NDC)
                          ).then_inc(S_['wl'], 16)
            gp_.dma_start(wvt[:, :].rearrange("p (c e) -> p c e", c=NDC), wv_d.rearrange("(c p) e -> p c e", c=NDC)
                          ).then_inc(S_['wl'], 16)
            for g in range(NGA):
                for u in range(8):
                    gp_.wait_ge(S_['ev'], UB[g] + u + 2)
                    gp_.dma_start(KT_d[u * 128:(u + 1) * 128, g * TB:(g + 1) * TB],
                                  evK[u % 4][:, :]).then_inc(S_['eo'], 16)
                for i in range(8):
                    gp_.wait_ge(S_['ev'], UB[g] + 8 + i + 2)
                    gp_.dma_start(V_d[g * TB + tc128(i): g * TB + tc128(i) + 128,
                                      (i % 2) * 512:((i % 2) + 1) * 512],
                                  evV[i % 4][:, :]).then_inc(S_['vo'], 16)
                if n_units(g) > 16:
                    for u in range(16, 24):
                        gp_.wait_ge(S_['ev'], UB[g] + u + 2)
                        gp_.dma_start(QT_d[(u - 16) * 128:(u - 15) * 128,
                                           g * TB:(g + 1) * TB],
                                      evK[(8 + u - 16) % 4][:, :]).then_inc(S_['eo'], 16)

        @block.vector
        def _(v):
            v.wait_ge(S_['ld'], 49)
            v.wait_ge(S_['wl'], 48)
            v.tensor_copy(nsrow_b[:, :], nsrow_f[:, :])
            stats_dve_pre(v, env1, 0)

            def rows_chain(g2):
                kind2 = blk_kind(g2)
                stats_dve_rows1(v, env1, g2)
                stats_dve_rows2(v, env1, g2)
                stats_dve_rows3(v, env1, g2, kind2)
                stats_dve_rows4(v, env1, g2)
            rows_chain(0)
            stats_dve_pre(v, env1, 1)
            for g in range(NGA):
                kind = blk_kind(g)
                stats_dve_h(v, env1, g, kind)
                if g + 1 < NGA:
                    g2 = g + 1
                    stats_dve_rows1(v, env1, g2)
                    stats_dve_rows2(v, env1, g2)
                    stats_dve_rows3(v, env1, g2, blk_kind(g2))
                    stats_dve_rows4(v, env1, g2)
                    if g2 + 1 < NGA:
                        stats_dve_pre(v, env1, g2 + 1)

        @block.scalar
        def _(sc):
            sc.wait_ge(S_['ld'], 1)
            stats_act_rows1(sc, env1, 0)
            stats_act_rows2(sc, env1, 0, blk_kind(0))
            stats_act_rows3(sc, env1, 0, blk_kind(0))
            for g in range(NGA):
                nu = n_units(g)
                if g + 1 < NGA:
                    g2 = g + 1
                    hooks = {
                        5 * nu // 8: (lambda g2=g2: stats_act_rows1(sc, env1, g2)),
                        6 * nu // 8: (lambda g2=g2: stats_act_rows2(sc, env1, g2,
                                                                    blk_kind(g2))),
                        7 * nu // 8: (lambda g2=g2: stats_act_rows3(sc, env1, g2,
                                                                    blk_kind(g2))),
                    }
                else:
                    hooks = None
                stats_act_evac(sc, env1, g, evK=evK, evV=evV, hooks=hooks)
            padA(sc, 'ev')

        @block.tensor
        def _(t):
            t.wait_ge(S_['ld'], 49)
            t.wait_ge(S_['wl'], 48)
            for g in range(NGA):
                if g >= 1:
                    stats_pe_work_bcast(t, env1, g - 1)
                stats_pe_stats(t, env1, g)
                if g >= 1:
                    stats_pe_work(t, env1, g - 1, blk_kind(g - 1), wkt=wkt, wvt=wvt,
                                  wqt=wqt, split=True)
            stats_pe_work(t, env1, NGA - 1, blk_kind(NGA - 1), wkt=wkt, wvt=wvt, wqt=wqt)

    if rank < 2:
        ctx.close()
        return nc
    with ExitStack() as sc2:
        t2 = lambda *a: sc2.enter_context(nc.sbuf_tensor(*a))
        Kb = [t2("Kb%d" % i, [128, S], BF16) for i in range(2)]
        Vb = [t2("Vb%d" % i, [128, S], BF16) for i in range(2)]
        Qb = [t2("Qb%d" % i, [128, QOWN], BF16) for i in range(2)]
        NV = t2("NV", [128, NPAIR * QOWN], BF16)
        ptl = {0: [t2("pA%d" % i, [128, 1024], BF16) for i in range(2)],
               1: [t2("pB%d" % i, [128, 1024], BF16) for i in range(2)]}
        sums = t2("sums", [128, QT], F32)
        sums2 = t2("sums2", [128, QT], F32)
        sumsT = [sums, sums2]
        recs = t2("recs", [128, QT], F32)
        recs2 = t2("recs2", [128, QT], F32)
        recsT = [recs, recs2]
        recsP3 = [t2("recsP3%d" % i, [65, QT], F32) for i in range(2)]
        ksq = t2("ksq", [128, 2048], BF16)
        kqc = t2("kqc", [128, 24], F32)
        biasA = t2("biasA", [128, 1], F32)
        biasB = t2("biasB", [128, 1], F32)
        cneg = t2("cneg", [1, 2], BF16)
        taur = t2("taur", [128, 1], F32)
        tau_up = t2("tau_up", [128, 1], F32)
        tau_dn = t2("tau_dn", [128, 1], F32)
        twk = [t2("twk%d" % i, [128, 1024], F32) for i in range(3)]
        TWS = None
        tabs = t2("tabs", [128, 1024], F32)
        tmf = t2("tmf", [128, 1024], BF16)
        tout = [t2("tout%d" % i, [128, 1024], BF16) for i in range(2)]
        tacc = t2("tacc", [128, 64], F32)
        wot = t2("wot", [128, NDC * D], BF16)

        slots = [PS[0], PS[1], PS[2]]

        def tern_src(kind, k):
            if kind in ('u1', 'u2'):
                dc, q4 = k // 4, k % 4
                return wupT_d[dc * 128:(dc + 1) * 128, q4 * 1024:(q4 + 1) * 1024], 1024
            return wdnT_d[k * 128:(k + 1) * 128, :], 1024

        @block.sync
        def _(sy):
            for j in range(NPAIR):
                if j == 0:
                    sy.wait_ge(S_['eo'], 16 * KO[NGA])
                    sy.wait_ge(S_['vo'], 16 * VO[NGA])
                if j >= 2:
                    sy.wait_ge(S_['av'], NQT * NCP * (j - 1))
                p2s = S_['p2'] if j % 2 == 0 else S_['p2b']
                sy.dma_start(Kb[j % 2][:, :], KT_d[j * 128:(j + 1) * 128, :]
                             ).then_inc(p2s, 16)
                sy.dma_start(Qb[j % 2][:, :], QT_d[j * 128:(j + 1) * 128, :]
                             ).then_inc(p2s, 16)
                for i in range(8):
                    sy.dma_start(Vb[j % 2][:, i * 1024:(i + 1) * 1024]
                                 .rearrange("p (c e) -> p c e", c=8),
                                 V_d[i * 1024:(i + 1) * 1024, j * 128:(j + 1) * 128]
                                 .rearrange("(c p) e -> p c e", p=128)).then_inc(p2s, 16)

        @block.gpsimd
        def _(gp_):
            gp_.wait_ge(S_['eo'], 16 * KO[NGA])
            gp_.wait_ge(S_['vo'], 16 * VO[NGA])
            gp_.wait_ge(S_['ev'], UB[NGA])
            gp_.memset(recs[:, :], 0.0)
            gp_.memset(recs2[:, :], 0.0)
            gp_.memset(recsP3[0][:, :], 0.0)
            gp_.memset(recsP3[1][:, :], 0.0)
            gp_.memset(tacc[:, :], 0.0).then_inc(S_['tv'], 1)
            gp_.dma_start(wot[:, :].rearrange("p (c e) -> p c e", c=NDC), wo_d.rearrange("(c p) e -> p c e", c=NDC)
                          ).then_inc(S_['woc'], 16)
            def tern_out(gp_, i, kind, k):
                gp_.wait_ge(S_['tv'], 1 + i + 1)
                if kind == 'u2':
                    dc, q4 = k // 4, k % 4
                    r0 = dc * NFC + q4 * 8
                    dst = WUPt_d[r0:r0 + 8, :]
                else:
                    r0 = k * NDC
                    dst = WDNt_d[r0:r0 + 8, :]
                gp_.dma_start(dst.rearrange("f (p e) -> p f e", p=128),
                              tout[i % 2][:, 0:1024].rearrange("p (f e) -> p f e", f=8)
                              ).then_inc(S_['to'], 16)

            pend_out = [None]
            TWS = [S_['tw'], S_['tw1'], S_['tw2']]
            for i, (kind, k) in enumerate(TERN_UNITS):
                if i >= 3:
                    gp_.wait_ge(S_['tv'], 1 + i - 2)
                src, w = tern_src(kind, k)
                gp_.dma_start(twk[i % 3][:, 0:w], src).then_inc(TWS[i % 3], 16)
                if pend_out[0] is not None:
                    tern_out(gp_, *pend_out[0])
                    pend_out[0] = None
                if kind in ('u2', 'd2'):
                    pend_out[0] = (i, kind, k)
            tern_out(gp_, *pend_out[0])
            pend_out[0] = None

        def _avblock(t, j, qg, prev, u, first, drain=False, need_cb=False):
            t.wait_ge(S_['ex'], u if drain else u - 2)
            if first:
                t.wait_ge(S_['tl'], 2 * qg)
                if need_cb:
                    t.wait_ge(S_['cb'], j + 1)
            mm = None
            for ci in (0, 1):
                c = 2 * prev + ci
                fst = first and ci == 0
                lst = drain and ci == 1
                t.matmul(p_nv[0:64, :], Vb[j % 2][:, c * 128: c * 128 + 64],
                         ptl[0][prev % 2][:, ci * 512:(ci + 1) * 512],
                         start=fst, stop=lst, skip_group_check=True, tile_position=(0, 0))
                t.matmul(p_nv[64:128, :], Vb[j % 2][:, c * 128 + 64: c * 128 + 128],
                         ptl[1][prev % 2][:, ci * 512:(ci + 1) * 512],
                         start=fst, stop=lst, skip_group_check=True, tile_position=(0, 64))
                t.matmul(p_sum[0:1, :], ones_b[:, :],
                         ptl[0][prev % 2][:, ci * 512:(ci + 1) * 512],
                         start=fst, stop=lst, skip_group_check=True, tile_position=(0, 0))
                mm = t.matmul(p_sum[64:65, :], ones_b[:, :],
                              ptl[1][prev % 2][:, ci * 512:(ci + 1) * 512],
                              start=fst, stop=lst, skip_group_check=True, tile_position=(0, 64))
            mm.then_inc(S_['av'], 1)

        @block.tensor
        def _(t):
            u = 0
            t.wait_ge(S_['tv'], 1)
            for j in range(NPAIR):
                jq0 = j * NQT
                t.wait_ge(S_['p2'] if j % 2 == 0 else S_['p2b'], 160 * (j // 2 + 1))
                t.wait_ge(S_['tl'], 2 * jq0)
                for nm, pj in (('up', 2), ('dn', 4)):
                    if j == pj:
                        e = 0 if nm == 'up' else 1
                        t.wait_ge(S_['tau'], 5 * e + 1)
                        t.matmul(p_nv[0:1, 0:1], ones_f[:, :], taur[:, :],
                                 start=True, stop=True,
                                 skip_group_check=True).then_inc(S_['tau'], 1)
                        t.wait_ge(S_['tau'], 5 * e + 3)
                        t.matmul(p_nv[:, 2:3], onesr_f[:, :], kqc[0:1, 22:23],
                                 start=True, stop=True,
                                 skip_group_check=True).then_inc(S_['tau'], 1)
                        t.wait_ge(S_['tau'], 5 * e + 5)
                slK = slots[(128 * j + 2) % 3]
                fbK = [p_nv, p_sum, slK[:, 0:512], slK[:, 512:1024]]
                for f in range(20):
                    if f < 16:
                        qb, cg = f // 4, f % 4
                        t.wait_ge(S_['ksq'], 5 * j + qb + 1)
                        srcq = ksq
                    else:
                        cg = f - 16
                        t.wait_ge(S_['ksq'], 5 * j + 5)
                        srcq = ksq
                    if f >= 4:
                        t.wait_ge(S_['knr'], 11 * j + (f - 4) // 2 + 1)
                    bank = fbK[f % 4]
                    t.matmul(bank[0:64, :], selk[0:64, :], srcq[0:64, cg * 512:(cg + 1) * 512],
                             start=True, stop=True, skip_group_check=True, tile_position=(0, 0))
                    t.matmul(bank[64:128, :], selk[64:128, :],
                             srcq[64:128, cg * 512:(cg + 1) * 512],
                             start=True, stop=True, skip_group_check=True,
                             tile_position=(64, 64)).then_inc(S_['kn'], 1)
                t.wait_ge(S_['knr'], 11 * j + 11)
                t.matmul(p_nv[:, 0:1], onesr_b[:, :], cneg[0:1, 0:1],
                         start=True, stop=True, skip_group_check=True)
                t.matmul(p_nv[:, 1:2], onesr_b[:, :], cneg[0:1, 1:2],
                         start=True, stop=True, skip_group_check=True).then_inc(S_['kn'], 1)
                t.wait_ge(S_['knr'], 11 * j + 10)
                for qtr in range(NQT):
                    qg = jq0 + qtr
                    qc = qtr * QT
                    for cp in range(NCP):
                        for h_ in (0, 1):
                            slot = slots[(u + h_) % 3]
                            if u + h_ >= 3:
                                t.wait_ge(S_['ex'], u + h_ - 2)
                            base = 64 * h_
                            mm = None
                            for ci in (0, 1):
                                c = 2 * cp + ci
                                mm = t.matmul(slot[:, ci * 512:(ci + 1) * 512],
                                              Kb[j % 2][base:base + 64, c * 128:(c + 1) * 128],
                                              Qb[j % 2][base:base + 64, qc:qc + QT],
                                              start=True, stop=True, skip_group_check=True,
                                              tile_position=(base, 0))
                            mm.then_inc(S_['sc'], 1)
                        u += 2
                        if cp >= 1:
                            _avblock(t, j, qg, cp - 1, u, first=(cp == 1),
                                     need_cb=(qtr == 0))
                    _avblock(t, j, qg, NCP - 1, u, first=False, drain=True)

        def recs_store(eng, qs):
            js, qcs = qs // NQT, (qs % NQT) * QT
            eng.wait_ge(S_['bc'], qs + 1)
            eng.dma_start(recs_d[0:2, js * QOWN + qcs: js * QOWN + qcs + QT],
                          recsT[qs % 2][0:65:64, 0:QT]).then_inc(S_['rc'], 16)

        @block.scalar
        def _(sc):
            u = 0
            for j in range(NPAIR):
                sc.wait_ge(S_['cb'], j + 1)
                for qtr in range(NQT):
                    qg = j * NQT + qtr
                    for cp in range(NCP):
                        if cp == 12 and qg >= 1:
                            recs_store(sc, qg - 1)
                        for h_ in (0, 1):
                            sc.wait_ge(S_['sc'], u + 1)
                            guard = qg * NCP + cp - 1
                            if guard > 0:
                                sc.wait_ge(S_['av'], guard)
                            bias = biasA if h_ == 0 else biasB
                            sc.activation(ptl[h_][cp % 2][:, :], slots[u % 3][:, :], AF.Exp,
                                          bias=bias[:, 0:1], scale=1.0).then_inc(S_['ex'], 1)
                            u += 1



        @block.vector
        def _(v):
            pend_rec = [None]

            def emit_recs(v):
                if pend_rec[0] is None:
                    return
                qg = pend_rec[0]
                pend_rec[0] = None
                if qg >= 2:
                    v.wait_ge(S_['rc'], 16 * (qg - 1))
                v.reciprocal(recsT[qg % 2][0:1, :], sumsT[qg % 2][0:1, :])
                v.reciprocal(recsT[qg % 2][64:65, :], sumsT[qg % 2][64:65, :])
                padV(v, 'bc')
            for j in range(NPAIR):
                jq0 = j * NQT
                for nm, pj in (('up', 2), ('dn', 4)):
                    if j == pj:
                        e = 0 if nm == 'up' else 1
                        cols = (0, 32) if nm == 'up' else (32, 64)
                        v.wait_ge(S_['tv'], 1 + (32 if nm == 'up' else 64))
                        v.reduce_sum(taur[:, 0:1], tacc[:, cols[0]:cols[1]],
                                     axis=mybir.AxisListType.X)
                        padV(v, 'tau')
                        v.wait_ge(S_['tau'], 5 * e + 2)
                        v.tensor_single_scalar(kqc[0:1, 22:23], p_nv[0:1, 0:1], TAUF, ALU.mult)
                        padV(v, 'tau')
                        v.wait_ge(S_['tau'], 5 * e + 4)
                        dst = tau_up if nm == 'up' else tau_dn
                        v.tensor_copy(dst[:, :], p_nv[:, 2:3])
                        padV(v, 'tau')
                slK = slots[(128 * j + 2) % 3]
                fbK = [p_nv, p_sum, slK[:, 0:512], slK[:, 512:1024]]
                v.wait_ge(S_['p2'] if j % 2 == 0 else S_['p2b'], 160 * (j // 2 + 1))
                fb2 = [PS[3][:, 0:1024], slK[:, 0:1024]]
                for qb in range(4):
                    if qb >= 1:
                        v.wait_ge(S_['kn'], 21 * j + 4 * qb)
                    v.tensor_tensor(ksq[:, :], Kb[j % 2][:, qb * 2048:(qb + 1) * 2048],
                                    Kb[j % 2][:, qb * 2048:(qb + 1) * 2048],
                                    ALU.mult).then_inc(S_['ksq'], 1)
                    for r in range(2 * qb, 2 * qb + 2):
                        v.wait_ge(S_['kn'], 21 * j + 2 * r + 2)
                        v.reduce_max(kqc[:, r:r + 1], fb2[r % 2][:, :],
                                     axis=mybir.AxisListType.X).then_inc(S_['knr'], 1)
                v.wait_ge(S_['kn'], 21 * j + 16)
                v.tensor_tensor(ksq[:, :], Qb[j % 2][:, :], Qb[j % 2][:, :],
                                ALU.mult).then_inc(S_['ksq'], 1)
                for r in range(8, 10):
                    v.wait_ge(S_['kn'], 21 * j + 2 * r + 2)
                    v.reduce_max(kqc[:, r:r + 1], fb2[r % 2][:, :],
                                 axis=mybir.AxisListType.X).then_inc(S_['knr'], 1)
                v.wait_ge(S_['knr'], 11 * j + 10)
                v.reduce_max(kqc[:, 20:21], kqc[:, 0:8], axis=mybir.AxisListType.X)
                v.reduce_max(kqc[:, 21:22], kqc[:, 8:10], axis=mybir.AxisListType.X)
                v.tensor_tensor(kqc[:, 22:23], kqc[:, 20:21], kqc[:, 21:22], ALU.add)
                v.tensor_single_scalar(kqc[:, 22:23], kqc[:, 22:23], 0.5, ALU.mult)
                v.tensor_single_scalar(cneg[0:1, 0:1], kqc[0:1, 22:23], -1.0, ALU.mult)
                v.tensor_single_scalar(cneg[0:1, 1:2], kqc[64:65, 22:23], -1.0, ALU.mult)
                padV(v, 'knr')
                if j >= 1:
                    v.wait_ge(S_['ex'], 256 * j)
                v.wait_ge(S_['kn'], 21 * j + 21)
                v.tensor_copy(biasA[:, :], p_nv[:, 0:1])
                v.tensor_copy(biasB[:, :], p_nv[:, 1:2])
                padV(v, 'cb')
                emit_recs(v)
                i0 = sum(len(TERN_SCHED[q]) for q in range(j))
                tu = TERN_SCHED[j]
                for qtr in range(NQT):
                    qg = jq0 + qtr
                    qc = qtr * QT
                    emit_recs(v)
                    for ii in range(4 * qtr, 4 * qtr + 4):
                        if ii >= len(tu):
                            break
                        kind, k = tu[ii]
                        i = i0 + ii
                        tws = [S_['tw'], S_['tw1'], S_['tw2']][i % 3]
                        v.wait_ge(tws, 16 * (i // 3 + 1))
                        w = 1024
                        if kind in ('u1', 'd1'):
                            col = k if kind == 'u1' else 32 + k
                            v.reduce_sum(tacc[:, col:col + 1], twk[i % 3][:, 0:w],
                                         axis=mybir.AxisListType.X,
                                         apply_absolute_value=True).then_inc(S_['tv'], 1)
                        else:
                            tau_t = tau_up if kind == 'u2' else tau_dn
                            if TERN_OUT_BEFORE[i] > 0:
                                v.wait_ge(S_['to'], 16 * TERN_OUT_BEFORE[i])
                            v.scalar_tensor_tensor(tabs[:, 0:w], twk[i % 3][:, 0:w], -1.0,
                                                   twk[i % 3][:, 0:w], ALU.mult, ALU.max)
                            v.tensor_single_scalar(tabs[:, 0:w], tabs[:, 0:w],
                                                   tau_t[:, 0:1], ALU.subtract)
                            v.tensor_single_scalar(tmf[:, 0:w], tabs[:, 0:w], 0.0, ALU.is_gt)
                            v.tensor_single_scalar(tabs[:, 0:w], twk[i % 3][:, 0:w], 0.0,
                                                   ALU.is_gt)
                            v.tensor_scalar(tabs[:, 0:w], tabs[:, 0:w], 2.0, -1.0,
                                            ALU.mult, ALU.add)
                            v.tensor_tensor(tout[i % 2][:, 0:w], tabs[:, 0:w], tmf[:, 0:w],
                                            ALU.mult).then_inc(S_['tv'], 1)
                    v.wait_ge(S_['av'], (qg + 1) * NCP)
                    v.tensor_copy(sumsT[qg % 2][:, :], p_sum).then_inc(S_['tl'], 1)
                    v.tensor_copy(NV[:, j * QOWN + qc: j * QOWN + qc + QT],
                                  p_nv).then_inc(S_['tl'], 1)
                    pend_rec[0] = qg

        @block.vector
        def _(v):
            qg = NPAIR * NQT - 1
            v.wait_ge(S_['rc'], 16 * (qg - 1))
            v.reciprocal(recsT[qg % 2][0:1, :], sumsT[qg % 2][0:1, :])
            v.reciprocal(recsT[qg % 2][64:65, :], sumsT[qg % 2][64:65, :])
            padV(v, 'bc')

        if rank == 2:
            @block.sync
            def _(sy):
                sy.wait_ge(S_['tl'], 2 * NPAIR * NQT)
                sy.wait_ge(S_['to'], 16 * N_TERN_OUT)
                sy.dma_start(dbg4, NV[:, :]).then_inc(S_['out'], 16)
                sy.dma_start(dbg5[:, 0:1], tau_up[:, :], allow_slow_non_contiguous=True).then_inc(S_['out'], 16)
                sy.dma_start(dbg5[:, 1:2], tau_dn[:, :], allow_slow_non_contiguous=True).then_inc(S_['out'], 16)
                sy.dma_start(dbg5[:, 16:80], tacc[:, :]).then_inc(S_['out'], 16)
                sy.dma_start(dbg3[:, 0:1024], tout[0][:, 0:1024]).then_inc(S_['out'], 16)
                sy.dma_start(dbg1.rearrange("p (a q) -> p a q", a=64),
                             WUPt_d[0:64, :].rearrange("a (p q) -> p a q", p=128)
                             ).then_inc(S_['out'], 16)
                sy.dma_start(dbg5[:, 2048:3072], twk[0][:, :]).then_inc(S_['out'], 16)
                sy.dma_start(dbg5[:, 4096:6144],
                             wupT_d[0:128, 0:2048]).then_inc(S_['out'], 16)
                sy.wait_ge(S_['out'], 128)

        p3slots = [PS[0][:, 0:512], PS[0][:, 512:1024], PS[1][:, 0:512], PS[1][:, 512:1024]]

        if rank >= 3:

            nb3 = [PS[2][:, 0:512], PS[2][:, 512:1024], PS[3][:, 0:512], PS[3][:, 512:1024]]

            @block.sync
            def _(sy):
                sy.wait_ge(S_['rc'], 16 * (NPAIR * NQT - 1))
                for qg in range(4):
                    for jj in range(NPAIR):
                        bi = qg * NPAIR + jj
                        if jj == NPAIR - 1 and qg == 3:
                            continue
                        if bi >= 2:
                            sy.wait_ge(S_['nn'], bi - 1)
                        sy.dma_start(recsP3[bi % 2][0:65:64, :],
                                     recs_d[0:2, jj * QOWN + qg * QT: jj * QOWN + (qg + 1) * QT]
                                     ).then_inc(S_['rl'], 16)

            @block.tensor
            def _(t):
                t.wait_ge(S_['tl'], 2 * NPAIR * NQT)
                t.wait_ge(S_['woc'], 16)
                def p3bcast(t, qg2, jj):
                    bi = qg2 * NPAIR + jj
                    if bi >= 4:
                        t.wait_ge(S_['p3n'], bi - 3)
                    if bi == NPAIR * 4 - 1:
                        t.wait_ge(S_['bc'], NPAIR * NQT)
                        src_ = recsT[(NPAIR * 4 - 1) % 2][0:65, :]
                    else:
                        t.wait_ge(S_['rl'], 16 * min(bi + 2, NPAIR * 4 - 1))
                        src_ = recsP3[bi % 2][0:65, :]
                    t.matmul(nb3[bi % 4], sel[0:65, :], src_,
                             start=True, stop=True,
                             skip_group_check=True).then_inc(S_['nn'], 1)
                for jj in range(NPAIR):
                    p3bcast(t, 0, jj)
                for qg in range(4):
                    for dc in range(NDC):
                        idx = qg * NDC + dc
                        if idx >= 4:
                            t.wait_ge(S_['p3e'], idx - 3)
                        t.wait_ge(S_['p3n'], NPAIR * (qg + 1))
                        mm = None
                        for hc in range(NDC):
                            mm = t.matmul(p3slots[idx % 4],
                                          wot[:, hc * D + dc * 128: hc * D + (dc + 1) * 128],
                                          NV[:, hc * QOWN + qg * 512: hc * QOWN + (qg + 1) * 512],
                                          start=(hc == 0), stop=(hc == NDC - 1),
                                          skip_group_check=True)
                        mm.then_inc(S_['p3'], 1)
                        if qg < 3:
                            p3bcast(t, qg + 1, dc)

            @block.vector
            def _(v):
                def p3tt(v, qg2, jj):
                    bi = qg2 * NPAIR + jj
                    v.wait_ge(S_['nn'], bi + 1)
                    v.tensor_tensor(NV[:, jj * QOWN + qg2 * 512: jj * QOWN + (qg2 + 1) * 512],
                                    NV[:, jj * QOWN + qg2 * 512: jj * QOWN + (qg2 + 1) * 512],
                                    nb3[bi % 4], ALU.mult).then_inc(S_['p3n'], 1)
                for jj in range(NPAIR):
                    p3tt(v, 0, jj)
                for qg in range(4):
                    for dc in range(NDC):
                        idx = qg * NDC + dc
                        v.wait_ge(S_['p3'], idx + 1)
                        v.tensor_scalar(ST[:, dc * QOWN + qg * 512: dc * QOWN + (qg + 1) * 512],
                                        p3slots[idx % 4], bout_c[:, dc:dc + 1], None,
                                        ALU.add).then_inc(S_['p3e'], 1)
                        if qg < 3:
                            p3tt(v, qg + 1, dc)

    if rank == 3:
        @block.sync
        def _(sy):
            sy.wait_ge(S_['p3e'], 32)
            sy.dma_start(dbg4, ST[:, :]).then_inc(S_['out'], 16)
            sy.wait_ge(S_['out'], 16)
    if rank < 4:
        ctx.close()
        return nc
    with ExitStack() as sc4:
        t4 = lambda *a: sc4.enter_context(nc.sbuf_tensor(*a))
        xblk4s = t4("xblk4", [128, NDC * TB], F32)
        xblk4 = [xblk4s, xblk4s]
        xbf4 = t4("xbf4", [128, NDC * TB], BF16)
        xsq4 = t4("xsq4", [128, NDC * TB], BF16)
        rowA4 = t4("rowA4", [1, 6 * TB], F32)
        rowB4 = t4("rowB4", [1, 4 * TB], F32)
        rowC4 = t4("rowC4", [1, 4 * TB], BF16)
        HF = t4("HF", [128, NDC * QOWN], BF16)
        mid = t4("mid", [128, NFC * 512], BF16)
        Wub = [t4("Wub%d" % i, [128, NDC * 128], BF16) for i in range(4)]
        Wdb = [t4("Wdb%d" % i, [128, NFC * 128], BF16) for i in range(3)]
        WD3 = [3, 3, 2]
        a3s = t4("a3s", [128, 512], F32)
        gtile = t4("gtile", [128, 512], F32)
        gsil = t4("gsil", [128, 512], BF16)
        xfin = [t4("xfin%d" % i, [128, 512], F32) for i in range(2)]
        a3hist = None
        yout = [t4("yout", [128, 512], F32)]
        yout = [yout[0], yout[0]]
        hsc4 = t4("hsc4", [128, TB], BF16)
        env4 = dict(xblk=xblk4, xbf=xbf4, xsq=xsq4, rowA=rowA4, rowB=rowB4, rowC=rowC4,
                    hdst=HF, hT=None, hsc=hsc4, xd4base=GP4)

        NTB4 = 4

        def wo_count(tb, item, i):
            base = tb * (NFC + 5 * NDC)
            if item == 'wub':
                return 16 * (base + i + 1)
            if item == 'wdb':
                return 16 * (base + NFC + 5 * i + 4)
            return 16 * (base + NFC + 5 * i + 5)

        @block.sync
        def _(sy):
            sy.wait_ge(S_['p3e'], 32)
            for g in range(GP4, NG):
                off = (g - GP4) * TB
                for cc in range(NDC):
                    if g >= GP4 + 1:
                        sy.wait_ge(S_['hh'], 8 * (g - 1) + cc + 1)
                    sy.dma_start(xblk4[g % 2][:, cc * TB:(cc + 1) * TB],
                                 xTo[cc * 128:(cc + 1) * 128, off:off + TB]
                                 ).then_inc(S_['xd4'], 16)
            if not p4_full:
                sy.wait_ge(S_['hh'], 8 * NG)
                sy.dma_start(dbg4, ST[:, :]).then_inc(S_['out'], 16)
                sy.wait_ge(S_['out'], 16)
            for tb in range(NTB4 if p4_full else 0):
                if tb == 1 and rank == 5:
                    sy.wait_ge(S_['fa4'], 32)
                    sy.dma_start(dbgM, mid[:, :]).then_inc(S_['dmp'], 16)
                    sy.wait_ge(S_['dmp'], 16)
                for fc in range(NFC):
                    if fc >= 4:
                        sy.wait_ge(S_['f4'], tb * F4_PER + fc - 3)
                    elif tb >= 1:
                        sy.wait_ge(S_['f4'], (tb - 1) * F4_PER + 29 + fc)
                    if tb == 0 and fc == 0:
                        sy.wait_ge(S_['to'], 16 * N_TERN_OUT)
                    sy.dma_start(Wub[fc % 4][:, :].rearrange("p (c e) -> p c e", c=NDC),
                                 WUPt_d[fc::NFC, :].rearrange("c (p e) -> p c e", p=128)
                                 ).then_inc(S_['wu%d' % (fc % 4)], 16)
                for dc in range(NDC):
                    if dc >= 3:
                        sy.wait_ge(S_['f4'], tb * F4_PER + 31 + dc)
                    elif tb >= 1:
                        sy.wait_ge(S_['f4'], (tb - 1) * F4_PER + 39 + dc)
                    for k4 in range(4):
                        sy.dma_start(Wdb[dc % 3][:, k4 * 1024:(k4 + 1) * 1024]
                                     .rearrange("p (c e) -> p c e", c=8),
                                     WDNt_d[dc::NDC, :][k4 * 8:(k4 + 1) * 8, :]
                                     .rearrange("c (p e) -> p c e", p=128)
                                     ).then_inc(S_['wd%d' % (dc % 3)], 16)
                    if tb * NDC + dc - 2 >= 0:
                        sy.wait_ge(S_['fy'], tb * NDC + dc - 1)
                    sy.dma_start(xfin[dc % 2][:, :],
                                 xTo[dc * 128:(dc + 1) * 128, tb * 512:(tb + 1) * 512]
                                 ).then_inc(S_['wx'], 16)
            if p4_full:
                sy.wait_ge(S_['out'], 16 * NTB4 * NDC)
            if rank == 5:
                sy.dma_start(dbg1[:, 0:1024].rearrange("p (a q) -> p a q", a=8),
                             WUPt_d[0:8, :].rearrange("a (p q) -> p a q", p=128)
                             ).then_inc(S_['out'], 16)
                sy.dma_start(dbg2[:, 0:1024].rearrange("p (a q) -> p a q", a=8),
                             WDNt_d[0:8, :].rearrange("a (p q) -> p a q", p=128)
                             ).then_inc(S_['out'], 16)
                sy.dma_start(dbg4, ST[:, :]).then_inc(S_['out'], 16)
                sy.dma_start(dbg5[:, 0:512], a3s[:, :]).then_inc(S_['out'], 16)

                sy.dma_start(dbg5[0:1, 1024:1536], rowA4[0:1, 512:1024],
                             allow_slow_non_contiguous=True).then_inc(S_['out'], 16)
                sy.dma_start(dbg5[0:1, 1536:2048], rowA4[0:1, 1024:1536],
                             allow_slow_non_contiguous=True).then_inc(S_['out'], 16)
                sy.dma_start(dbg5[:, 512:1024], gtile[:, :]).then_inc(S_['out'], 16)
                sy.dma_start(dbg3[:, 1024:1536], mid[:, 0:512]).then_inc(S_['out'], 16)
                sy.dma_start(dbgH, HF[:, :]).then_inc(S_['out'], 16)
                sy.wait_ge(S_['out'], 16 * (NTB4 * NDC + 9))

        def mid_sq_view(fc):
            return xsq4[:, (fc % NDC) * 512:(fc % NDC) * 512 + 512]

        def _summ(t, tb, fc):
            t.wait_ge(S_['fe4'], tb * FE4_PER + fc + 1)
            t.matmul(PS[3][0:1, 0:512], ones_b[:, :], mid_sq_view(fc),
                     start=(fc == 0), stop=(fc == NFC - 1),
                     skip_group_check=True).then_inc(S_['fs'], 1)

        @block.gpsimd
        def _(gp_):
            for tb in range(NTB4 if p4_full else 0):
                for dc in range(NDC):
                    gp_.wait_ge(S_['fy'], tb * NDC + dc + 1)
                    gp_.dma_start(yT[dc * 128:(dc + 1) * 128, tb * 512:(tb + 1) * 512],
                                  yout[0][:, :]).then_inc(S_['out'], 16)

        @block.tensor
        def _(t):
            for g in range(GP4, NG):
                if g > GP4:
                    stats_pe_work(t, env4, g - 1, 'p4')
                stats_pe_stats(t, env4, g)
            stats_pe_work(t, env4, NG - 1, 'p4')
            for tb in range(NTB4 if p4_full else 0):
                for fc in range(NFC):
                    t.wait_ge(S_['wu%d' % (fc % 4)], 16 * (8 * tb + fc // 4 + 1))
                    if tb == 0 and fc == 0:
                        t.wait_ge(S_['hh'], 8 * NG)
                        t.wait_ge(S_['sa'], 3 * (NG - 1) + 1)
                    if fc >= 3:
                        t.wait_ge(S_['fa4'], tb * FA4_PER + fc - 2)
                    elif tb >= 1:
                        t.wait_ge(S_['fe4'], tb * FE4_PER)
                    mm = None
                    for dc in range(NDC):
                        mm = t.matmul(PS[fc % 3][:, 0:512],
                                      Wub[fc % 4][:, dc * 128:(dc + 1) * 128],
                                      HF[:, dc * QOWN + tb * 512: dc * QOWN + tb * 512 + 512],
                                      start=(dc == 0), stop=(dc == NDC - 1),
                                      skip_group_check=True)
                    mm.then_inc(S_['f4'], 1)
                    if fc >= 2:
                        _summ(t, tb, fc - 2)
                _summ(t, tb, NFC - 2)
                _summ(t, tb, NFC - 1)
                t.wait_ge(S_['fe4'], tb * FE4_PER + 33)
                t.matmul(PS[3][:, 512:1024], onesr_b[:, :], rowC4[0:1, 0:512],
                         start=True, stop=False, skip_group_check=True)
                t.matmul(PS[3][:, 512:1024], onesr_b[:, :], rowC4[0:1, 512:1024],
                         start=False, stop=True, skip_group_check=True).then_inc(S_['f4'], 1)
                for dc in range(NDC):
                    t.wait_ge(S_['fa4'], tb * FA4_PER + 32)
                    if dc == 0:
                        t.wait_ge(S_['fe4'], tb * FE4_PER + 32)
                    t.wait_ge(S_['wd%d' % (dc % 3)],
                              16 * 4 * (WD3[dc % 3] * tb + dc // 3 + 1))
                    if dc >= 2:
                        t.wait_ge(S_['fe4'], tb * FE4_PER + 35 + dc - 2)
                    mm = None
                    for fc in range(NFC):
                        mm = t.matmul(PS[dc % 2][:, 0:512],
                                      Wdb[dc % 3][:, fc * 128:(fc + 1) * 128],
                                      mid[:, fc * 512:(fc + 1) * 512],
                                      start=(fc == 0), stop=(fc == NFC - 1),
                                      skip_group_check=True)
                    mm.then_inc(S_['f4'], 1)

        @block.scalar
        def _(sc):
            for g in range(GP4, NG):
                stats_act_rows1(sc, env4, g)
                stats_act_rows2(sc, env4, g, 'p4')
                stats_act_rows3(sc, env4, g, 'p4')
            for tb in range(NTB4 if p4_full else 0):
                for fc in range(NFC):
                    sc.wait_ge(S_['f4'], tb * F4_PER + fc + 1)
                    sc.activation(mid[:, fc * 512:(fc + 1) * 512], PS[fc % 3][:, 0:512],
                                  AF.Gelu).then_inc(S_['fa4'], 1)
                sc.wait_ge(S_['fs'], (tb + 1) * FS_PER)
                sc.activation(rowA4[0:1, 0:512], PS[3][0:1, 0:512], AF.Ln, scale=1.0 / FF)
                sc.activation(rowA4[0:1, 1024:1536], rowA4[0:1, 0:512], AF.Exp, scale=0.5)
                sc.activation(rowA4[0:1, 1536:2048], rowA4[0:1, 1024:1536], AF.Identity,
                              bias=eps_t[0:1, 0:1])
                sc.activation(rowA4[0:1, 0:512], rowA4[0:1, 1536:2048], AF.Ln)
                sc.activation(rowA4[0:1, 1024:1536], rowA4[0:1, 0:512], AF.Exp,
                              scale=-1.0)
                padA(sc, 'fa4')
                for dc in range(NDC):
                    sc.wait_ge(S_['fe4'], tb * FE4_PER + 35 + dc)
                    sc.activation(gsil[:, :], gtile[:, :], AF.Silu)
                    padA(sc, 'fa4')

        @block.vector
        def _(v):
            stats_dve_pre(v, env4, GP4)
            for g in range(GP4, NG):
                stats_dve_rows1(v, env4, g)
                stats_dve_rows2(v, env4, g)
                stats_dve_rows3(v, env4, g, 'p4')
                stats_dve_rows4(v, env4, g)
                stats_dve_h(v, env4, g, 'p4')
                if g + 1 < NG:
                    stats_dve_pre(v, env4, g + 1)
            for tb in range(NTB4 if p4_full else 0):
                for fc in range(NFC):
                    v.wait_ge(S_['fa4'], tb * FA4_PER + fc + 1)
                    if fc >= NDC:
                        v.wait_ge(S_['fs'], tb * FS_PER + fc - NDC + 1)
                    v.tensor_tensor(mid_sq_view(fc), mid[:, fc * 512:(fc + 1) * 512],
                                    mid[:, fc * 512:(fc + 1) * 512],
                                    ALU.mult).then_inc(S_['fe4'], 1)
                v.wait_ge(S_['fa4'], tb * FA4_PER + 33)
                v.tensor_copy(rowC4[0:1, 0:512], rowA4[0:1, 1024:1536])
                v.tensor_tensor(rowC4[0:1, 512:1024], rowA4[0:1, 1024:1536], rowC4[0:1, 0:512],
                                ALU.subtract)
                padV(v, 'fe4')
                v.wait_ge(S_['f4'], tb * F4_PER + 33)

                v.tensor_copy(a3s[:, :], PS[3][:, 512:1024]).then_inc(S_['fe4'], 1)
                for dc in range(NDC):
                    v.wait_ge(S_['f4'], tb * F4_PER + 33 + dc + 1)
                    v.tensor_tensor(gtile[:, :], PS[dc % 2][:, 0:512], a3s[:, :],
                                    ALU.mult)
                    padV(v, 'fe4')
                    v.wait_ge(S_['fa4'], tb * FA4_PER + 34 + dc)
                    v.wait_ge(S_['wx'], 16 * (tb * NDC + dc + 1))
                    yi = tb * NDC + dc
                    if yi >= 1:
                        v.wait_ge(S_['out'], 16 * yi)
                    v.tensor_tensor(gsil[:, :],
                                    ST[:, dc * QOWN + tb * 512: dc * QOWN + (tb + 1) * 512],
                                    gsil[:, :], ALU.mult)
                    v.tensor_tensor(yout[0][:, :], xfin[dc % 2][:, :], gsil[:, :],
                                    ALU.add)
                    padV(v, 'fy')
    ctx.close()
    return nc


def kernel(L0_attn_x, L0_attn_norm_scale, L0_attn_wq, L0_attn_wk, L0_attn_wv,
           L0_attn_w_out, L0_attn_b_out, L0_ffn_w_up, L0_ffn_w_down):
    x = np.asarray(L0_attn_x, dtype=np.float32)
    wq = np.ascontiguousarray(np.asarray(L0_attn_wq, np.float32).reshape(D, D))
    wk = np.ascontiguousarray(np.asarray(L0_attn_wk, np.float32).reshape(D, D))
    wv = np.ascontiguousarray(np.asarray(L0_attn_wv, np.float32).reshape(D, D))
    wo = np.ascontiguousarray(np.asarray(L0_attn_w_out, np.float32).reshape(D, D))
    nsrow = np.ascontiguousarray(np.asarray(L0_attn_norm_scale, np.float32).reshape(1, D))
    boutc = np.ascontiguousarray(np.asarray(L0_attn_b_out, np.float32).reshape(NDC, 128).T)
    wupT = np.ascontiguousarray(np.asarray(L0_ffn_w_up, np.float32).T)
    wdnT = np.ascontiguousarray(np.asarray(L0_ffn_w_down, np.float32).T)
    xTb = [np.ascontiguousarray(x[b].T) for b in range(2)]

    nc = build_program()
    in_maps = []
    for c in range(8):
        b, qi = c // 4, c % 4
        xTr = np.ascontiguousarray(np.roll(xTb[b], -qi * QOWN, axis=1))
        xTo = np.ascontiguousarray(x[b, qi * QOWN:(qi + 1) * QOWN, :].T)
        in_maps.append({"xT": xTr, "xTo": xTo, "wq": wq, "wk": wk, "wv": wv, "wo": wo,
                        "nsr": nsrow, "boc": boutc, "wupT": wupT, "wdnT": wdnT})
    res = run_bass_kernel_spmd(nc, in_maps, core_ids=list(range(8)))
    y = np.empty((2, S, D), dtype=np.float32)
    for c in range(8):
        b, qi = c // 4, c % 4
        y[b, qi * QOWN:(qi + 1) * QOWN, :] = res.results[c]["yT"].T
    return y


# version 24 (final)
# speedup vs baseline: 1.2270x; 1.0063x over previous
import numpy as np
from contextlib import ExitStack

import concourse.bass as bass
import concourse.mybir as mybir
from concourse.bass_utils import run_bass_kernel_spmd

F32 = mybir.dt.float32
BF16 = mybir.dt.bfloat16
AF = mybir.ActivationFunctionType
ALU = mybir.AluOpType

D, S, QOWN = 1024, 8192, 2048
NDC, NPAIR, NCH, NCP, NQT, QT = 8, 8, 64, 32, 4, 512
FF, NFC, TB = 4096, 32, 512
NB1 = S // TB
NB2 = QOWN // TB
GP1B, GP4 = NB1, NB1
NG = GP4 + NB2
NGA = GP4
RT2 = 0.35355339059327373
INV_D = 1.0 / D
TAUF = 0.5 / (D * FF)

TERN_SCHED = {
    0: [('u1', k) for k in range(16)],
    1: [('u1', k) for k in range(16, 32)],
    2: [('d1', k) for k in range(16)],
    3: [('d1', k) for k in range(16, 32)],
    4: [('u2', k) for k in range(16)],
    5: [('u2', k) for k in range(16, 32)],
    6: [('d2', k) for k in range(16)],
    7: [('d2', k) for k in range(16, 32)],
}
TERN_UNITS = [u for p in range(8) for u in TERN_SCHED[p]]


def tern_out_dmas(kind):
    return 1 if kind in ('u2', 'd2') else 0


TERN_OUT_BEFORE = [0]
for _u in TERN_UNITS:
    TERN_OUT_BEFORE.append(TERN_OUT_BEFORE[-1] + tern_out_dmas(_u[0]))
N_TERN_OUT = TERN_OUT_BEFORE[-1]

F4_PER = 41
FA4_PER = 41
FE4_PER = 42
FS_PER = 32


def blk_kind(g):
    return 'p1' if g < GP1B else 'p4'


def n_units(g):
    if blk_kind(g) == 'p4':
        return 0
    return 24 if g < NB2 else 16


UB = [0] * (NG + 1)
KO = [0] * (NG + 1)
VO = [0] * (NG + 1)
for g in range(NG):
    UB[g + 1] = UB[g] + n_units(g)
    KO[g + 1] = KO[g] + (0 if blk_kind(g) == 'p4' else (16 if g < NB2 else 8))
    VO[g + 1] = VO[g] + (8 if blk_kind(g) == 'p1' else 0)


def tc128(i):
    return (i // 2) * 128


def build_program(stop='full'):
    rank = {'p1': 1, 'p2': 2, 'p3': 3, 'full': 4, 'dbg': 5, 'p4s': 6}[stop]
    p4_full = rank != 6
    nc = bass.Bass("TRN2", target_bir_lowering=False)

    xT = nc.dram_tensor("xT", [D, S], F32, kind="ExternalInput").ap()
    xTo = nc.dram_tensor("xTo", [D, QOWN], F32, kind="ExternalInput").ap()
    wq_d = nc.dram_tensor("wq", [D, D], F32, kind="ExternalInput").ap()
    wk_d = nc.dram_tensor("wk", [D, D], F32, kind="ExternalInput").ap()
    wv_d = nc.dram_tensor("wv", [D, D], F32, kind="ExternalInput").ap()
    wo_d = nc.dram_tensor("wo", [D, D], F32, kind="ExternalInput").ap()
    nsr = nc.dram_tensor("nsr", [1, D], F32, kind="ExternalInput").ap()
    boc = nc.dram_tensor("boc", [128, NDC], F32, kind="ExternalInput").ap()
    wupT_d = nc.dram_tensor("wupT", [D, FF], F32, kind="ExternalInput").ap()
    wdnT_d = nc.dram_tensor("wdnT", [FF, D], F32, kind="ExternalInput").ap()
    yT = nc.dram_tensor("yT", [D, QOWN], F32, kind="ExternalOutput").ap()
    dbg1 = nc.dram_tensor("dbg1", [128, S], BF16, kind="ExternalOutput").ap()
    dbg2 = nc.dram_tensor("dbg2", [128, S], BF16, kind="ExternalOutput").ap()
    dbg3 = nc.dram_tensor("dbg3", [128, QOWN], BF16, kind="ExternalOutput").ap()
    dbg4 = nc.dram_tensor("dbg4", [128, NPAIR * QOWN], BF16, kind="ExternalOutput").ap()
    dbg5 = nc.dram_tensor("dbg5", [128, 6144], F32, kind="ExternalOutput").ap()
    dbgH = nc.dram_tensor("dbgH", [128, 16384], BF16, kind="ExternalOutput").ap()
    dbgM = nc.dram_tensor("dbgM", [128, 16384], BF16, kind="ExternalOutput").ap()

    KT_d = nc.dram_tensor("KT_d", [D, S], BF16).ap()
    V_d = nc.dram_tensor("V_d", [S, D], BF16).ap()
    QT_d = nc.dram_tensor("QT_d", [D, QOWN], BF16).ap()
    WUPt_d = nc.dram_tensor("WUPt_d", [NDC * NFC, 128 * 128], BF16).ap()
    WDNt_d = nc.dram_tensor("WDNt_d", [NFC * NDC, 128 * 128], BF16).ap()
    recs_d = nc.dram_tensor("recs_d", [2, 8 * 2048], F32).ap()

    ctx = ExitStack()
    ten = lambda *a: ctx.enter_context(nc.sbuf_tensor(*a))
    psu = lambda *a: ctx.enter_context(nc.psum_tensor(*a))
    S_ = {}
    for n in ("ld wl woc xd xq stm sa sb ab pbc hh kv ev eo vo p2 ksq kn knr cb tau sc ex av tl "
              "bc tw tv to wo p3 p3e f4 fs fa4 fe4 fy out dmp wd wx rc rl nn p3n p2b wu0 wu1 wu2 wu3 wd0 wd1 wd2 xd4 tw1 tw2 tw3").split():
        S_[n] = ctx.enter_context(nc.semaphore("s_" + n))

    ones_b = ten("ones_b", [128, 1], BF16)
    onesr_b = ten("onesr_b", [1, 128], BF16)
    onesr_f = ten("onesr_f", [1, 128], F32)
    ones_f = ten("ones_f", [128, 1], F32)
    sel = ten("sel", [128, 128], F32)
    selk = ten("selk", [128, 64], BF16)
    nsrow_b = ten("nsrow_b", [1, D], BF16)
    nscol = ten("nscol", [128, NDC], F32)
    nsrow_f = ten("nsrow_f", [1, D], F32)
    bout_c = ten("bout_c", [128, NDC], F32)
    eps_ln = ten("eps_ln", [1, 1], F32)
    scr1 = ten("scr1", [1, 1], F32)
    scrV = ten("scrV", [1, 1], F32)
    eps_rms = ten("eps_rms", [1, 1], F32)
    eps_t = ten("eps_t", [1, 1], F32)
    eps_f = ten("eps_f", [1, 1], F32)
    ST = ten("ST", [128, NDC * QOWN], BF16)

    PS = [psu("ps%d" % i, [128, 1024], F32) for i in range(4)]
    pA = [PS[0][:, 0:512], PS[1][:, 0:512]]
    pB = [PS[0][:, 512:1024], PS[1][:, 512:1024]]
    pKV = [PS[2][:, 0:512], PS[2][:, 512:1024]]
    pS1 = PS[3][0:1, 0:512]
    pS2 = PS[3][0:1, 512:1024]
    p_nv = PS[3][:, 0:512]
    p_sum = PS[3][:, 512:1024]

    block = ctx.enter_context(nc.Block())

    def padA(sc, sem, n=1):
        sc.activation(scr1[0:1, 0:1], scr1[0:1, 0:1], AF.Identity).then_inc(S_[sem], n)

    def padV(v, sem, n=1):
        v.tensor_copy(scrV[0:1, 0:1], scrV[0:1, 0:1]).then_inc(S_[sem], n)

    def RS(env, w, i):
        return env[w][0:1, i * TB:(i + 1) * TB]

    def stats_pe_stats(t, env, g):
        t.wait_ge(S_['xq'], g + 1)
        if g >= 1:
            t.wait_ge(S_['sa'], 3 * (g - 1) + 1)
        for c in range(NDC):
            t.matmul(pS1, ones_b[:, :], env['xbf'][:, c * TB:(c + 1) * TB],
                     start=(c == 0), stop=(c == NDC - 1), skip_group_check=True)
        mm = None
        for c in range(NDC):
            mm = t.matmul(pS2, ones_b[:, :], env['xsq'][:, c * TB:(c + 1) * TB],
                          start=(c == 0), stop=(c == NDC - 1), skip_group_check=True)
        mm.then_inc(S_['stm'], 1)

    def stats_dve_pre(v, env, g):
        if 'xd4base' in env:
            v.wait_ge(S_['xd4'], 128 * (g - env['xd4base'] + 1))
        else:
            v.wait_ge(S_['xd'], 16 * (g + 1))
        v.tensor_copy(env['xbf'][:, :], env['xblk'][g % 2][:, :])
        v.tensor_tensor(env['xsq'][:, :], env['xbf'][:, :], env['xbf'][:, :],
                        ALU.mult).then_inc(S_['xq'], 1)

    def stats_dve_rows1(v, env, g, n=1):
        v.wait_ge(S_['stm'], g + 1)
        v.tensor_single_scalar(RS(env, 'rowA', 0), pS1, INV_D, ALU.mult)
        v.tensor_single_scalar(RS(env, 'rowA', 5), pS2, INV_D, ALU.mult)
        v.tensor_tensor(RS(env, 'rowB', 0), RS(env, 'rowA', 0), RS(env, 'rowA', 0), ALU.mult)
        v.tensor_tensor(RS(env, 'rowA', 1), RS(env, 'rowA', 5), RS(env, 'rowB', 0),
                        ALU.subtract)
        padV(v, 'sa', n)

    def stats_act_fused(sc, env, g):
        sc.wait_ge(S_['sa'], 3 * g + 3)
        sc.activation(RS(env, 'rowA', 2), RS(env, 'rowA', 1), AF.Ln,
                      scale=1.0 + 1e-6, bias=eps_f[0:1, 0:1])
        sc.activation(RS(env, 'rowA', 4), RS(env, 'rowA', 2), AF.Exp, scale=-0.5)
        padA(sc, 'sb', 3)

    def stats_act_rows1(sc, env, g):
        sc.wait_ge(S_['sa'], 3 * g + 1)
        sc.activation(RS(env, 'rowA', 2), RS(env, 'rowA', 1), AF.Ln, bias=eps_ln[0:1, 0:1])
        sc.activation(RS(env, 'rowB', 1), RS(env, 'rowA', 2), AF.Exp, scale=-1.0)
        padA(sc, 'sb')

    def stats_dve_rows2(v, env, g):
        v.wait_ge(S_['sb'], 3 * g + 1)
        v.tensor_tensor(RS(env, 'rowA', 3), RS(env, 'rowA', 1), RS(env, 'rowB', 1),
                        ALU.mult)
        padV(v, 'sa')

    def stats_act_rows2(sc, env, g, kind):
        sc.wait_ge(S_['sa'], 3 * g + 2)
        if kind != 'p4':
            sc.activation(RS(env, 'rowB', 0), RS(env, 'rowA', 3), AF.Ln,
                          bias=eps_rms[0:1, 0:1])
            padA(sc, 'sb')
        else:
            sc.activation(RS(env, 'rowB', 0), RS(env, 'rowA', 3), AF.Ln)
            sc.activation(RS(env, 'rowB', 1), RS(env, 'rowB', 0), AF.Exp, scale=0.5)
            sc.activation(RS(env, 'rowB', 3), RS(env, 'rowA', 2), AF.Exp, scale=0.5)
            padA(sc, 'sb')

    def stats_dve_rows3(v, env, g, kind):
        v.wait_ge(S_['sb'], 3 * g + 2)
        if kind != 'p4':
            v.tensor_tensor(RS(env, 'rowB', 0), RS(env, 'rowA', 2), RS(env, 'rowB', 0),
                            ALU.add)
            padV(v, 'sa')
        else:
            v.tensor_single_scalar(RS(env, 'rowB', 0), RS(env, 'rowB', 1), 1e-8, ALU.add)
            v.tensor_tensor(RS(env, 'rowB', 0), RS(env, 'rowB', 3), RS(env, 'rowB', 0),
                            ALU.mult)
            padV(v, 'sa')

    def stats_act_rows3(sc, env, g, kind):
        sc.wait_ge(S_['sa'], 3 * g + 3)
        if kind != 'p4':
            sc.activation(RS(env, 'rowA', 4), RS(env, 'rowB', 0), AF.Exp, scale=-0.5)
            padA(sc, 'sb')
        else:
            sc.activation(RS(env, 'rowB', 3), RS(env, 'rowB', 0), AF.Ln)
            sc.activation(RS(env, 'rowA', 4), RS(env, 'rowB', 3), AF.Exp, scale=-1.0)
            padA(sc, 'sb')

    def stats_dve_rows4(v, env, g):
        v.wait_ge(S_['sb'], 3 * g + 3)
        v.tensor_tensor(RS(env, 'rowB', 2), RS(env, 'rowA', 0), RS(env, 'rowA', 4), ALU.mult)
        v.tensor_copy(RS(env, 'rowC', 0), RS(env, 'rowA', 4))
        v.tensor_tensor(RS(env, 'rowC', 1), RS(env, 'rowA', 4), RS(env, 'rowC', 0), ALU.subtract)
        v.tensor_scalar(RS(env, 'rowC', 2), RS(env, 'rowB', 2), -1.0, None, ALU.mult)
        v.scalar_tensor_tensor(RS(env, 'rowC', 3), RS(env, 'rowB', 2), -1.0,
                               RS(env, 'rowC', 2), ALU.mult, ALU.subtract)
        padV(v, 'ab')

    def stats_pe_work_bcast(t, env, g):
        t.wait_ge(S_['ab'], g + 1)
        if g >= 2:
            t.wait_ge(S_['hh'], 8 * (g - 1))
        ls = onesr_b[0:1, 0:128]
        t.matmul(pA[g % 2], ls, RS(env, 'rowC', 0), start=True, stop=False,
                 skip_group_check=True)
        t.matmul(pA[g % 2], ls, RS(env, 'rowC', 1), start=False, stop=True,
                 skip_group_check=True)
        t.matmul(pB[g % 2], ls, RS(env, 'rowC', 2), start=True, stop=False,
                 skip_group_check=True)
        t.matmul(pB[g % 2], ls, RS(env, 'rowC', 3), start=False, stop=True,
                 skip_group_check=True).then_inc(S_['pbc'], 1)

    def stats_pe_work(t, env, g, kind, wkt=None, wvt=None, wqt=None, split=False):
        if not split:
            stats_pe_work_bcast(t, env, g)
        nu = n_units(g)
        if nu == 0:
            return
        for u in range(nu):
            if u == 1:
                t.wait_ge(S_['hh'], 8 * (g + 1))
            if UB[g] + u - 1 >= 1:
                t.wait_ge(S_['ev'], UB[g] + u - 1)
            mm = None
            if u < 8:
                for dc in range(NDC):
                    if u == 0:
                        t.wait_ge(S_['hh'], min(8 * g + dc + 5, 8 * (g + 1)))
                    mm = t.matmul(pKV[u % 2], wkt[:, dc * D + u * 128: dc * D + (u + 1) * 128],
                                  env['hT'][g % 2][:, dc * TB:(dc + 1) * TB],
                                  start=(dc == 0), stop=(dc == NDC - 1), skip_group_check=True)
            elif u < 16:
                i = u - 8
                tc, half = i // 2, i % 2
                for dc in range(NDC):
                    mm = t.matmul(pKV[u % 2],
                                  env['hT'][g % 2][:, dc * TB + tc * 128: dc * TB + (tc + 1) * 128],
                                  wvt[:, dc * D + half * 512: dc * D + (half + 1) * 512],
                                  start=(dc == 0), stop=(dc == NDC - 1), skip_group_check=True)
            else:
                uq = u - 16
                for dc in range(NDC):
                    mm = t.matmul(pKV[u % 2], wqt[:, dc * D + uq * 128: dc * D + (uq + 1) * 128],
                                  env['hT'][g % 2][:, dc * TB:(dc + 1) * TB],
                                  start=(dc == 0), stop=(dc == NDC - 1), skip_group_check=True)
            mm.then_inc(S_['kv'], 1)

    def stats_dve_h(v, env, g, kind):
        for c in range(NDC):
            if c == 0:
                v.wait_ge(S_['pbc'], g + 1)
            if kind == 'p4':
                v.tensor_tensor(env['hsc'][:, :],
                                env['xblk'][g % 2][:, c * TB:(c + 1) * TB], pA[g % 2],
                                ALU.mult)
                boff = (g - GP4) * TB
                dst = env['hdst'][:, c * QOWN + boff: c * QOWN + boff + TB]
                v.tensor_tensor(dst, env['hsc'][:, :], pB[g % 2],
                                ALU.add).then_inc(S_['hh'], 1)
            else:
                v.scalar_tensor_tensor(env['hsc'][:, :],
                                       env['xblk'][g % 2][:, c * TB:(c + 1) * TB],
                                       nscol[:, c:c + 1], pA[g % 2], ALU.mult, ALU.mult)
                dst = env['hT'][g % 2][:, c * TB:(c + 1) * TB]
                v.scalar_tensor_tensor(dst, pB[g % 2], nscol[:, c:c + 1],
                                       env['hsc'][:, :],
                                       ALU.mult, ALU.add).then_inc(S_['hh'], 1)

    def stats_act_evac(sc, env, g, evK=None, evV=None, hooks=None):
        for u in range(n_units(g)):
            if hooks and u in hooks:
                hooks[u]()
            sc.wait_ge(S_['kv'], UB[g] + u + 1)
            if 8 <= u < 16:
                i = u - 8
                if VO[g] + i - 4 >= 0:
                    sc.wait_ge(S_['vo'], 16 * (VO[g] + i - 3))
                sc.activation(evV[i % 4][:, :], pKV[u % 2],
                              AF.Identity).then_inc(S_['ev'], 1)
            else:
                ko = u if u < 8 else 8 + (u - 16)
                if KO[g] + ko - 4 >= 0:
                    sc.wait_ge(S_['eo'], 16 * (KO[g] + ko - 3))
                sc.activation(evK[ko % 4][:, :], pKV[u % 2], AF.Identity,
                              scale=RT2).then_inc(S_['ev'], 1)

    with ExitStack() as sc1:
        t1 = lambda *a: sc1.enter_context(nc.sbuf_tensor(*a))
        xblk = [t1("xblk%d" % i, [128, NDC * TB], F32) for i in range(2)]
        xbf = t1("xbf", [128, NDC * TB], BF16)
        xsq = t1("xsq", [128, NDC * TB], BF16)
        hT = [t1("hT%d" % i, [128, NDC * TB], BF16) for i in range(2)]
        evK = [t1("evK%d" % i, [128, TB], BF16) for i in range(4)]
        evV = [t1("evV%d" % i, [128, TB], BF16) for i in range(4)]
        wqt = t1("wqt", [128, NDC * D], BF16)
        wkt = t1("wkt", [128, NDC * D], BF16)
        wvt = t1("wvt", [128, NDC * D], BF16)
        rowA = t1("rowA", [1, 6 * TB], F32)
        rowB = t1("rowB", [1, 4 * TB], F32)
        rowC = t1("rowC", [1, 4 * TB], BF16)
        hsc = t1("hsc", [128, TB], BF16)
        env1 = dict(xblk=xblk, xbf=xbf, xsq=xsq, hT=hT, rowA=rowA, rowB=rowB, rowC=rowC,
                    hsc=hsc)

        def blk_src(g):
            return xT, g * TB

        @block.sync
        def _(sy):
            sy.dma_start(nsrow_f[:, :], nsr).then_inc(S_['ld'], 16)
            sy.dma_start(nscol[:, :], nsr.rearrange("a (c p) -> p (a c)", p=128),
                         allow_slow_non_contiguous=True).then_inc(S_['ld'], 16)
            sy.dma_start(bout_c[:, :], boc).then_inc(S_['ld'], 16)
            for g in range(NGA):
                src, off = blk_src(g)
                if g >= 2:
                    sy.wait_ge(S_['hh'], 8 * (g - 1))
                sy.dma_start(xblk[g % 2][:, :].rearrange("p (c t) -> p c t", c=NDC),
                             src[:, off:off + TB].rearrange("(c p) t -> p c t", c=NDC)
                             ).then_inc(S_['xd'], 16)
            if rank == 1:
                sy.wait_ge(S_['eo'], 16 * KO[NGA])
                sy.wait_ge(S_['vo'], 16 * VO[NGA])
                sy.dma_start(dbg1, KT_d[0:128, :]).then_inc(S_['out'], 16)
                sy.dma_start(dbg2.rearrange("p (a d) -> p a d", d=D),
                             V_d[0:8 * 128, :].rearrange("(a p) d -> p a d", p=128)
                             ).then_inc(S_['out'], 16)
                sy.dma_start(dbg3, QT_d[0:128, :]).then_inc(S_['out'], 16)
                sy.wait_ge(S_['out'], 48)

        @block.gpsimd
        def _(gp_):
            gp_.memset(ones_b[:, :], 1.0)
            gp_.memset(onesr_b[:, :], 1.0)
            gp_.memset(onesr_f[:, :], 1.0)
            gp_.memset(ones_f[:, :], 1.0)
            gp_.memset(sel[:, :], 0.0)
            gp_.memset(sel[0:1, 0:64], 1.0)
            gp_.memset(selk[:, :], 0.0)
            gp_.memset(selk[:, 0:1], 1.0)
            gp_.memset(sel[64:65, 64:128], 1.0)
            gp_.memset(eps_ln[:, :], 1e-5)
            gp_.memset(eps_rms[:, :], 1e-6)
            gp_.memset(eps_t[:, :], 1e-8)
            gp_.memset(eps_f[:, :], 1e-11)
            gp_.memset(scr1[:, :], 0.0)
            gp_.memset(scrV[:, :], 0.0).then_inc(S_['ld'], 1)
            gp_.dma_start(wqt[:, :].rearrange("p (c e) -> p c e", c=NDC), wq_d.rearrange("(c p) e -> p c e", c=NDC)
                          ).then_inc(S_['wl'], 16)
            gp_.dma_start(wkt[:, :].rearrange("p (c e) -> p c e", c=NDC), wk_d.rearrange("(c p) e -> p c e", c=NDC)
                          ).then_inc(S_['wl'], 16)
            gp_.dma_start(wvt[:, :].rearrange("p (c e) -> p c e", c=NDC), wv_d.rearrange("(c p) e -> p c e", c=NDC)
                          ).then_inc(S_['wl'], 16)
            for g in range(NGA):
                for u in range(8):
                    gp_.wait_ge(S_['ev'], UB[g] + u + 2)
                    gp_.dma_start(KT_d[u * 128:(u + 1) * 128, g * TB:(g + 1) * TB],
                                  evK[u % 4][:, :]).then_inc(S_['eo'], 16)
                for i in range(8):
                    gp_.wait_ge(S_['ev'], UB[g] + 8 + i + 2)
                    gp_.dma_start(V_d[g * TB + tc128(i): g * TB + tc128(i) + 128,
                                      (i % 2) * 512:((i % 2) + 1) * 512],
                                  evV[i % 4][:, :]).then_inc(S_['vo'], 16)
                if n_units(g) > 16:
                    for u in range(16, 24):
                        gp_.wait_ge(S_['ev'], UB[g] + u + 2)
                        gp_.dma_start(QT_d[(u - 16) * 128:(u - 15) * 128,
                                           g * TB:(g + 1) * TB],
                                      evK[(8 + u - 16) % 4][:, :]).then_inc(S_['eo'], 16)

        @block.vector
        def _(v):
            v.wait_ge(S_['ld'], 49)
            v.wait_ge(S_['wl'], 48)
            v.tensor_copy(nsrow_b[:, :], nsrow_f[:, :])
            stats_dve_pre(v, env1, 0)

            def rows_chain(g2):
                kind2 = blk_kind(g2)
                stats_dve_rows1(v, env1, g2)
                stats_dve_rows2(v, env1, g2)
                stats_dve_rows3(v, env1, g2, kind2)
                stats_dve_rows4(v, env1, g2)
            rows_chain(0)
            stats_dve_pre(v, env1, 1)
            for g in range(NGA):
                kind = blk_kind(g)
                stats_dve_h(v, env1, g, kind)
                if g + 1 < NGA:
                    g2 = g + 1
                    stats_dve_rows1(v, env1, g2)
                    stats_dve_rows2(v, env1, g2)
                    stats_dve_rows3(v, env1, g2, blk_kind(g2))
                    stats_dve_rows4(v, env1, g2)
                    if g2 + 1 < NGA:
                        stats_dve_pre(v, env1, g2 + 1)

        @block.scalar
        def _(sc):
            sc.wait_ge(S_['ld'], 1)
            stats_act_rows1(sc, env1, 0)
            stats_act_rows2(sc, env1, 0, blk_kind(0))
            stats_act_rows3(sc, env1, 0, blk_kind(0))
            for g in range(NGA):
                nu = n_units(g)
                if g + 1 < NGA:
                    g2 = g + 1
                    hooks = {
                        5 * nu // 8: (lambda g2=g2: stats_act_rows1(sc, env1, g2)),
                        6 * nu // 8: (lambda g2=g2: stats_act_rows2(sc, env1, g2,
                                                                    blk_kind(g2))),
                        7 * nu // 8: (lambda g2=g2: stats_act_rows3(sc, env1, g2,
                                                                    blk_kind(g2))),
                    }
                else:
                    hooks = None
                stats_act_evac(sc, env1, g, evK=evK, evV=evV, hooks=hooks)
            padA(sc, 'ev')

        @block.tensor
        def _(t):
            t.wait_ge(S_['ld'], 49)
            t.wait_ge(S_['wl'], 48)
            for g in range(NGA):
                if g >= 1:
                    stats_pe_work_bcast(t, env1, g - 1)
                stats_pe_stats(t, env1, g)
                if g >= 1:
                    stats_pe_work(t, env1, g - 1, blk_kind(g - 1), wkt=wkt, wvt=wvt,
                                  wqt=wqt, split=True)
            stats_pe_work(t, env1, NGA - 1, blk_kind(NGA - 1), wkt=wkt, wvt=wvt, wqt=wqt)

    if rank < 2:
        ctx.close()
        return nc
    with ExitStack() as sc2:
        t2 = lambda *a: sc2.enter_context(nc.sbuf_tensor(*a))
        Kb = [t2("Kb%d" % i, [128, S], BF16) for i in range(2)]
        Vb = [t2("Vb%d" % i, [128, S], BF16) for i in range(2)]
        Qb = [t2("Qb%d" % i, [128, QOWN], BF16) for i in range(2)]
        NV = t2("NV", [128, NPAIR * QOWN], BF16)
        ptl = {0: [t2("pA%d" % i, [128, 1024], BF16) for i in range(2)],
               1: [t2("pB%d" % i, [128, 1024], BF16) for i in range(2)]}
        sums = t2("sums", [128, QT], F32)
        sums2 = t2("sums2", [128, QT], F32)
        sumsT = [sums, sums2]
        recs = t2("recs", [128, QT], F32)
        recs2 = t2("recs2", [128, QT], F32)
        recsT = [recs, recs2]
        recsP3 = [t2("recsP3%d" % i, [65, QT], F32) for i in range(2)]
        ksq = t2("ksq", [128, 2048], BF16)
        kqc = t2("kqc", [128, 24], F32)
        biasA = t2("biasA", [128, 1], F32)
        biasB = t2("biasB", [128, 1], F32)
        cneg = t2("cneg", [1, 2], BF16)
        taur = t2("taur", [128, 1], F32)
        tau_up = t2("tau_up", [128, 1], F32)
        tau_dn = t2("tau_dn", [128, 1], F32)
        twk = [t2("twk%d" % i, [128, 1024], F32) for i in range(3)]
        TWS = None
        tabs = t2("tabs", [128, 1024], F32)
        tmf = t2("tmf", [128, 1024], BF16)
        tout = [t2("tout%d" % i, [128, 1024], BF16) for i in range(2)]
        tacc = t2("tacc", [128, 64], F32)
        wot = t2("wot", [128, NDC * D], BF16)

        slots = [PS[0], PS[1], PS[2]]

        def tern_src(kind, k):
            if kind in ('u1', 'u2'):
                dc, q4 = k // 4, k % 4
                return wupT_d[dc * 128:(dc + 1) * 128, q4 * 1024:(q4 + 1) * 1024], 1024
            return wdnT_d[k * 128:(k + 1) * 128, :], 1024

        @block.sync
        def _(sy):
            for j in range(NPAIR):
                if j == 0:
                    sy.wait_ge(S_['eo'], 16 * KO[NGA])
                    sy.wait_ge(S_['vo'], 16 * VO[NGA])
                if j >= 2:
                    sy.wait_ge(S_['av'], NQT * NCP * (j - 1))
                p2s = S_['p2'] if j % 2 == 0 else S_['p2b']
                sy.dma_start(Kb[j % 2][:, :], KT_d[j * 128:(j + 1) * 128, :]
                             ).then_inc(p2s, 16)
                sy.dma_start(Qb[j % 2][:, :], QT_d[j * 128:(j + 1) * 128, :]
                             ).then_inc(p2s, 16)
                for i in range(8):
                    sy.dma_start(Vb[j % 2][:, i * 1024:(i + 1) * 1024]
                                 .rearrange("p (c e) -> p c e", c=8),
                                 V_d[i * 1024:(i + 1) * 1024, j * 128:(j + 1) * 128]
                                 .rearrange("(c p) e -> p c e", p=128)).then_inc(p2s, 16)

        @block.gpsimd
        def _(gp_):
            gp_.wait_ge(S_['eo'], 16 * KO[NGA])
            gp_.wait_ge(S_['vo'], 16 * VO[NGA])
            gp_.wait_ge(S_['ev'], UB[NGA])
            gp_.memset(recs[:, :], 0.0)
            gp_.memset(recs2[:, :], 0.0)
            gp_.memset(recsP3[0][:, :], 0.0)
            gp_.memset(recsP3[1][:, :], 0.0)
            gp_.memset(tacc[:, :], 0.0).then_inc(S_['tv'], 1)
            gp_.dma_start(wot[:, :].rearrange("p (c e) -> p c e", c=NDC), wo_d.rearrange("(c p) e -> p c e", c=NDC)
                          ).then_inc(S_['woc'], 16)
            def tern_out(gp_, i, kind, k):
                gp_.wait_ge(S_['tv'], 1 + i + 1)
                if kind == 'u2':
                    dc, q4 = k // 4, k % 4
                    r0 = dc * NFC + q4 * 8
                    dst = WUPt_d[r0:r0 + 8, :]
                else:
                    r0 = k * NDC
                    dst = WDNt_d[r0:r0 + 8, :]
                gp_.dma_start(dst.rearrange("f (p e) -> p f e", p=128),
                              tout[i % 2][:, 0:1024].rearrange("p (f e) -> p f e", f=8)
                              ).then_inc(S_['to'], 16)

            pend_out = [None]
            TWS = [S_['tw'], S_['tw1'], S_['tw2']]
            for i, (kind, k) in enumerate(TERN_UNITS):
                if i >= 3:
                    gp_.wait_ge(S_['tv'], 1 + i - 2)
                src, w = tern_src(kind, k)
                gp_.dma_start(twk[i % 3][:, 0:w], src).then_inc(TWS[i % 3], 16)
                if pend_out[0] is not None:
                    tern_out(gp_, *pend_out[0])
                    pend_out[0] = None
                if kind in ('u2', 'd2'):
                    pend_out[0] = (i, kind, k)
            tern_out(gp_, *pend_out[0])
            pend_out[0] = None

        def _avblock(t, j, qg, prev, u, first, drain=False, need_cb=False):
            t.wait_ge(S_['ex'], u if drain else u - 2)
            if first:
                t.wait_ge(S_['tl'], 2 * qg)
                if need_cb:
                    t.wait_ge(S_['cb'], j + 1)
            mm = None
            for ci in (0, 1):
                c = 2 * prev + ci
                fst = first and ci == 0
                lst = drain and ci == 1
                t.matmul(p_nv[0:64, :], Vb[j % 2][:, c * 128: c * 128 + 64],
                         ptl[0][prev % 2][:, ci * 512:(ci + 1) * 512],
                         start=fst, stop=lst, skip_group_check=True, tile_position=(0, 0))
                t.matmul(p_nv[64:128, :], Vb[j % 2][:, c * 128 + 64: c * 128 + 128],
                         ptl[1][prev % 2][:, ci * 512:(ci + 1) * 512],
                         start=fst, stop=lst, skip_group_check=True, tile_position=(0, 64))
                t.matmul(p_sum[0:1, :], ones_b[:, :],
                         ptl[0][prev % 2][:, ci * 512:(ci + 1) * 512],
                         start=fst, stop=lst, skip_group_check=True, tile_position=(0, 0))
                mm = t.matmul(p_sum[64:65, :], ones_b[:, :],
                              ptl[1][prev % 2][:, ci * 512:(ci + 1) * 512],
                              start=fst, stop=lst, skip_group_check=True, tile_position=(0, 64))
            mm.then_inc(S_['av'], 1)

        @block.tensor
        def _(t):
            u = 0
            t.wait_ge(S_['tv'], 1)
            for j in range(NPAIR):
                jq0 = j * NQT
                t.wait_ge(S_['p2'] if j % 2 == 0 else S_['p2b'], 160 * (j // 2 + 1))
                t.wait_ge(S_['tl'], 2 * jq0)
                for nm, pj in (('up', 2), ('dn', 4)):
                    if j == pj:
                        e = 0 if nm == 'up' else 1
                        t.wait_ge(S_['tau'], 5 * e + 1)
                        t.matmul(p_nv[0:1, 0:1], ones_f[:, :], taur[:, :],
                                 start=True, stop=True,
                                 skip_group_check=True).then_inc(S_['tau'], 1)
                        t.wait_ge(S_['tau'], 5 * e + 3)
                        t.matmul(p_nv[:, 2:3], onesr_f[:, :], kqc[0:1, 22:23],
                                 start=True, stop=True,
                                 skip_group_check=True).then_inc(S_['tau'], 1)
                        t.wait_ge(S_['tau'], 5 * e + 5)
                slK = slots[(128 * j + 2) % 3]
                fbK = [p_nv, p_sum, slK[:, 0:512], slK[:, 512:1024]]
                for f in range(20):
                    if f < 16:
                        qb, cg = f // 4, f % 4
                        t.wait_ge(S_['ksq'], 5 * j + qb + 1)
                        srcq = ksq
                    else:
                        cg = f - 16
                        t.wait_ge(S_['ksq'], 5 * j + 5)
                        srcq = ksq
                    if f >= 4:
                        t.wait_ge(S_['knr'], 11 * j + (f - 4) // 2 + 1)
                    bank = fbK[f % 4]
                    t.matmul(bank[0:64, :], selk[0:64, :], srcq[0:64, cg * 512:(cg + 1) * 512],
                             start=True, stop=True, skip_group_check=True, tile_position=(0, 0))
                    t.matmul(bank[64:128, :], selk[64:128, :],
                             srcq[64:128, cg * 512:(cg + 1) * 512],
                             start=True, stop=True, skip_group_check=True,
                             tile_position=(64, 64)).then_inc(S_['kn'], 1)
                t.wait_ge(S_['knr'], 11 * j + 11)
                t.matmul(p_nv[:, 0:1], onesr_b[:, :], cneg[0:1, 0:1],
                         start=True, stop=True, skip_group_check=True)
                t.matmul(p_nv[:, 1:2], onesr_b[:, :], cneg[0:1, 1:2],
                         start=True, stop=True, skip_group_check=True).then_inc(S_['kn'], 1)
                t.wait_ge(S_['knr'], 11 * j + 10)
                for qtr in range(NQT):
                    qg = jq0 + qtr
                    qc = qtr * QT
                    for cp in range(NCP):
                        for h_ in (0, 1):
                            slot = slots[(u + h_) % 3]
                            if u + h_ >= 3:
                                t.wait_ge(S_['ex'], u + h_ - 2)
                            base = 64 * h_
                            mm = None
                            for ci in (0, 1):
                                c = 2 * cp + ci
                                mm = t.matmul(slot[:, ci * 512:(ci + 1) * 512],
                                              Kb[j % 2][base:base + 64, c * 128:(c + 1) * 128],
                                              Qb[j % 2][base:base + 64, qc:qc + QT],
                                              start=True, stop=True, skip_group_check=True,
                                              tile_position=(base, 0))
                            mm.then_inc(S_['sc'], 1)
                        u += 2
                        if cp >= 1:
                            _avblock(t, j, qg, cp - 1, u, first=(cp == 1),
                                     need_cb=(qtr == 0))
                    _avblock(t, j, qg, NCP - 1, u, first=False, drain=True)

        def recs_store(eng, qs):
            js, qcs = qs // NQT, (qs % NQT) * QT
            eng.wait_ge(S_['bc'], qs + 1)
            eng.dma_start(recs_d[0:2, js * QOWN + qcs: js * QOWN + qcs + QT],
                          recsT[qs % 2][0:65:64, 0:QT]).then_inc(S_['rc'], 16)

        @block.scalar
        def _(sc):
            u = 0
            for j in range(NPAIR):
                sc.wait_ge(S_['p2'] if j % 2 == 0 else S_['p2b'], 160 * (j // 2 + 1))
                for qb in range(4):
                    if qb >= 1:
                        sc.wait_ge(S_['kn'], 21 * j + 4 * qb)
                    elif j >= 1:
                        sc.wait_ge(S_['kn'], 21 * (j - 1) + 20)
                    sc.activation(ksq[:, :], Kb[j % 2][:, qb * 2048:(qb + 1) * 2048],
                                  AF.Square).then_inc(S_['ksq'], 1)
                sc.wait_ge(S_['kn'], 21 * j + 16)
                sc.activation(ksq[:, :], Qb[j % 2][:, :],
                              AF.Square).then_inc(S_['ksq'], 1)
                sc.wait_ge(S_['cb'], j + 1)
                for qtr in range(NQT):
                    qg = j * NQT + qtr
                    for cp in range(NCP):
                        if cp == 12 and qg >= 1:
                            recs_store(sc, qg - 1)
                        for h_ in (0, 1):
                            sc.wait_ge(S_['sc'], u + 1)
                            guard = qg * NCP + cp - 1
                            if guard > 0:
                                sc.wait_ge(S_['av'], guard)
                            bias = biasA if h_ == 0 else biasB
                            sc.activation(ptl[h_][cp % 2][:, :], slots[u % 3][:, :], AF.Exp,
                                          bias=bias[:, 0:1], scale=1.0).then_inc(S_['ex'], 1)
                            u += 1



        @block.vector
        def _(v):
            pend_rec = [None]

            def emit_recs(v):
                if pend_rec[0] is None:
                    return
                qg = pend_rec[0]
                pend_rec[0] = None
                if qg >= 2:
                    v.wait_ge(S_['rc'], 16 * (qg - 1))
                v.reciprocal(recsT[qg % 2][0:1, :], sumsT[qg % 2][0:1, :])
                v.reciprocal(recsT[qg % 2][64:65, :], sumsT[qg % 2][64:65, :])
                padV(v, 'bc')
            for j in range(NPAIR):
                jq0 = j * NQT
                for nm, pj in (('up', 2), ('dn', 4)):
                    if j == pj:
                        e = 0 if nm == 'up' else 1
                        cols = (0, 32) if nm == 'up' else (32, 64)
                        v.wait_ge(S_['tv'], 1 + (32 if nm == 'up' else 64))
                        v.reduce_sum(taur[:, 0:1], tacc[:, cols[0]:cols[1]],
                                     axis=mybir.AxisListType.X)
                        padV(v, 'tau')
                        v.wait_ge(S_['tau'], 5 * e + 2)
                        v.tensor_single_scalar(kqc[0:1, 22:23], p_nv[0:1, 0:1], TAUF, ALU.mult)
                        padV(v, 'tau')
                        v.wait_ge(S_['tau'], 5 * e + 4)
                        dst = tau_up if nm == 'up' else tau_dn
                        v.tensor_copy(dst[:, :], p_nv[:, 2:3])
                        padV(v, 'tau')
                slK = slots[(128 * j + 2) % 3]
                fbK = [p_nv, p_sum, slK[:, 0:512], slK[:, 512:1024]]
                v.wait_ge(S_['p2'] if j % 2 == 0 else S_['p2b'], 160 * (j // 2 + 1))
                fb2 = [PS[3][:, 0:1024], slK[:, 0:1024]]
                for r in range(10):
                    v.wait_ge(S_['kn'], 21 * j + 2 * r + 2)
                    v.reduce_max(kqc[:, r:r + 1], fb2[r % 2][:, :],
                                 axis=mybir.AxisListType.X).then_inc(S_['knr'], 1)
                v.wait_ge(S_['knr'], 11 * j + 10)
                v.reduce_max(kqc[:, 20:21], kqc[:, 0:8], axis=mybir.AxisListType.X)
                v.reduce_max(kqc[:, 21:22], kqc[:, 8:10], axis=mybir.AxisListType.X)
                v.tensor_tensor(kqc[:, 22:23], kqc[:, 20:21], kqc[:, 21:22], ALU.add)
                v.tensor_single_scalar(kqc[:, 22:23], kqc[:, 22:23], 0.5, ALU.mult)
                v.tensor_single_scalar(cneg[0:1, 0:1], kqc[0:1, 22:23], -1.0, ALU.mult)
                v.tensor_single_scalar(cneg[0:1, 1:2], kqc[64:65, 22:23], -1.0, ALU.mult)
                padV(v, 'knr')
                if j >= 1:
                    v.wait_ge(S_['ex'], 256 * j)
                v.wait_ge(S_['kn'], 21 * j + 21)
                v.tensor_copy(biasA[:, :], p_nv[:, 0:1])
                v.tensor_copy(biasB[:, :], p_nv[:, 1:2])
                padV(v, 'cb')
                emit_recs(v)
                i0 = sum(len(TERN_SCHED[q]) for q in range(j))
                tu = TERN_SCHED[j]
                for qtr in range(NQT):
                    qg = jq0 + qtr
                    qc = qtr * QT
                    emit_recs(v)
                    for ii in range(4 * qtr, 4 * qtr + 4):
                        if ii >= len(tu):
                            break
                        kind, k = tu[ii]
                        i = i0 + ii
                        tws = [S_['tw'], S_['tw1'], S_['tw2']][i % 3]
                        v.wait_ge(tws, 16 * (i // 3 + 1))
                        w = 1024
                        if kind in ('u1', 'd1'):
                            col = k if kind == 'u1' else 32 + k
                            v.reduce_sum(tacc[:, col:col + 1], twk[i % 3][:, 0:w],
                                         axis=mybir.AxisListType.X,
                                         apply_absolute_value=True).then_inc(S_['tv'], 1)
                        else:
                            tau_t = tau_up if kind == 'u2' else tau_dn
                            if TERN_OUT_BEFORE[i] > 0:
                                v.wait_ge(S_['to'], 16 * TERN_OUT_BEFORE[i])
                            v.scalar_tensor_tensor(tabs[:, 0:w], twk[i % 3][:, 0:w], -1.0,
                                                   twk[i % 3][:, 0:w], ALU.mult, ALU.max)
                            v.tensor_single_scalar(tabs[:, 0:w], tabs[:, 0:w],
                                                   tau_t[:, 0:1], ALU.subtract)
                            v.tensor_single_scalar(tmf[:, 0:w], tabs[:, 0:w], 0.0, ALU.is_gt)
                            v.tensor_single_scalar(tabs[:, 0:w], twk[i % 3][:, 0:w], 0.0,
                                                   ALU.is_gt)
                            v.tensor_scalar(tabs[:, 0:w], tabs[:, 0:w], 2.0, -1.0,
                                            ALU.mult, ALU.add)
                            v.tensor_tensor(tout[i % 2][:, 0:w], tabs[:, 0:w], tmf[:, 0:w],
                                            ALU.mult).then_inc(S_['tv'], 1)
                    v.wait_ge(S_['av'], (qg + 1) * NCP)
                    v.tensor_copy(sumsT[qg % 2][:, :], p_sum).then_inc(S_['tl'], 1)
                    v.tensor_copy(NV[:, j * QOWN + qc: j * QOWN + qc + QT],
                                  p_nv).then_inc(S_['tl'], 1)
                    pend_rec[0] = qg

        @block.vector
        def _(v):
            qg = NPAIR * NQT - 1
            v.wait_ge(S_['rc'], 16 * (qg - 1))
            v.reciprocal(recsT[qg % 2][0:1, :], sumsT[qg % 2][0:1, :])
            v.reciprocal(recsT[qg % 2][64:65, :], sumsT[qg % 2][64:65, :])
            padV(v, 'bc')

        if rank == 2:
            @block.sync
            def _(sy):
                sy.wait_ge(S_['tl'], 2 * NPAIR * NQT)
                sy.wait_ge(S_['to'], 16 * N_TERN_OUT)
                sy.dma_start(dbg4, NV[:, :]).then_inc(S_['out'], 16)
                sy.dma_start(dbg5[:, 0:1], tau_up[:, :], allow_slow_non_contiguous=True).then_inc(S_['out'], 16)
                sy.dma_start(dbg5[:, 1:2], tau_dn[:, :], allow_slow_non_contiguous=True).then_inc(S_['out'], 16)
                sy.dma_start(dbg5[:, 16:80], tacc[:, :]).then_inc(S_['out'], 16)
                sy.dma_start(dbg3[:, 0:1024], tout[0][:, 0:1024]).then_inc(S_['out'], 16)
                sy.dma_start(dbg1.rearrange("p (a q) -> p a q", a=64),
                             WUPt_d[0:64, :].rearrange("a (p q) -> p a q", p=128)
                             ).then_inc(S_['out'], 16)
                sy.dma_start(dbg5[:, 2048:3072], twk[0][:, :]).then_inc(S_['out'], 16)
                sy.dma_start(dbg5[:, 4096:6144],
                             wupT_d[0:128, 0:2048]).then_inc(S_['out'], 16)
                sy.wait_ge(S_['out'], 128)

        p3slots = [PS[0][:, 0:512], PS[0][:, 512:1024], PS[1][:, 0:512], PS[1][:, 512:1024]]

        if rank >= 3:

            nb3 = [PS[2][:, 0:512], PS[2][:, 512:1024], PS[3][:, 0:512], PS[3][:, 512:1024]]

            @block.sync
            def _(sy):
                sy.wait_ge(S_['rc'], 16 * (NPAIR * NQT - 1))
                for qg in range(4):
                    for jj in range(NPAIR):
                        bi = qg * NPAIR + jj
                        if jj == NPAIR - 1 and qg == 3:
                            continue
                        if bi >= 2:
                            sy.wait_ge(S_['nn'], bi - 1)
                        sy.dma_start(recsP3[bi % 2][0:65:64, :],
                                     recs_d[0:2, jj * QOWN + qg * QT: jj * QOWN + (qg + 1) * QT]
                                     ).then_inc(S_['rl'], 16)

            @block.tensor
            def _(t):
                t.wait_ge(S_['tl'], 2 * NPAIR * NQT)
                t.wait_ge(S_['woc'], 16)
                def p3bcast(t, qg2, jj):
                    bi = qg2 * NPAIR + jj
                    if bi >= 4:
                        t.wait_ge(S_['p3n'], bi - 3)
                    if bi == NPAIR * 4 - 1:
                        t.wait_ge(S_['bc'], NPAIR * NQT)
                        src_ = recsT[(NPAIR * 4 - 1) % 2][0:65, :]
                    else:
                        t.wait_ge(S_['rl'], 16 * min(bi + 2, NPAIR * 4 - 1))
                        src_ = recsP3[bi % 2][0:65, :]
                    t.matmul(nb3[bi % 4], sel[0:65, :], src_,
                             start=True, stop=True,
                             skip_group_check=True).then_inc(S_['nn'], 1)
                for jj in range(NPAIR):
                    p3bcast(t, 0, jj)
                for qg in range(4):
                    for dc in range(NDC):
                        idx = qg * NDC + dc
                        if idx >= 4:
                            t.wait_ge(S_['p3e'], idx - 3)
                        t.wait_ge(S_['p3n'], NPAIR * (qg + 1))
                        mm = None
                        for hc in range(NDC):
                            mm = t.matmul(p3slots[idx % 4],
                                          wot[:, hc * D + dc * 128: hc * D + (dc + 1) * 128],
                                          NV[:, hc * QOWN + qg * 512: hc * QOWN + (qg + 1) * 512],
                                          start=(hc == 0), stop=(hc == NDC - 1),
                                          skip_group_check=True)
                        mm.then_inc(S_['p3'], 1)
                        if qg < 3:
                            p3bcast(t, qg + 1, dc)

            @block.vector
            def _(v):
                def p3tt(v, qg2, jj):
                    bi = qg2 * NPAIR + jj
                    v.wait_ge(S_['nn'], bi + 1)
                    v.tensor_tensor(NV[:, jj * QOWN + qg2 * 512: jj * QOWN + (qg2 + 1) * 512],
                                    NV[:, jj * QOWN + qg2 * 512: jj * QOWN + (qg2 + 1) * 512],
                                    nb3[bi % 4], ALU.mult).then_inc(S_['p3n'], 1)
                for jj in range(NPAIR):
                    p3tt(v, 0, jj)
                for qg in range(4):
                    for dc in range(NDC):
                        idx = qg * NDC + dc
                        v.wait_ge(S_['p3'], idx + 1)
                        v.tensor_scalar(ST[:, dc * QOWN + qg * 512: dc * QOWN + (qg + 1) * 512],
                                        p3slots[idx % 4], bout_c[:, dc:dc + 1], None,
                                        ALU.add).then_inc(S_['p3e'], 1)
                        if qg < 3:
                            p3tt(v, qg + 1, dc)

    if rank == 3:
        @block.sync
        def _(sy):
            sy.wait_ge(S_['p3e'], 32)
            sy.dma_start(dbg4, ST[:, :]).then_inc(S_['out'], 16)
            sy.wait_ge(S_['out'], 16)
    if rank < 4:
        ctx.close()
        return nc
    with ExitStack() as sc4:
        t4 = lambda *a: sc4.enter_context(nc.sbuf_tensor(*a))
        xblk4s = t4("xblk4", [128, NDC * TB], F32)
        xblk4 = [xblk4s, xblk4s]
        xbf4 = t4("xbf4", [128, NDC * TB], BF16)
        xsq4 = t4("xsq4", [128, NDC * TB], BF16)
        rowA4 = t4("rowA4", [1, 6 * TB], F32)
        rowB4 = t4("rowB4", [1, 4 * TB], F32)
        rowC4 = t4("rowC4", [1, 4 * TB], BF16)
        HF = t4("HF", [128, NDC * QOWN], BF16)
        mid = t4("mid", [128, NFC * 512], BF16)
        Wub = [t4("Wub%d" % i, [128, NDC * 128], BF16) for i in range(4)]
        Wdb = [t4("Wdb%d" % i, [128, NFC * 128], BF16) for i in range(3)]
        WD3 = [3, 3, 2]
        a3s = t4("a3s", [128, 512], F32)
        gtile = t4("gtile", [128, 512], F32)
        gsil = t4("gsil", [128, 512], BF16)
        xfin = [t4("xfin%d" % i, [128, 512], F32) for i in range(2)]
        a3hist = None
        yout = [t4("yout", [128, 512], F32)]
        yout = [yout[0], yout[0]]
        hsc4 = t4("hsc4", [128, TB], BF16)
        env4 = dict(xblk=xblk4, xbf=xbf4, xsq=xsq4, rowA=rowA4, rowB=rowB4, rowC=rowC4,
                    hdst=HF, hT=None, hsc=hsc4, xd4base=GP4)

        NTB4 = 4

        def wo_count(tb, item, i):
            base = tb * (NFC + 5 * NDC)
            if item == 'wub':
                return 16 * (base + i + 1)
            if item == 'wdb':
                return 16 * (base + NFC + 5 * i + 4)
            return 16 * (base + NFC + 5 * i + 5)

        @block.sync
        def _(sy):
            sy.wait_ge(S_['p3e'], 32)
            for g in range(GP4, NG):
                off = (g - GP4) * TB
                for cc in range(NDC):
                    if g >= GP4 + 1:
                        sy.wait_ge(S_['hh'], 8 * (g - 1) + cc + 1)
                    sy.dma_start(xblk4[g % 2][:, cc * TB:(cc + 1) * TB],
                                 xTo[cc * 128:(cc + 1) * 128, off:off + TB]
                                 ).then_inc(S_['xd4'], 16)
            if not p4_full:
                sy.wait_ge(S_['hh'], 8 * NG)
                sy.dma_start(dbg4, ST[:, :]).then_inc(S_['out'], 16)
                sy.wait_ge(S_['out'], 16)
            for tb in range(NTB4 if p4_full else 0):
                if tb == 1 and rank == 5:
                    sy.wait_ge(S_['fa4'], 32)
                    sy.dma_start(dbgM, mid[:, :]).then_inc(S_['dmp'], 16)
                    sy.wait_ge(S_['dmp'], 16)
                for fc in range(NFC):
                    if fc >= 4:
                        sy.wait_ge(S_['f4'], tb * F4_PER + fc - 3)
                    elif tb >= 1:
                        sy.wait_ge(S_['f4'], (tb - 1) * F4_PER + 29 + fc)
                    if tb == 0 and fc == 0:
                        sy.wait_ge(S_['to'], 16 * N_TERN_OUT)
                    sy.dma_start(Wub[fc % 4][:, :].rearrange("p (c e) -> p c e", c=NDC),
                                 WUPt_d[fc::NFC, :].rearrange("c (p e) -> p c e", p=128)
                                 ).then_inc(S_['wu%d' % (fc % 4)], 16)
                for dc in range(NDC):
                    if dc >= 3:
                        sy.wait_ge(S_['f4'], tb * F4_PER + 31 + dc)
                    elif tb >= 1:
                        sy.wait_ge(S_['f4'], (tb - 1) * F4_PER + 39 + dc)
                    for k4 in range(4):
                        sy.dma_start(Wdb[dc % 3][:, k4 * 1024:(k4 + 1) * 1024]
                                     .rearrange("p (c e) -> p c e", c=8),
                                     WDNt_d[dc::NDC, :][k4 * 8:(k4 + 1) * 8, :]
                                     .rearrange("c (p e) -> p c e", p=128)
                                     ).then_inc(S_['wd%d' % (dc % 3)], 16)
                    if tb * NDC + dc - 2 >= 0:
                        sy.wait_ge(S_['fy'], tb * NDC + dc - 1)
                    sy.dma_start(xfin[dc % 2][:, :],
                                 xTo[dc * 128:(dc + 1) * 128, tb * 512:(tb + 1) * 512]
                                 ).then_inc(S_['wx'], 16)
            if p4_full:
                sy.wait_ge(S_['out'], 16 * NTB4 * NDC)
            if rank == 5:
                sy.dma_start(dbg1[:, 0:1024].rearrange("p (a q) -> p a q", a=8),
                             WUPt_d[0:8, :].rearrange("a (p q) -> p a q", p=128)
                             ).then_inc(S_['out'], 16)
                sy.dma_start(dbg2[:, 0:1024].rearrange("p (a q) -> p a q", a=8),
                             WDNt_d[0:8, :].rearrange("a (p q) -> p a q", p=128)
                             ).then_inc(S_['out'], 16)
                sy.dma_start(dbg4, ST[:, :]).then_inc(S_['out'], 16)
                sy.dma_start(dbg5[:, 0:512], a3s[:, :]).then_inc(S_['out'], 16)

                sy.dma_start(dbg5[0:1, 1024:1536], rowA4[0:1, 512:1024],
                             allow_slow_non_contiguous=True).then_inc(S_['out'], 16)
                sy.dma_start(dbg5[0:1, 1536:2048], rowA4[0:1, 1024:1536],
                             allow_slow_non_contiguous=True).then_inc(S_['out'], 16)
                sy.dma_start(dbg5[:, 512:1024], gtile[:, :]).then_inc(S_['out'], 16)
                sy.dma_start(dbg3[:, 1024:1536], mid[:, 0:512]).then_inc(S_['out'], 16)
                sy.dma_start(dbgH, HF[:, :]).then_inc(S_['out'], 16)
                sy.wait_ge(S_['out'], 16 * (NTB4 * NDC + 9))

        def mid_sq_view(fc):
            return xsq4[:, (fc % NDC) * 512:(fc % NDC) * 512 + 512]

        def _summ(t, tb, fc):
            t.wait_ge(S_['fe4'], tb * FE4_PER + fc + 1)
            t.matmul(PS[3][0:1, 0:512], ones_b[:, :], mid_sq_view(fc),
                     start=(fc == 0), stop=(fc == NFC - 1),
                     skip_group_check=True).then_inc(S_['fs'], 1)

        @block.gpsimd
        def _(gp_):
            for tb in range(NTB4 if p4_full else 0):
                for dc in range(NDC):
                    gp_.wait_ge(S_['fy'], tb * NDC + dc + 1)
                    gp_.dma_start(yT[dc * 128:(dc + 1) * 128, tb * 512:(tb + 1) * 512],
                                  yout[0][:, :]).then_inc(S_['out'], 16)

        @block.tensor
        def _(t):
            for g in range(GP4, NG):
                if g > GP4:
                    stats_pe_work(t, env4, g - 1, 'p4')
                stats_pe_stats(t, env4, g)
            stats_pe_work(t, env4, NG - 1, 'p4')
            for tb in range(NTB4 if p4_full else 0):
                for fc in range(NFC):
                    t.wait_ge(S_['wu%d' % (fc % 4)], 16 * (8 * tb + fc // 4 + 1))
                    if tb == 0 and fc == 0:
                        t.wait_ge(S_['hh'], 8 * NG)
                        t.wait_ge(S_['sa'], 3 * (NG - 1) + 1)
                    if fc >= 3:
                        t.wait_ge(S_['fa4'], tb * FA4_PER + fc - 2)
                    elif tb >= 1:
                        t.wait_ge(S_['fe4'], tb * FE4_PER)
                    mm = None
                    for dc in range(NDC):
                        mm = t.matmul(PS[fc % 3][:, 0:512],
                                      Wub[fc % 4][:, dc * 128:(dc + 1) * 128],
                                      HF[:, dc * QOWN + tb * 512: dc * QOWN + tb * 512 + 512],
                                      start=(dc == 0), stop=(dc == NDC - 1),
                                      skip_group_check=True)
                    mm.then_inc(S_['f4'], 1)
                    if fc >= 2:
                        _summ(t, tb, fc - 2)
                _summ(t, tb, NFC - 2)
                _summ(t, tb, NFC - 1)
                t.wait_ge(S_['fe4'], tb * FE4_PER + 33)
                t.matmul(PS[3][:, 512:1024], onesr_b[:, :], rowC4[0:1, 0:512],
                         start=True, stop=False, skip_group_check=True)
                t.matmul(PS[3][:, 512:1024], onesr_b[:, :], rowC4[0:1, 512:1024],
                         start=False, stop=True, skip_group_check=True).then_inc(S_['f4'], 1)
                for dc in range(NDC):
                    t.wait_ge(S_['fa4'], tb * FA4_PER + 32)
                    if dc == 0:
                        t.wait_ge(S_['fe4'], tb * FE4_PER + 32)
                    t.wait_ge(S_['wd%d' % (dc % 3)],
                              16 * 4 * (WD3[dc % 3] * tb + dc // 3 + 1))
                    if dc >= 2:
                        t.wait_ge(S_['fe4'], tb * FE4_PER + 35 + dc - 2)
                    mm = None
                    for fc in range(NFC):
                        mm = t.matmul(PS[dc % 2][:, 0:512],
                                      Wdb[dc % 3][:, fc * 128:(fc + 1) * 128],
                                      mid[:, fc * 512:(fc + 1) * 512],
                                      start=(fc == 0), stop=(fc == NFC - 1),
                                      skip_group_check=True)
                    mm.then_inc(S_['f4'], 1)

        @block.scalar
        def _(sc):
            for g in range(GP4, NG):
                stats_act_rows1(sc, env4, g)
                stats_act_rows2(sc, env4, g, 'p4')
                stats_act_rows3(sc, env4, g, 'p4')
            for tb in range(NTB4 if p4_full else 0):
                for fc in range(NFC):
                    sc.wait_ge(S_['f4'], tb * F4_PER + fc + 1)
                    sc.activation(mid[:, fc * 512:(fc + 1) * 512], PS[fc % 3][:, 0:512],
                                  AF.Gelu).then_inc(S_['fa4'], 1)
                sc.wait_ge(S_['fs'], (tb + 1) * FS_PER)
                sc.activation(rowA4[0:1, 0:512], PS[3][0:1, 0:512], AF.Ln, scale=1.0 / FF)
                sc.activation(rowA4[0:1, 1024:1536], rowA4[0:1, 0:512], AF.Exp, scale=0.5)
                sc.activation(rowA4[0:1, 1536:2048], rowA4[0:1, 1024:1536], AF.Identity,
                              bias=eps_t[0:1, 0:1])
                sc.activation(rowA4[0:1, 0:512], rowA4[0:1, 1536:2048], AF.Ln)
                sc.activation(rowA4[0:1, 1024:1536], rowA4[0:1, 0:512], AF.Exp,
                              scale=-1.0)
                padA(sc, 'fa4')
                for dc in range(NDC):
                    sc.wait_ge(S_['fe4'], tb * FE4_PER + 35 + dc)
                    sc.activation(gsil[:, :], gtile[:, :], AF.Silu)
                    padA(sc, 'fa4')

        @block.vector
        def _(v):
            stats_dve_pre(v, env4, GP4)
            for g in range(GP4, NG):
                stats_dve_rows1(v, env4, g)
                stats_dve_rows2(v, env4, g)
                stats_dve_rows3(v, env4, g, 'p4')
                stats_dve_rows4(v, env4, g)
                stats_dve_h(v, env4, g, 'p4')
                if g + 1 < NG:
                    stats_dve_pre(v, env4, g + 1)
            for tb in range(NTB4 if p4_full else 0):
                for fc in range(NFC):
                    v.wait_ge(S_['fa4'], tb * FA4_PER + fc + 1)
                    if fc >= NDC:
                        v.wait_ge(S_['fs'], tb * FS_PER + fc - NDC + 1)
                    v.tensor_tensor(mid_sq_view(fc), mid[:, fc * 512:(fc + 1) * 512],
                                    mid[:, fc * 512:(fc + 1) * 512],
                                    ALU.mult).then_inc(S_['fe4'], 1)
                v.wait_ge(S_['fa4'], tb * FA4_PER + 33)
                v.tensor_copy(rowC4[0:1, 0:512], rowA4[0:1, 1024:1536])
                v.tensor_tensor(rowC4[0:1, 512:1024], rowA4[0:1, 1024:1536], rowC4[0:1, 0:512],
                                ALU.subtract)
                padV(v, 'fe4')
                v.wait_ge(S_['f4'], tb * F4_PER + 33)

                v.tensor_copy(a3s[:, :], PS[3][:, 512:1024]).then_inc(S_['fe4'], 1)
                for dc in range(NDC):
                    v.wait_ge(S_['f4'], tb * F4_PER + 33 + dc + 1)
                    v.tensor_tensor(gtile[:, :], PS[dc % 2][:, 0:512], a3s[:, :],
                                    ALU.mult)
                    padV(v, 'fe4')
                    v.wait_ge(S_['fa4'], tb * FA4_PER + 34 + dc)
                    v.wait_ge(S_['wx'], 16 * (tb * NDC + dc + 1))
                    yi = tb * NDC + dc
                    if yi >= 1:
                        v.wait_ge(S_['out'], 16 * yi)
                    v.tensor_tensor(gsil[:, :],
                                    ST[:, dc * QOWN + tb * 512: dc * QOWN + (tb + 1) * 512],
                                    gsil[:, :], ALU.mult)
                    v.tensor_tensor(yout[0][:, :], xfin[dc % 2][:, :], gsil[:, :],
                                    ALU.add)
                    padV(v, 'fy')
    ctx.close()
    return nc


def kernel(L0_attn_x, L0_attn_norm_scale, L0_attn_wq, L0_attn_wk, L0_attn_wv,
           L0_attn_w_out, L0_attn_b_out, L0_ffn_w_up, L0_ffn_w_down):
    x = np.asarray(L0_attn_x, dtype=np.float32)
    wq = np.ascontiguousarray(np.asarray(L0_attn_wq, np.float32).reshape(D, D))
    wk = np.ascontiguousarray(np.asarray(L0_attn_wk, np.float32).reshape(D, D))
    wv = np.ascontiguousarray(np.asarray(L0_attn_wv, np.float32).reshape(D, D))
    wo = np.ascontiguousarray(np.asarray(L0_attn_w_out, np.float32).reshape(D, D))
    nsrow = np.ascontiguousarray(np.asarray(L0_attn_norm_scale, np.float32).reshape(1, D))
    boutc = np.ascontiguousarray(np.asarray(L0_attn_b_out, np.float32).reshape(NDC, 128).T)
    wupT = np.ascontiguousarray(np.asarray(L0_ffn_w_up, np.float32).T)
    wdnT = np.ascontiguousarray(np.asarray(L0_ffn_w_down, np.float32).T)
    xTb = [np.ascontiguousarray(x[b].T) for b in range(2)]

    nc = build_program()
    in_maps = []
    for c in range(8):
        b, qi = c // 4, c % 4
        xTr = np.ascontiguousarray(np.roll(xTb[b], -qi * QOWN, axis=1))
        xTo = np.ascontiguousarray(x[b, qi * QOWN:(qi + 1) * QOWN, :].T)
        in_maps.append({"xT": xTr, "xTo": xTo, "wq": wq, "wk": wk, "wv": wv, "wo": wo,
                        "nsr": nsrow, "boc": boutc, "wupT": wupT, "wdnT": wdnT})
    res = run_bass_kernel_spmd(nc, in_maps, core_ids=list(range(8)))
    y = np.empty((2, S, D), dtype=np.float32)
    for c in range(8):
        b, qi = c // 4, c % 4
        y[b, qi * QOWN:(qi + 1) * QOWN, :] = res.results[c]["yT"].T
    return y


# version 25
# speedup vs baseline: 1.2335x; 1.0053x over previous
import numpy as np
from contextlib import ExitStack

import concourse.bass as bass
import concourse.mybir as mybir
from concourse.bass_utils import run_bass_kernel_spmd

F32 = mybir.dt.float32
BF16 = mybir.dt.bfloat16
AF = mybir.ActivationFunctionType
ALU = mybir.AluOpType

D, S, QOWN = 1024, 8192, 2048
NDC, NPAIR, NCH, NCP, NQT, QT = 8, 8, 64, 32, 4, 512
FF, NFC, TB = 4096, 32, 512
NB1 = S // TB
NB2 = QOWN // TB
GP1B, GP4 = NB1, NB1
NG = GP4 + NB2
NGA = GP4
RT2 = 0.35355339059327373
INV_D = 1.0 / D
TAUF = 0.5 / (D * FF)

TERN_SCHED = {
    0: [('u1', k) for k in range(16)],
    1: [('u1', k) for k in range(16, 32)],
    2: [('d1', k) for k in range(16)],
    3: [('d1', k) for k in range(16, 32)],
    4: [('u2', k) for k in range(16)],
    5: [('u2', k) for k in range(16, 32)],
    6: [('d2', k) for k in range(16)],
    7: [('d2', k) for k in range(16, 32)],
}
TERN_UNITS = [u for p in range(8) for u in TERN_SCHED[p]]


def tern_out_dmas(kind):
    return 1 if kind in ('u2', 'd2') else 0


TERN_OUT_BEFORE = [0]
for _u in TERN_UNITS:
    TERN_OUT_BEFORE.append(TERN_OUT_BEFORE[-1] + tern_out_dmas(_u[0]))
N_TERN_OUT = TERN_OUT_BEFORE[-1]

F4_PER = 41
FA4_PER = 41
FE4_PER = 42
FS_PER = 32


def blk_kind(g):
    return 'p1' if g < GP1B else 'p4'


def n_units(g):
    if blk_kind(g) == 'p4':
        return 0
    return 24 if g < NB2 else 16


UB = [0] * (NG + 1)
KO = [0] * (NG + 1)
VO = [0] * (NG + 1)
for g in range(NG):
    UB[g + 1] = UB[g] + n_units(g)
    KO[g + 1] = KO[g] + (0 if blk_kind(g) == 'p4' else (16 if g < NB2 else 8))
    VO[g + 1] = VO[g] + (8 if blk_kind(g) == 'p1' else 0)


def tc128(i):
    return (i // 2) * 128


def build_program(stop='full'):
    rank = {'p1': 1, 'p2': 2, 'p3': 3, 'full': 4, 'dbg': 5, 'p4s': 6}[stop]
    p4_full = rank != 6
    nc = bass.Bass("TRN2", target_bir_lowering=False)

    xT = nc.dram_tensor("xT", [D, S], F32, kind="ExternalInput").ap()
    xTo = nc.dram_tensor("xTo", [D, QOWN], F32, kind="ExternalInput").ap()
    wq_d = nc.dram_tensor("wq", [D, D], F32, kind="ExternalInput").ap()
    wk_d = nc.dram_tensor("wk", [D, D], F32, kind="ExternalInput").ap()
    wv_d = nc.dram_tensor("wv", [D, D], F32, kind="ExternalInput").ap()
    wo_d = nc.dram_tensor("wo", [D, D], F32, kind="ExternalInput").ap()
    nsr = nc.dram_tensor("nsr", [1, D], F32, kind="ExternalInput").ap()
    boc = nc.dram_tensor("boc", [128, NDC], F32, kind="ExternalInput").ap()
    wupT_d = nc.dram_tensor("wupT", [D, FF], F32, kind="ExternalInput").ap()
    wdnT_d = nc.dram_tensor("wdnT", [FF, D], F32, kind="ExternalInput").ap()
    yT = nc.dram_tensor("yT", [D, QOWN], F32, kind="ExternalOutput").ap()
    dbg1 = nc.dram_tensor("dbg1", [128, S], BF16, kind="ExternalOutput").ap()
    dbg2 = nc.dram_tensor("dbg2", [128, S], BF16, kind="ExternalOutput").ap()
    dbg3 = nc.dram_tensor("dbg3", [128, QOWN], BF16, kind="ExternalOutput").ap()
    dbg4 = nc.dram_tensor("dbg4", [128, NPAIR * QOWN], BF16, kind="ExternalOutput").ap()
    dbg5 = nc.dram_tensor("dbg5", [128, 6144], F32, kind="ExternalOutput").ap()
    dbgH = nc.dram_tensor("dbgH", [128, 16384], BF16, kind="ExternalOutput").ap()
    dbgM = nc.dram_tensor("dbgM", [128, 16384], BF16, kind="ExternalOutput").ap()

    KT_d = nc.dram_tensor("KT_d", [D, S], BF16).ap()
    V_d = nc.dram_tensor("V_d", [S, D], BF16).ap()
    QT_d = nc.dram_tensor("QT_d", [D, QOWN], BF16).ap()
    WUPt_d = nc.dram_tensor("WUPt_d", [NDC * NFC, 128 * 128], BF16).ap()
    WDNt_d = nc.dram_tensor("WDNt_d", [NFC * NDC, 128 * 128], BF16).ap()
    recs_d = nc.dram_tensor("recs_d", [2, 8 * 2048], F32).ap()

    ctx = ExitStack()
    ten = lambda *a: ctx.enter_context(nc.sbuf_tensor(*a))
    psu = lambda *a: ctx.enter_context(nc.psum_tensor(*a))
    S_ = {}
    for n in ("ld wl woc xd xq stm sa sb ab pbc hh kv ev eo vo p2 ksq kn knr cb tau sc ex av tl "
              "bc tw tv to wo p3 p3e f4 fs fa4 fe4 fy out dmp wd wx rc rl nn p3n p2b wu0 wu1 wu2 wu3 wd0 wd1 wd2 xd4 tw1 tw2 tw3 ou1").split():
        S_[n] = ctx.enter_context(nc.semaphore("s_" + n))

    ones_b = ten("ones_b", [128, 1], BF16)
    onesr_b = ten("onesr_b", [1, 128], BF16)
    onesr_f = ten("onesr_f", [1, 128], F32)
    ones_f = ten("ones_f", [128, 1], F32)
    sel = ten("sel", [128, 128], F32)
    selk = ten("selk", [128, 64], BF16)
    nsrow_b = ten("nsrow_b", [1, D], BF16)
    nscol = ten("nscol", [128, NDC], F32)
    nsrow_f = ten("nsrow_f", [1, D], F32)
    bout_c = ten("bout_c", [128, NDC], F32)
    eps_ln = ten("eps_ln", [1, 1], F32)
    scr1 = ten("scr1", [1, 1], F32)
    scrV = ten("scrV", [1, 1], F32)
    eps_rms = ten("eps_rms", [1, 1], F32)
    eps_t = ten("eps_t", [1, 1], F32)
    eps_f = ten("eps_f", [1, 1], F32)
    ST = ten("ST", [128, NDC * QOWN], BF16)

    PS = [psu("ps%d" % i, [128, 1024], F32) for i in range(4)]
    pA = [PS[0][:, 0:512], PS[1][:, 0:512]]
    pB = [PS[0][:, 512:1024], PS[1][:, 512:1024]]
    pKV = [PS[2][:, 0:512], PS[2][:, 512:1024]]
    pS1 = PS[3][0:1, 0:512]
    pS2 = PS[3][0:1, 512:1024]
    p_nv = PS[3][:, 0:512]
    p_sum = PS[3][:, 512:1024]

    block = ctx.enter_context(nc.Block())

    def padA(sc, sem, n=1):
        sc.activation(scr1[0:1, 0:1], scr1[0:1, 0:1], AF.Identity).then_inc(S_[sem], n)

    def padV(v, sem, n=1):
        v.tensor_copy(scrV[0:1, 0:1], scrV[0:1, 0:1]).then_inc(S_[sem], n)

    def RS(env, w, i):
        return env[w][0:1, i * TB:(i + 1) * TB]

    def stats_pe_stats(t, env, g):
        t.wait_ge(S_['xq'], g + 1)
        if g >= 1:
            t.wait_ge(S_['sa'], 3 * (g - 1) + 1)
        for c in range(NDC):
            t.matmul(pS1, ones_b[:, :], env['xbf'][:, c * TB:(c + 1) * TB],
                     start=(c == 0), stop=(c == NDC - 1), skip_group_check=True)
        mm = None
        for c in range(NDC):
            mm = t.matmul(pS2, ones_b[:, :], env['xsq'][:, c * TB:(c + 1) * TB],
                          start=(c == 0), stop=(c == NDC - 1), skip_group_check=True)
        mm.then_inc(S_['stm'], 1)

    def stats_dve_pre(v, env, g):
        if 'xd4base' in env:
            v.wait_ge(S_['xd4'], 128 * (g - env['xd4base'] + 1))
        else:
            v.wait_ge(S_['xd'], 16 * (g + 1))
        v.tensor_copy(env['xbf'][:, :], env['xblk'][g % 2][:, :])
        v.tensor_tensor(env['xsq'][:, :], env['xbf'][:, :], env['xbf'][:, :],
                        ALU.mult).then_inc(S_['xq'], 1)

    def stats_dve_rows1(v, env, g, n=1):
        v.wait_ge(S_['stm'], g + 1)
        v.tensor_single_scalar(RS(env, 'rowA', 0), pS1, INV_D, ALU.mult)
        v.tensor_single_scalar(RS(env, 'rowA', 5), pS2, INV_D, ALU.mult)
        v.tensor_tensor(RS(env, 'rowB', 0), RS(env, 'rowA', 0), RS(env, 'rowA', 0), ALU.mult)
        v.tensor_tensor(RS(env, 'rowA', 1), RS(env, 'rowA', 5), RS(env, 'rowB', 0),
                        ALU.subtract)
        padV(v, 'sa', n)

    def stats_act_fused(sc, env, g):
        sc.wait_ge(S_['sa'], 3 * g + 3)
        sc.activation(RS(env, 'rowA', 2), RS(env, 'rowA', 1), AF.Ln,
                      scale=1.0 + 1e-6, bias=eps_f[0:1, 0:1])
        sc.activation(RS(env, 'rowA', 4), RS(env, 'rowA', 2), AF.Exp, scale=-0.5)
        padA(sc, 'sb', 3)

    def stats_act_rows1(sc, env, g):
        sc.wait_ge(S_['sa'], 3 * g + 1)
        sc.activation(RS(env, 'rowA', 2), RS(env, 'rowA', 1), AF.Ln, bias=eps_ln[0:1, 0:1])
        sc.activation(RS(env, 'rowB', 1), RS(env, 'rowA', 2), AF.Exp, scale=-1.0)
        padA(sc, 'sb')

    def stats_dve_rows2(v, env, g):
        v.wait_ge(S_['sb'], 3 * g + 1)
        v.tensor_tensor(RS(env, 'rowA', 3), RS(env, 'rowA', 1), RS(env, 'rowB', 1),
                        ALU.mult)
        padV(v, 'sa')

    def stats_act_rows2(sc, env, g, kind):
        sc.wait_ge(S_['sa'], 3 * g + 2)
        if kind != 'p4':
            sc.activation(RS(env, 'rowB', 0), RS(env, 'rowA', 3), AF.Ln,
                          bias=eps_rms[0:1, 0:1])
            padA(sc, 'sb')
        else:
            sc.activation(RS(env, 'rowB', 0), RS(env, 'rowA', 3), AF.Ln)
            sc.activation(RS(env, 'rowB', 1), RS(env, 'rowB', 0), AF.Exp, scale=0.5)
            sc.activation(RS(env, 'rowB', 3), RS(env, 'rowA', 2), AF.Exp, scale=0.5)
            padA(sc, 'sb')

    def stats_dve_rows3(v, env, g, kind):
        v.wait_ge(S_['sb'], 3 * g + 2)
        if kind != 'p4':
            v.tensor_tensor(RS(env, 'rowB', 0), RS(env, 'rowA', 2), RS(env, 'rowB', 0),
                            ALU.add)
            padV(v, 'sa')
        else:
            v.tensor_single_scalar(RS(env, 'rowB', 0), RS(env, 'rowB', 1), 1e-8, ALU.add)
            v.tensor_tensor(RS(env, 'rowB', 0), RS(env, 'rowB', 3), RS(env, 'rowB', 0),
                            ALU.mult)
            padV(v, 'sa')

    def stats_act_rows3(sc, env, g, kind):
        sc.wait_ge(S_['sa'], 3 * g + 3)
        if kind != 'p4':
            sc.activation(RS(env, 'rowA', 4), RS(env, 'rowB', 0), AF.Exp, scale=-0.5)
            padA(sc, 'sb')
        else:
            sc.activation(RS(env, 'rowB', 3), RS(env, 'rowB', 0), AF.Ln)
            sc.activation(RS(env, 'rowA', 4), RS(env, 'rowB', 3), AF.Exp, scale=-1.0)
            padA(sc, 'sb')

    def stats_dve_rows4(v, env, g):
        v.wait_ge(S_['sb'], 3 * g + 3)
        v.tensor_tensor(RS(env, 'rowB', 2), RS(env, 'rowA', 0), RS(env, 'rowA', 4), ALU.mult)
        v.tensor_copy(RS(env, 'rowC', 0), RS(env, 'rowA', 4))
        v.tensor_tensor(RS(env, 'rowC', 1), RS(env, 'rowA', 4), RS(env, 'rowC', 0), ALU.subtract)
        v.tensor_scalar(RS(env, 'rowC', 2), RS(env, 'rowB', 2), -1.0, None, ALU.mult)
        v.scalar_tensor_tensor(RS(env, 'rowC', 3), RS(env, 'rowB', 2), -1.0,
                               RS(env, 'rowC', 2), ALU.mult, ALU.subtract)
        padV(v, 'ab')

    def stats_pe_work_bcast(t, env, g):
        t.wait_ge(S_['ab'], g + 1)
        if g >= 2:
            t.wait_ge(S_['hh'], 8 * (g - 1))
        ls = onesr_b[0:1, 0:128]
        t.matmul(pA[g % 2], ls, RS(env, 'rowC', 0), start=True, stop=False,
                 skip_group_check=True)
        t.matmul(pA[g % 2], ls, RS(env, 'rowC', 1), start=False, stop=True,
                 skip_group_check=True)
        t.matmul(pB[g % 2], ls, RS(env, 'rowC', 2), start=True, stop=False,
                 skip_group_check=True)
        t.matmul(pB[g % 2], ls, RS(env, 'rowC', 3), start=False, stop=True,
                 skip_group_check=True).then_inc(S_['pbc'], 1)

    def stats_pe_work(t, env, g, kind, wkt=None, wvt=None, wqt=None, split=False):
        if not split:
            stats_pe_work_bcast(t, env, g)
        nu = n_units(g)
        if nu == 0:
            return
        for u in range(nu):
            if u == 1:
                t.wait_ge(S_['hh'], 8 * (g + 1))
            if UB[g] + u - 1 >= 1:
                t.wait_ge(S_['ev'], UB[g] + u - 1)
            mm = None
            if u < 8:
                for dc in range(NDC):
                    if u == 0:
                        t.wait_ge(S_['hh'], min(8 * g + dc + 5, 8 * (g + 1)))
                    mm = t.matmul(pKV[u % 2], wkt[:, dc * D + u * 128: dc * D + (u + 1) * 128],
                                  env['hT'][g % 2][:, dc * TB:(dc + 1) * TB],
                                  start=(dc == 0), stop=(dc == NDC - 1), skip_group_check=True)
            elif u < 16:
                i = u - 8
                tc, half = i // 2, i % 2
                for dc in range(NDC):
                    mm = t.matmul(pKV[u % 2],
                                  env['hT'][g % 2][:, dc * TB + tc * 128: dc * TB + (tc + 1) * 128],
                                  wvt[:, dc * D + half * 512: dc * D + (half + 1) * 512],
                                  start=(dc == 0), stop=(dc == NDC - 1), skip_group_check=True)
            else:
                uq = u - 16
                for dc in range(NDC):
                    mm = t.matmul(pKV[u % 2], wqt[:, dc * D + uq * 128: dc * D + (uq + 1) * 128],
                                  env['hT'][g % 2][:, dc * TB:(dc + 1) * TB],
                                  start=(dc == 0), stop=(dc == NDC - 1), skip_group_check=True)
            mm.then_inc(S_['kv'], 1)

    def stats_dve_h(v, env, g, kind):
        for c in range(NDC):
            if c == 0:
                v.wait_ge(S_['pbc'], g + 1)
            if kind == 'p4':
                v.tensor_tensor(env['hsc'][:, :],
                                env['xblk'][g % 2][:, c * TB:(c + 1) * TB], pA[g % 2],
                                ALU.mult)
                boff = (g - GP4) * TB
                dst = env['hdst'][:, c * QOWN + boff: c * QOWN + boff + TB]
                v.tensor_tensor(dst, env['hsc'][:, :], pB[g % 2],
                                ALU.add).then_inc(S_['hh'], 1)
            else:
                v.scalar_tensor_tensor(env['hsc'][:, :],
                                       env['xblk'][g % 2][:, c * TB:(c + 1) * TB],
                                       nscol[:, c:c + 1], pA[g % 2], ALU.mult, ALU.mult)
                dst = env['hT'][g % 2][:, c * TB:(c + 1) * TB]
                v.scalar_tensor_tensor(dst, pB[g % 2], nscol[:, c:c + 1],
                                       env['hsc'][:, :],
                                       ALU.mult, ALU.add).then_inc(S_['hh'], 1)

    def stats_act_evac(sc, env, g, evK=None, evV=None, hooks=None):
        for u in range(n_units(g)):
            if hooks and u in hooks:
                hooks[u]()
            sc.wait_ge(S_['kv'], UB[g] + u + 1)
            if 8 <= u < 16:
                i = u - 8
                if VO[g] + i - 4 >= 0:
                    sc.wait_ge(S_['vo'], 16 * (VO[g] + i - 3))
                sc.activation(evV[i % 4][:, :], pKV[u % 2],
                              AF.Identity).then_inc(S_['ev'], 1)
            else:
                ko = u if u < 8 else 8 + (u - 16)
                if KO[g] + ko - 4 >= 0:
                    sc.wait_ge(S_['eo'], 16 * (KO[g] + ko - 3))
                sc.activation(evK[ko % 4][:, :], pKV[u % 2], AF.Identity,
                              scale=RT2).then_inc(S_['ev'], 1)

    with ExitStack() as sc1:
        t1 = lambda *a: sc1.enter_context(nc.sbuf_tensor(*a))
        xblk = [t1("xblk%d" % i, [128, NDC * TB], F32) for i in range(2)]
        xbf = t1("xbf", [128, NDC * TB], BF16)
        xsq = t1("xsq", [128, NDC * TB], BF16)
        hT = [t1("hT%d" % i, [128, NDC * TB], BF16) for i in range(2)]
        evK = [t1("evK%d" % i, [128, TB], BF16) for i in range(4)]
        evV = [t1("evV%d" % i, [128, TB], BF16) for i in range(4)]
        wqt = t1("wqt", [128, NDC * D], BF16)
        wkt = t1("wkt", [128, NDC * D], BF16)
        wvt = t1("wvt", [128, NDC * D], BF16)
        rowA = t1("rowA", [1, 6 * TB], F32)
        rowB = t1("rowB", [1, 4 * TB], F32)
        rowC = t1("rowC", [1, 4 * TB], BF16)
        hsc = t1("hsc", [128, TB], BF16)
        env1 = dict(xblk=xblk, xbf=xbf, xsq=xsq, hT=hT, rowA=rowA, rowB=rowB, rowC=rowC,
                    hsc=hsc)

        def blk_src(g):
            return xT, g * TB

        @block.sync
        def _(sy):
            sy.dma_start(nsrow_f[:, :], nsr).then_inc(S_['ld'], 16)
            sy.dma_start(nscol[:, :], nsr.rearrange("a (c p) -> p (a c)", p=128),
                         allow_slow_non_contiguous=True).then_inc(S_['ld'], 16)
            sy.dma_start(bout_c[:, :], boc).then_inc(S_['ld'], 16)
            for g in range(NGA):
                src, off = blk_src(g)
                if g >= 2:
                    sy.wait_ge(S_['hh'], 8 * (g - 1))
                sy.dma_start(xblk[g % 2][:, :].rearrange("p (c t) -> p c t", c=NDC),
                             src[:, off:off + TB].rearrange("(c p) t -> p c t", c=NDC)
                             ).then_inc(S_['xd'], 16)
            if rank == 1:
                sy.wait_ge(S_['eo'], 16 * KO[NGA])
                sy.wait_ge(S_['vo'], 16 * VO[NGA])
                sy.dma_start(dbg1, KT_d[0:128, :]).then_inc(S_['out'], 16)
                sy.dma_start(dbg2.rearrange("p (a d) -> p a d", d=D),
                             V_d[0:8 * 128, :].rearrange("(a p) d -> p a d", p=128)
                             ).then_inc(S_['out'], 16)
                sy.dma_start(dbg3, QT_d[0:128, :]).then_inc(S_['out'], 16)
                sy.wait_ge(S_['out'], 48)

        @block.gpsimd
        def _(gp_):
            gp_.memset(ones_b[:, :], 1.0)
            gp_.memset(onesr_b[:, :], 1.0)
            gp_.memset(onesr_f[:, :], 1.0)
            gp_.memset(ones_f[:, :], 1.0)
            gp_.memset(sel[:, :], 0.0)
            gp_.memset(sel[0:1, 0:64], 1.0)
            gp_.memset(selk[:, :], 0.0)
            gp_.memset(selk[:, 0:1], 1.0)
            gp_.memset(sel[64:65, 64:128], 1.0)
            gp_.memset(eps_ln[:, :], 1e-5)
            gp_.memset(eps_rms[:, :], 1e-6)
            gp_.memset(eps_t[:, :], 1e-8)
            gp_.memset(eps_f[:, :], 1e-11)
            gp_.memset(scr1[:, :], 0.0)
            gp_.memset(scrV[:, :], 0.0).then_inc(S_['ld'], 1)
            gp_.dma_start(wqt[:, :].rearrange("p (c e) -> p c e", c=NDC), wq_d.rearrange("(c p) e -> p c e", c=NDC)
                          ).then_inc(S_['wl'], 16)
            gp_.dma_start(wkt[:, :].rearrange("p (c e) -> p c e", c=NDC), wk_d.rearrange("(c p) e -> p c e", c=NDC)
                          ).then_inc(S_['wl'], 16)
            gp_.dma_start(wvt[:, :].rearrange("p (c e) -> p c e", c=NDC), wv_d.rearrange("(c p) e -> p c e", c=NDC)
                          ).then_inc(S_['wl'], 16)
            for g in range(NGA):
                for u in range(8):
                    gp_.wait_ge(S_['ev'], UB[g] + u + 2)
                    gp_.dma_start(KT_d[u * 128:(u + 1) * 128, g * TB:(g + 1) * TB],
                                  evK[u % 4][:, :]).then_inc(S_['eo'], 16)
                for i in range(8):
                    gp_.wait_ge(S_['ev'], UB[g] + 8 + i + 2)
                    gp_.dma_start(V_d[g * TB + tc128(i): g * TB + tc128(i) + 128,
                                      (i % 2) * 512:((i % 2) + 1) * 512],
                                  evV[i % 4][:, :]).then_inc(S_['vo'], 16)
                if n_units(g) > 16:
                    for u in range(16, 24):
                        gp_.wait_ge(S_['ev'], UB[g] + u + 2)
                        gp_.dma_start(QT_d[(u - 16) * 128:(u - 15) * 128,
                                           g * TB:(g + 1) * TB],
                                      evK[(8 + u - 16) % 4][:, :]).then_inc(S_['eo'], 16)

        @block.vector
        def _(v):
            v.wait_ge(S_['ld'], 49)
            v.wait_ge(S_['wl'], 48)
            v.tensor_copy(nsrow_b[:, :], nsrow_f[:, :])
            stats_dve_pre(v, env1, 0)

            def rows_chain(g2):
                kind2 = blk_kind(g2)
                stats_dve_rows1(v, env1, g2)
                stats_dve_rows2(v, env1, g2)
                stats_dve_rows3(v, env1, g2, kind2)
                stats_dve_rows4(v, env1, g2)
            rows_chain(0)
            stats_dve_pre(v, env1, 1)
            for g in range(NGA):
                kind = blk_kind(g)
                stats_dve_h(v, env1, g, kind)
                if g + 1 < NGA:
                    g2 = g + 1
                    stats_dve_rows1(v, env1, g2)
                    stats_dve_rows2(v, env1, g2)
                    stats_dve_rows3(v, env1, g2, blk_kind(g2))
                    stats_dve_rows4(v, env1, g2)
                    if g2 + 1 < NGA:
                        stats_dve_pre(v, env1, g2 + 1)

        @block.scalar
        def _(sc):
            sc.wait_ge(S_['ld'], 1)
            stats_act_rows1(sc, env1, 0)
            stats_act_rows2(sc, env1, 0, blk_kind(0))
            stats_act_rows3(sc, env1, 0, blk_kind(0))
            for g in range(NGA):
                nu = n_units(g)
                if g + 1 < NGA:
                    g2 = g + 1
                    hooks = {
                        5 * nu // 8: (lambda g2=g2: stats_act_rows1(sc, env1, g2)),
                        6 * nu // 8: (lambda g2=g2: stats_act_rows2(sc, env1, g2,
                                                                    blk_kind(g2))),
                        7 * nu // 8: (lambda g2=g2: stats_act_rows3(sc, env1, g2,
                                                                    blk_kind(g2))),
                    }
                else:
                    hooks = None
                stats_act_evac(sc, env1, g, evK=evK, evV=evV, hooks=hooks)
            padA(sc, 'ev')

        @block.tensor
        def _(t):
            t.wait_ge(S_['ld'], 49)
            t.wait_ge(S_['wl'], 48)
            for g in range(NGA):
                if g >= 1:
                    stats_pe_work_bcast(t, env1, g - 1)
                stats_pe_stats(t, env1, g)
                if g >= 1:
                    stats_pe_work(t, env1, g - 1, blk_kind(g - 1), wkt=wkt, wvt=wvt,
                                  wqt=wqt, split=True)
            stats_pe_work(t, env1, NGA - 1, blk_kind(NGA - 1), wkt=wkt, wvt=wvt, wqt=wqt)

    if rank < 2:
        ctx.close()
        return nc
    with ExitStack() as sc2:
        t2 = lambda *a: sc2.enter_context(nc.sbuf_tensor(*a))
        Kb = [t2("Kb%d" % i, [128, S], BF16) for i in range(2)]
        Vb = [t2("Vb%d" % i, [128, S], BF16) for i in range(2)]
        Qb = [t2("Qb%d" % i, [128, QOWN], BF16) for i in range(2)]
        NV = t2("NV", [128, NPAIR * QOWN], BF16)
        ptl = {0: [t2("pA%d" % i, [128, 1024], BF16) for i in range(2)],
               1: [t2("pB%d" % i, [128, 1024], BF16) for i in range(2)]}
        sums = t2("sums", [128, QT], F32)
        sums2 = t2("sums2", [128, QT], F32)
        sumsT = [sums, sums2]
        recs = t2("recs", [128, QT], F32)
        recs2 = t2("recs2", [128, QT], F32)
        recsT = [recs, recs2]
        recsP3 = [t2("recsP3%d" % i, [65, QT], F32) for i in range(2)]
        ksq = t2("ksq", [128, 2048], BF16)
        kqc = t2("kqc", [128, 24], F32)
        biasA = t2("biasA", [128, 1], F32)
        biasB = t2("biasB", [128, 1], F32)
        cneg = t2("cneg", [1, 2], BF16)
        taur = t2("taur", [128, 1], F32)
        tau_up = t2("tau_up", [128, 1], F32)
        tau_dn = t2("tau_dn", [128, 1], F32)
        twk = [t2("twk%d" % i, [128, 1024], F32) for i in range(3)]
        TWS = None
        tabs = t2("tabs", [128, 1024], F32)
        tmf = t2("tmf", [128, 1024], BF16)
        tout = [t2("tout%d" % i, [128, 1024], BF16) for i in range(2)]
        tacc = t2("tacc", [128, 64], F32)
        wot = t2("wot", [128, NDC * D], BF16)

        slots = [PS[0], PS[1], PS[2]]

        def tern_src(kind, k):
            if kind in ('u1', 'u2'):
                dc, q4 = k // 4, k % 4
                return wupT_d[dc * 128:(dc + 1) * 128, q4 * 1024:(q4 + 1) * 1024], 1024
            return wdnT_d[k * 128:(k + 1) * 128, :], 1024

        @block.sync
        def _(sy):
            for j in range(NPAIR):
                if j == 0:
                    sy.wait_ge(S_['eo'], 16 * KO[NGA])
                    sy.wait_ge(S_['vo'], 16 * VO[NGA])
                if j >= 2:
                    sy.wait_ge(S_['av'], NQT * NCP * (j - 1))
                p2s = S_['p2'] if j % 2 == 0 else S_['p2b']
                sy.dma_start(Kb[j % 2][:, :], KT_d[j * 128:(j + 1) * 128, :]
                             ).then_inc(p2s, 16)
                sy.dma_start(Qb[j % 2][:, :], QT_d[j * 128:(j + 1) * 128, :]
                             ).then_inc(p2s, 16)
                for i in range(8):
                    sy.dma_start(Vb[j % 2][:, i * 1024:(i + 1) * 1024]
                                 .rearrange("p (c e) -> p c e", c=8),
                                 V_d[i * 1024:(i + 1) * 1024, j * 128:(j + 1) * 128]
                                 .rearrange("(c p) e -> p c e", p=128)).then_inc(p2s, 16)

        @block.gpsimd
        def _(gp_):
            gp_.wait_ge(S_['eo'], 16 * KO[NGA])
            gp_.wait_ge(S_['vo'], 16 * VO[NGA])
            gp_.wait_ge(S_['ev'], UB[NGA])
            gp_.memset(recs[:, :], 0.0)
            gp_.memset(recs2[:, :], 0.0)
            gp_.memset(recsP3[0][:, :], 0.0)
            gp_.memset(recsP3[1][:, :], 0.0)
            gp_.memset(tacc[:, :], 0.0).then_inc(S_['tv'], 1)
            gp_.dma_start(wot[:, :].rearrange("p (c e) -> p c e", c=NDC), wo_d.rearrange("(c p) e -> p c e", c=NDC)
                          ).then_inc(S_['woc'], 16)
            def tern_out(gp_, i, kind, k):
                gp_.wait_ge(S_['tv'], 1 + i + 1)
                if kind == 'u2':
                    dc, q4 = k // 4, k % 4
                    r0 = dc * NFC + q4 * 8
                    dst = WUPt_d[r0:r0 + 8, :]
                else:
                    r0 = k * NDC
                    dst = WDNt_d[r0:r0 + 8, :]
                gp_.dma_start(dst.rearrange("f (p e) -> p f e", p=128),
                              tout[i % 2][:, 0:1024].rearrange("p (f e) -> p f e", f=8)
                              ).then_inc(S_['to'], 16)

            pend_out = [None]
            TWS = [S_['tw'], S_['tw1'], S_['tw2']]
            for i, (kind, k) in enumerate(TERN_UNITS):
                if i >= 3:
                    gp_.wait_ge(S_['tv'], 1 + i - 2)
                src, w = tern_src(kind, k)
                gp_.dma_start(twk[i % 3][:, 0:w], src).then_inc(TWS[i % 3], 16)
                if pend_out[0] is not None:
                    tern_out(gp_, *pend_out[0])
                    pend_out[0] = None
                if kind in ('u2', 'd2'):
                    pend_out[0] = (i, kind, k)
            tern_out(gp_, *pend_out[0])
            pend_out[0] = None

        def _avblock(t, j, qg, prev, u, first, drain=False, need_cb=False):
            t.wait_ge(S_['ex'], u if drain else u - 2)
            if first:
                t.wait_ge(S_['tl'], 2 * qg)
                if need_cb:
                    t.wait_ge(S_['cb'], j + 1)
            mm = None
            for ci in (0, 1):
                c = 2 * prev + ci
                fst = first and ci == 0
                lst = drain and ci == 1
                t.matmul(p_nv[0:64, :], Vb[j % 2][:, c * 128: c * 128 + 64],
                         ptl[0][prev % 2][:, ci * 512:(ci + 1) * 512],
                         start=fst, stop=lst, skip_group_check=True, tile_position=(0, 0))
                t.matmul(p_nv[64:128, :], Vb[j % 2][:, c * 128 + 64: c * 128 + 128],
                         ptl[1][prev % 2][:, ci * 512:(ci + 1) * 512],
                         start=fst, stop=lst, skip_group_check=True, tile_position=(0, 64))
                t.matmul(p_sum[0:1, :], ones_b[:, :],
                         ptl[0][prev % 2][:, ci * 512:(ci + 1) * 512],
                         start=fst, stop=lst, skip_group_check=True, tile_position=(0, 0))
                mm = t.matmul(p_sum[64:65, :], ones_b[:, :],
                              ptl[1][prev % 2][:, ci * 512:(ci + 1) * 512],
                              start=fst, stop=lst, skip_group_check=True, tile_position=(0, 64))
            mm.then_inc(S_['av'], 1)

        @block.tensor
        def _(t):
            u = 0
            t.wait_ge(S_['tv'], 1)
            for j in range(NPAIR):
                jq0 = j * NQT
                t.wait_ge(S_['p2'] if j % 2 == 0 else S_['p2b'], 160 * (j // 2 + 1))
                t.wait_ge(S_['tl'], 2 * jq0)
                for nm, pj in (('up', 2), ('dn', 4)):
                    if j == pj:
                        e = 0 if nm == 'up' else 1
                        t.wait_ge(S_['tau'], 5 * e + 1)
                        t.matmul(p_nv[0:1, 0:1], ones_f[:, :], taur[:, :],
                                 start=True, stop=True,
                                 skip_group_check=True).then_inc(S_['tau'], 1)
                        t.wait_ge(S_['tau'], 5 * e + 3)
                        t.matmul(p_nv[:, 2:3], onesr_f[:, :], kqc[0:1, 22:23],
                                 start=True, stop=True,
                                 skip_group_check=True).then_inc(S_['tau'], 1)
                        t.wait_ge(S_['tau'], 5 * e + 5)
                slK = slots[(128 * j + 2) % 3]
                fbK = [p_nv, p_sum, slK[:, 0:512], slK[:, 512:1024]]
                for f in range(20):
                    if f < 16:
                        qb, cg = f // 4, f % 4
                        t.wait_ge(S_['ksq'], 5 * j + qb + 1)
                        srcq = ksq
                    else:
                        cg = f - 16
                        t.wait_ge(S_['ksq'], 5 * j + 5)
                        srcq = ksq
                    if f >= 4:
                        t.wait_ge(S_['knr'], 11 * j + (f - 4) // 2 + 1)
                    bank = fbK[f % 4]
                    t.matmul(bank[0:64, :], selk[0:64, :], srcq[0:64, cg * 512:(cg + 1) * 512],
                             start=True, stop=True, skip_group_check=True, tile_position=(0, 0))
                    t.matmul(bank[64:128, :], selk[64:128, :],
                             srcq[64:128, cg * 512:(cg + 1) * 512],
                             start=True, stop=True, skip_group_check=True,
                             tile_position=(64, 64)).then_inc(S_['kn'], 1)
                t.wait_ge(S_['knr'], 11 * j + 11)
                t.matmul(p_nv[:, 0:1], onesr_b[:, :], cneg[0:1, 0:1],
                         start=True, stop=True, skip_group_check=True)
                t.matmul(p_nv[:, 1:2], onesr_b[:, :], cneg[0:1, 1:2],
                         start=True, stop=True, skip_group_check=True).then_inc(S_['kn'], 1)
                t.wait_ge(S_['knr'], 11 * j + 10)
                for qtr in range(NQT):
                    qg = jq0 + qtr
                    qc = qtr * QT
                    for cp in range(NCP):
                        for h_ in (0, 1):
                            slot = slots[(u + h_) % 3]
                            if u + h_ >= 3:
                                t.wait_ge(S_['ex'], u + h_ - 2)
                            base = 64 * h_
                            mm = None
                            for ci in (0, 1):
                                c = 2 * cp + ci
                                mm = t.matmul(slot[:, ci * 512:(ci + 1) * 512],
                                              Kb[j % 2][base:base + 64, c * 128:(c + 1) * 128],
                                              Qb[j % 2][base:base + 64, qc:qc + QT],
                                              start=True, stop=True, skip_group_check=True,
                                              tile_position=(base, 0))
                            mm.then_inc(S_['sc'], 1)
                        u += 2
                        if cp >= 1:
                            _avblock(t, j, qg, cp - 1, u, first=(cp == 1),
                                     need_cb=(qtr == 0))
                    _avblock(t, j, qg, NCP - 1, u, first=False, drain=True)

        def recs_store(eng, qs):
            js, qcs = qs // NQT, (qs % NQT) * QT
            eng.wait_ge(S_['bc'], qs + 1)
            eng.dma_start(recs_d[0:2, js * QOWN + qcs: js * QOWN + qcs + QT],
                          recsT[qs % 2][0:65:64, 0:QT]).then_inc(S_['rc'], 16)

        @block.scalar
        def _(sc):
            u = 0
            for j in range(NPAIR):
                sc.wait_ge(S_['p2'] if j % 2 == 0 else S_['p2b'], 160 * (j // 2 + 1))
                for qb in range(4):
                    if qb >= 1:
                        sc.wait_ge(S_['kn'], 21 * j + 4 * qb)
                    elif j >= 1:
                        sc.wait_ge(S_['kn'], 21 * (j - 1) + 20)
                    sc.activation(ksq[:, :], Kb[j % 2][:, qb * 2048:(qb + 1) * 2048],
                                  AF.Square).then_inc(S_['ksq'], 1)
                sc.wait_ge(S_['kn'], 21 * j + 16)
                sc.activation(ksq[:, :], Qb[j % 2][:, :],
                              AF.Square).then_inc(S_['ksq'], 1)
                sc.wait_ge(S_['cb'], j + 1)
                for qtr in range(NQT):
                    qg = j * NQT + qtr
                    for cp in range(NCP):
                        if cp == 12 and qg >= 1:
                            recs_store(sc, qg - 1)
                        for h_ in (0, 1):
                            sc.wait_ge(S_['sc'], u + 1)
                            guard = qg * NCP + cp - 1
                            if guard > 0:
                                sc.wait_ge(S_['av'], guard)
                            bias = biasA if h_ == 0 else biasB
                            sc.activation(ptl[h_][cp % 2][:, :], slots[u % 3][:, :], AF.Exp,
                                          bias=bias[:, 0:1], scale=1.0).then_inc(S_['ex'], 1)
                            u += 1



        @block.vector
        def _(v):
            pend_rec = [None]

            def emit_recs(v):
                if pend_rec[0] is None:
                    return
                qg = pend_rec[0]
                pend_rec[0] = None
                if qg >= 2:
                    v.wait_ge(S_['rc'], 16 * (qg - 1))
                v.reciprocal(recsT[qg % 2][0:1, :], sumsT[qg % 2][0:1, :])
                v.reciprocal(recsT[qg % 2][64:65, :], sumsT[qg % 2][64:65, :])
                padV(v, 'bc')
            for j in range(NPAIR):
                jq0 = j * NQT
                for nm, pj in (('up', 2), ('dn', 4)):
                    if j == pj:
                        e = 0 if nm == 'up' else 1
                        cols = (0, 32) if nm == 'up' else (32, 64)
                        v.wait_ge(S_['tv'], 1 + (32 if nm == 'up' else 64))
                        v.reduce_sum(taur[:, 0:1], tacc[:, cols[0]:cols[1]],
                                     axis=mybir.AxisListType.X)
                        padV(v, 'tau')
                        v.wait_ge(S_['tau'], 5 * e + 2)
                        v.tensor_single_scalar(kqc[0:1, 22:23], p_nv[0:1, 0:1], TAUF, ALU.mult)
                        padV(v, 'tau')
                        v.wait_ge(S_['tau'], 5 * e + 4)
                        dst = tau_up if nm == 'up' else tau_dn
                        v.tensor_copy(dst[:, :], p_nv[:, 2:3])
                        padV(v, 'tau')
                slK = slots[(128 * j + 2) % 3]
                fbK = [p_nv, p_sum, slK[:, 0:512], slK[:, 512:1024]]
                v.wait_ge(S_['p2'] if j % 2 == 0 else S_['p2b'], 160 * (j // 2 + 1))
                fb2 = [PS[3][:, 0:1024], slK[:, 0:1024]]
                for r in range(10):
                    v.wait_ge(S_['kn'], 21 * j + 2 * r + 2)
                    v.reduce_max(kqc[:, r:r + 1], fb2[r % 2][:, :],
                                 axis=mybir.AxisListType.X).then_inc(S_['knr'], 1)
                v.wait_ge(S_['knr'], 11 * j + 10)
                v.reduce_max(kqc[:, 20:21], kqc[:, 0:8], axis=mybir.AxisListType.X)
                v.reduce_max(kqc[:, 21:22], kqc[:, 8:10], axis=mybir.AxisListType.X)
                v.tensor_tensor(kqc[:, 22:23], kqc[:, 20:21], kqc[:, 21:22], ALU.add)
                v.tensor_single_scalar(kqc[:, 22:23], kqc[:, 22:23], 0.5, ALU.mult)
                v.tensor_single_scalar(cneg[0:1, 0:1], kqc[0:1, 22:23], -1.0, ALU.mult)
                v.tensor_single_scalar(cneg[0:1, 1:2], kqc[64:65, 22:23], -1.0, ALU.mult)
                padV(v, 'knr')
                if j >= 1:
                    v.wait_ge(S_['ex'], 256 * j)
                v.wait_ge(S_['kn'], 21 * j + 21)
                v.tensor_copy(biasA[:, :], p_nv[:, 0:1])
                v.tensor_copy(biasB[:, :], p_nv[:, 1:2])
                padV(v, 'cb')
                emit_recs(v)
                i0 = sum(len(TERN_SCHED[q]) for q in range(j))
                tu = TERN_SCHED[j]
                for qtr in range(NQT):
                    qg = jq0 + qtr
                    qc = qtr * QT
                    emit_recs(v)
                    for ii in range(4 * qtr, 4 * qtr + 4):
                        if ii >= len(tu):
                            break
                        kind, k = tu[ii]
                        i = i0 + ii
                        tws = [S_['tw'], S_['tw1'], S_['tw2']][i % 3]
                        v.wait_ge(tws, 16 * (i // 3 + 1))
                        w = 1024
                        if kind in ('u1', 'd1'):
                            col = k if kind == 'u1' else 32 + k
                            v.reduce_sum(tacc[:, col:col + 1], twk[i % 3][:, 0:w],
                                         axis=mybir.AxisListType.X,
                                         apply_absolute_value=True).then_inc(S_['tv'], 1)
                        else:
                            tau_t = tau_up if kind == 'u2' else tau_dn
                            if TERN_OUT_BEFORE[i] > 0:
                                v.wait_ge(S_['to'], 16 * TERN_OUT_BEFORE[i])
                            v.scalar_tensor_tensor(tabs[:, 0:w], twk[i % 3][:, 0:w], -1.0,
                                                   twk[i % 3][:, 0:w], ALU.mult, ALU.max)
                            v.tensor_single_scalar(tabs[:, 0:w], tabs[:, 0:w],
                                                   tau_t[:, 0:1], ALU.subtract)
                            v.tensor_single_scalar(tmf[:, 0:w], tabs[:, 0:w], 0.0, ALU.is_gt)
                            v.tensor_single_scalar(tabs[:, 0:w], twk[i % 3][:, 0:w], 0.0,
                                                   ALU.is_gt)
                            v.tensor_scalar(tabs[:, 0:w], tabs[:, 0:w], 2.0, -1.0,
                                            ALU.mult, ALU.add)
                            v.tensor_tensor(tout[i % 2][:, 0:w], tabs[:, 0:w], tmf[:, 0:w],
                                            ALU.mult).then_inc(S_['tv'], 1)
                    v.wait_ge(S_['av'], (qg + 1) * NCP)
                    v.tensor_copy(sumsT[qg % 2][:, :], p_sum).then_inc(S_['tl'], 1)
                    v.tensor_copy(NV[:, j * QOWN + qc: j * QOWN + qc + QT],
                                  p_nv).then_inc(S_['tl'], 1)
                    pend_rec[0] = qg

        @block.vector
        def _(v):
            qg = NPAIR * NQT - 1
            v.wait_ge(S_['rc'], 16 * (qg - 1))
            v.reciprocal(recsT[qg % 2][0:1, :], sumsT[qg % 2][0:1, :])
            v.reciprocal(recsT[qg % 2][64:65, :], sumsT[qg % 2][64:65, :])
            padV(v, 'bc')

        if rank == 2:
            @block.sync
            def _(sy):
                sy.wait_ge(S_['tl'], 2 * NPAIR * NQT)
                sy.wait_ge(S_['to'], 16 * N_TERN_OUT)
                sy.dma_start(dbg4, NV[:, :]).then_inc(S_['out'], 16)
                sy.dma_start(dbg5[:, 0:1], tau_up[:, :], allow_slow_non_contiguous=True).then_inc(S_['out'], 16)
                sy.dma_start(dbg5[:, 1:2], tau_dn[:, :], allow_slow_non_contiguous=True).then_inc(S_['out'], 16)
                sy.dma_start(dbg5[:, 16:80], tacc[:, :]).then_inc(S_['out'], 16)
                sy.dma_start(dbg3[:, 0:1024], tout[0][:, 0:1024]).then_inc(S_['out'], 16)
                sy.dma_start(dbg1.rearrange("p (a q) -> p a q", a=64),
                             WUPt_d[0:64, :].rearrange("a (p q) -> p a q", p=128)
                             ).then_inc(S_['out'], 16)
                sy.dma_start(dbg5[:, 2048:3072], twk[0][:, :]).then_inc(S_['out'], 16)
                sy.dma_start(dbg5[:, 4096:6144],
                             wupT_d[0:128, 0:2048]).then_inc(S_['out'], 16)
                sy.wait_ge(S_['out'], 128)

        p3slots = [PS[0][:, 0:512], PS[0][:, 512:1024], PS[1][:, 0:512], PS[1][:, 512:1024]]

        if rank >= 3:

            nb3 = [PS[2][:, 0:512], PS[2][:, 512:1024], PS[3][:, 0:512], PS[3][:, 512:1024]]

            @block.sync
            def _(sy):
                sy.wait_ge(S_['rc'], 16 * (NPAIR * NQT - 1))
                for qg in range(4):
                    for jj in range(NPAIR):
                        bi = qg * NPAIR + jj
                        if jj == NPAIR - 1 and qg == 3:
                            continue
                        if bi >= 2:
                            sy.wait_ge(S_['nn'], bi - 1)
                        sy.dma_start(recsP3[bi % 2][0:65:64, :],
                                     recs_d[0:2, jj * QOWN + qg * QT: jj * QOWN + (qg + 1) * QT]
                                     ).then_inc(S_['rl'], 16)

            @block.tensor
            def _(t):
                t.wait_ge(S_['tl'], 2 * NPAIR * NQT)
                t.wait_ge(S_['woc'], 16)
                def p3bcast(t, qg2, jj):
                    bi = qg2 * NPAIR + jj
                    if bi >= 4:
                        t.wait_ge(S_['p3n'], bi - 3)
                    if bi == NPAIR * 4 - 1:
                        t.wait_ge(S_['bc'], NPAIR * NQT)
                        src_ = recsT[(NPAIR * 4 - 1) % 2][0:65, :]
                    else:
                        t.wait_ge(S_['rl'], 16 * min(bi + 2, NPAIR * 4 - 1))
                        src_ = recsP3[bi % 2][0:65, :]
                    t.matmul(nb3[bi % 4], sel[0:65, :], src_,
                             start=True, stop=True,
                             skip_group_check=True).then_inc(S_['nn'], 1)
                for jj in range(NPAIR):
                    p3bcast(t, 0, jj)
                for qg in range(4):
                    for dc in range(NDC):
                        idx = qg * NDC + dc
                        if idx >= 4:
                            t.wait_ge(S_['p3e'], idx - 3)
                        t.wait_ge(S_['p3n'], NPAIR * (qg + 1))
                        mm = None
                        for hc in range(NDC):
                            mm = t.matmul(p3slots[idx % 4],
                                          wot[:, hc * D + dc * 128: hc * D + (dc + 1) * 128],
                                          NV[:, hc * QOWN + qg * 512: hc * QOWN + (qg + 1) * 512],
                                          start=(hc == 0), stop=(hc == NDC - 1),
                                          skip_group_check=True)
                        mm.then_inc(S_['p3'], 1)
                        if qg < 3:
                            p3bcast(t, qg + 1, dc)

            @block.vector
            def _(v):
                def p3tt(v, qg2, jj):
                    bi = qg2 * NPAIR + jj
                    v.wait_ge(S_['nn'], bi + 1)
                    v.tensor_tensor(NV[:, jj * QOWN + qg2 * 512: jj * QOWN + (qg2 + 1) * 512],
                                    NV[:, jj * QOWN + qg2 * 512: jj * QOWN + (qg2 + 1) * 512],
                                    nb3[bi % 4], ALU.mult).then_inc(S_['p3n'], 1)
                for jj in range(NPAIR):
                    p3tt(v, 0, jj)
                for qg in range(4):
                    for dc in range(NDC):
                        idx = qg * NDC + dc
                        v.wait_ge(S_['p3'], idx + 1)
                        v.tensor_scalar(ST[:, dc * QOWN + qg * 512: dc * QOWN + (qg + 1) * 512],
                                        p3slots[idx % 4], bout_c[:, dc:dc + 1], None,
                                        ALU.add).then_inc(S_['p3e'], 1)
                        if qg < 3:
                            p3tt(v, qg + 1, dc)

    if rank == 3:
        @block.sync
        def _(sy):
            sy.wait_ge(S_['p3e'], 32)
            sy.dma_start(dbg4, ST[:, :]).then_inc(S_['out'], 16)
            sy.wait_ge(S_['out'], 16)
    if rank < 4:
        ctx.close()
        return nc
    with ExitStack() as sc4:
        t4 = lambda *a: sc4.enter_context(nc.sbuf_tensor(*a))
        xblk4s = t4("xblk4", [128, NDC * TB], F32)
        xblk4 = [xblk4s, xblk4s]
        xbf4 = t4("xbf4", [128, NDC * TB], BF16)
        xsq4 = t4("xsq4", [128, NDC * TB], BF16)
        rowA4 = t4("rowA4", [1, 6 * TB], F32)
        rowB4 = t4("rowB4", [1, 4 * TB], F32)
        rowC4 = t4("rowC4", [1, 4 * TB], BF16)
        HF = t4("HF", [128, NDC * QOWN], BF16)
        mid = t4("mid", [128, NFC * 512], BF16)
        Wub = [t4("Wub%d" % i, [128, NDC * 128], BF16) for i in range(4)]
        Wdb = [t4("Wdb%d" % i, [128, NFC * 128], BF16) for i in range(3)]
        WD3 = [3, 3, 2]
        a3s = t4("a3s", [128, 512], F32)
        gtile = t4("gtile", [128, 512], F32)
        gsil = t4("gsil", [128, 512], BF16)
        xfin = [t4("xfin%d" % i, [128, 512], F32) for i in range(2)]
        a3hist = None
        yout = [t4("yout%d" % i, [128, 512], F32) for i in range(2)]
        hsc4 = t4("hsc4", [128, TB], BF16)
        env4 = dict(xblk=xblk4, xbf=xbf4, xsq=xsq4, rowA=rowA4, rowB=rowB4, rowC=rowC4,
                    hdst=HF, hT=None, hsc=hsc4, xd4base=GP4)

        NTB4 = 4

        def wo_count(tb, item, i):
            base = tb * (NFC + 5 * NDC)
            if item == 'wub':
                return 16 * (base + i + 1)
            if item == 'wdb':
                return 16 * (base + NFC + 5 * i + 4)
            return 16 * (base + NFC + 5 * i + 5)

        @block.sync
        def _(sy):
            sy.wait_ge(S_['p3e'], 32)
            for g in range(GP4, NG):
                off = (g - GP4) * TB
                for cc in range(NDC):
                    if g >= GP4 + 1:
                        sy.wait_ge(S_['hh'], 8 * (g - 1) + cc + 1)
                    sy.dma_start(xblk4[g % 2][:, cc * TB:(cc + 1) * TB],
                                 xTo[cc * 128:(cc + 1) * 128, off:off + TB]
                                 ).then_inc(S_['xd4'], 16)
            if not p4_full:
                sy.wait_ge(S_['hh'], 8 * NG)
                sy.dma_start(dbg4, ST[:, :]).then_inc(S_['out'], 16)
                sy.wait_ge(S_['out'], 16)
            for tb in range(NTB4 if p4_full else 0):
                if tb == 1 and rank == 5:
                    sy.wait_ge(S_['fa4'], 32)
                    sy.dma_start(dbgM, mid[:, :]).then_inc(S_['dmp'], 16)
                    sy.wait_ge(S_['dmp'], 16)
                for fc in range(NFC):
                    if fc >= 4:
                        sy.wait_ge(S_['f4'], tb * F4_PER + fc - 3)
                    elif tb >= 1:
                        sy.wait_ge(S_['f4'], (tb - 1) * F4_PER + 29 + fc)
                    if tb == 0 and fc == 0:
                        sy.wait_ge(S_['to'], 16 * N_TERN_OUT)
                    sy.dma_start(Wub[fc % 4][:, :].rearrange("p (c e) -> p c e", c=NDC),
                                 WUPt_d[fc::NFC, :].rearrange("c (p e) -> p c e", p=128)
                                 ).then_inc(S_['wu%d' % (fc % 4)], 16)
                for dc in range(NDC):
                    if dc >= 3:
                        sy.wait_ge(S_['f4'], tb * F4_PER + 31 + dc)
                    elif tb >= 1:
                        sy.wait_ge(S_['f4'], (tb - 1) * F4_PER + 39 + dc)
                    for k4 in range(4):
                        sy.dma_start(Wdb[dc % 3][:, k4 * 1024:(k4 + 1) * 1024]
                                     .rearrange("p (c e) -> p c e", c=8),
                                     WDNt_d[dc::NDC, :][k4 * 8:(k4 + 1) * 8, :]
                                     .rearrange("c (p e) -> p c e", p=128)
                                     ).then_inc(S_['wd%d' % (dc % 3)], 16)
                    if tb * NDC + dc - 2 >= 0:
                        sy.wait_ge(S_['fy'], tb * NDC + dc - 1)
                    sy.dma_start(xfin[dc % 2][:, :],
                                 xTo[dc * 128:(dc + 1) * 128, tb * 512:(tb + 1) * 512]
                                 ).then_inc(S_['wx'], 16)
            if p4_full:
                sy.wait_ge(S_['out'], 16 * NTB4 * NDC // 2)
                sy.wait_ge(S_['ou1'], 16 * NTB4 * NDC // 2)
            if rank == 5:
                sy.dma_start(dbg1[:, 0:1024].rearrange("p (a q) -> p a q", a=8),
                             WUPt_d[0:8, :].rearrange("a (p q) -> p a q", p=128)
                             ).then_inc(S_['out'], 16)
                sy.dma_start(dbg2[:, 0:1024].rearrange("p (a q) -> p a q", a=8),
                             WDNt_d[0:8, :].rearrange("a (p q) -> p a q", p=128)
                             ).then_inc(S_['out'], 16)
                sy.dma_start(dbg4, ST[:, :]).then_inc(S_['out'], 16)
                sy.dma_start(dbg5[:, 0:512], a3s[:, :]).then_inc(S_['out'], 16)

                sy.dma_start(dbg5[0:1, 1024:1536], rowA4[0:1, 512:1024],
                             allow_slow_non_contiguous=True).then_inc(S_['out'], 16)
                sy.dma_start(dbg5[0:1, 1536:2048], rowA4[0:1, 1024:1536],
                             allow_slow_non_contiguous=True).then_inc(S_['out'], 16)
                sy.dma_start(dbg5[:, 512:1024], gtile[:, :]).then_inc(S_['out'], 16)
                sy.dma_start(dbg3[:, 1024:1536], mid[:, 0:512]).then_inc(S_['out'], 16)
                sy.dma_start(dbgH, HF[:, :]).then_inc(S_['out'], 16)
                sy.wait_ge(S_['out'], 16 * (NTB4 * NDC // 2 + 9))

        def mid_sq_view(fc):
            return xsq4[:, (fc % NDC) * 512:(fc % NDC) * 512 + 512]

        def _summ(t, tb, fc):
            t.wait_ge(S_['fe4'], tb * FE4_PER + fc + 1)
            t.matmul(PS[3][0:1, 0:512], ones_b[:, :], mid_sq_view(fc),
                     start=(fc == 0), stop=(fc == NFC - 1),
                     skip_group_check=True).then_inc(S_['fs'], 1)

        @block.gpsimd
        def _(gp_):
            for tb in range(NTB4 if p4_full else 0):
                for dc in range(NDC):
                    yi = tb * NDC + dc
                    gp_.wait_ge(S_['fy'], yi + 1)
                    gp_.dma_start(yT[dc * 128:(dc + 1) * 128, tb * 512:(tb + 1) * 512],
                                  yout[yi % 2][:, :]
                                  ).then_inc(S_['out'] if yi % 2 == 0 else S_['ou1'], 16)

        @block.tensor
        def _(t):
            for g in range(GP4, NG):
                if g > GP4:
                    stats_pe_work(t, env4, g - 1, 'p4')
                stats_pe_stats(t, env4, g)
            stats_pe_work(t, env4, NG - 1, 'p4')
            for tb in range(NTB4 if p4_full else 0):
                for fc in range(NFC):
                    t.wait_ge(S_['wu%d' % (fc % 4)], 16 * (8 * tb + fc // 4 + 1))
                    if tb == 0 and fc == 0:
                        t.wait_ge(S_['hh'], 8 * NG)
                        t.wait_ge(S_['sa'], 3 * (NG - 1) + 1)
                    if fc >= 3:
                        t.wait_ge(S_['fa4'], tb * FA4_PER + fc - 2)
                    elif tb >= 1:
                        t.wait_ge(S_['fe4'], tb * FE4_PER)
                    mm = None
                    for dc in range(NDC):
                        mm = t.matmul(PS[fc % 3][:, 0:512],
                                      Wub[fc % 4][:, dc * 128:(dc + 1) * 128],
                                      HF[:, dc * QOWN + tb * 512: dc * QOWN + tb * 512 + 512],
                                      start=(dc == 0), stop=(dc == NDC - 1),
                                      skip_group_check=True)
                    mm.then_inc(S_['f4'], 1)
                    if fc >= 2:
                        _summ(t, tb, fc - 2)
                _summ(t, tb, NFC - 2)
                _summ(t, tb, NFC - 1)
                t.wait_ge(S_['fe4'], tb * FE4_PER + 33)
                t.matmul(PS[3][:, 512:1024], onesr_b[:, :], rowC4[0:1, 0:512],
                         start=True, stop=False, skip_group_check=True)
                t.matmul(PS[3][:, 512:1024], onesr_b[:, :], rowC4[0:1, 512:1024],
                         start=False, stop=True, skip_group_check=True).then_inc(S_['f4'], 1)
                for dc in range(NDC):
                    t.wait_ge(S_['fa4'], tb * FA4_PER + 32)
                    if dc == 0:
                        t.wait_ge(S_['fe4'], tb * FE4_PER + 32)
                    t.wait_ge(S_['wd%d' % (dc % 3)],
                              16 * 4 * (WD3[dc % 3] * tb + dc // 3 + 1))
                    if dc >= 2:
                        t.wait_ge(S_['fe4'], tb * FE4_PER + 35 + dc - 2)
                    mm = None
                    for fc in range(NFC):
                        mm = t.matmul(PS[dc % 2][:, 0:512],
                                      Wdb[dc % 3][:, fc * 128:(fc + 1) * 128],
                                      mid[:, fc * 512:(fc + 1) * 512],
                                      start=(fc == 0), stop=(fc == NFC - 1),
                                      skip_group_check=True)
                    mm.then_inc(S_['f4'], 1)

        @block.scalar
        def _(sc):
            for g in range(GP4, NG):
                stats_act_rows1(sc, env4, g)
                stats_act_rows2(sc, env4, g, 'p4')
                stats_act_rows3(sc, env4, g, 'p4')
            for tb in range(NTB4 if p4_full else 0):
                for fc in range(NFC):
                    sc.wait_ge(S_['f4'], tb * F4_PER + fc + 1)
                    sc.activation(mid[:, fc * 512:(fc + 1) * 512], PS[fc % 3][:, 0:512],
                                  AF.Gelu).then_inc(S_['fa4'], 1)
                sc.wait_ge(S_['fs'], (tb + 1) * FS_PER)
                sc.activation(rowA4[0:1, 0:512], PS[3][0:1, 0:512], AF.Ln, scale=1.0 / FF)
                sc.activation(rowA4[0:1, 1024:1536], rowA4[0:1, 0:512], AF.Exp, scale=0.5)
                sc.activation(rowA4[0:1, 1536:2048], rowA4[0:1, 1024:1536], AF.Identity,
                              bias=eps_t[0:1, 0:1])
                sc.activation(rowA4[0:1, 0:512], rowA4[0:1, 1536:2048], AF.Ln)
                sc.activation(rowA4[0:1, 1024:1536], rowA4[0:1, 0:512], AF.Exp,
                              scale=-1.0)
                padA(sc, 'fa4')
                for dc in range(NDC):
                    sc.wait_ge(S_['fe4'], tb * FE4_PER + 35 + dc)
                    sc.activation(gsil[:, :], gtile[:, :], AF.Silu)
                    padA(sc, 'fa4')

        @block.vector
        def _(v):
            stats_dve_pre(v, env4, GP4)
            for g in range(GP4, NG):
                stats_dve_rows1(v, env4, g)
                stats_dve_rows2(v, env4, g)
                stats_dve_rows3(v, env4, g, 'p4')
                stats_dve_rows4(v, env4, g)
                stats_dve_h(v, env4, g, 'p4')
                if g + 1 < NG:
                    stats_dve_pre(v, env4, g + 1)
            for tb in range(NTB4 if p4_full else 0):
                for fc in range(NFC):
                    v.wait_ge(S_['fa4'], tb * FA4_PER + fc + 1)
                    if fc >= NDC:
                        v.wait_ge(S_['fs'], tb * FS_PER + fc - NDC + 1)
                    v.tensor_tensor(mid_sq_view(fc), mid[:, fc * 512:(fc + 1) * 512],
                                    mid[:, fc * 512:(fc + 1) * 512],
                                    ALU.mult).then_inc(S_['fe4'], 1)
                v.wait_ge(S_['fa4'], tb * FA4_PER + 33)
                v.tensor_copy(rowC4[0:1, 0:512], rowA4[0:1, 1024:1536])
                v.tensor_tensor(rowC4[0:1, 512:1024], rowA4[0:1, 1024:1536], rowC4[0:1, 0:512],
                                ALU.subtract)
                padV(v, 'fe4')
                v.wait_ge(S_['f4'], tb * F4_PER + 33)

                v.tensor_copy(a3s[:, :], PS[3][:, 512:1024]).then_inc(S_['fe4'], 1)
                for dc in range(NDC):
                    v.wait_ge(S_['f4'], tb * F4_PER + 33 + dc + 1)
                    v.tensor_tensor(gtile[:, :], PS[dc % 2][:, 0:512], a3s[:, :],
                                    ALU.mult)
                    padV(v, 'fe4')
                    v.wait_ge(S_['fa4'], tb * FA4_PER + 34 + dc)
                    v.wait_ge(S_['wx'], 16 * (tb * NDC + dc + 1))
                    yi = tb * NDC + dc
                    if yi >= 2:
                        v.wait_ge(S_['out'] if yi % 2 == 0 else S_['ou1'],
                                  16 * ((yi - 2) // 2 + 1))
                    v.tensor_tensor(gsil[:, :],
                                    ST[:, dc * QOWN + tb * 512: dc * QOWN + (tb + 1) * 512],
                                    gsil[:, :], ALU.mult)
                    v.tensor_tensor(yout[yi % 2][:, :], xfin[dc % 2][:, :], gsil[:, :],
                                    ALU.add)
                    padV(v, 'fy')
    ctx.close()
    return nc


def kernel(L0_attn_x, L0_attn_norm_scale, L0_attn_wq, L0_attn_wk, L0_attn_wv,
           L0_attn_w_out, L0_attn_b_out, L0_ffn_w_up, L0_ffn_w_down):
    x = np.asarray(L0_attn_x, dtype=np.float32)
    wq = np.ascontiguousarray(np.asarray(L0_attn_wq, np.float32).reshape(D, D))
    wk = np.ascontiguousarray(np.asarray(L0_attn_wk, np.float32).reshape(D, D))
    wv = np.ascontiguousarray(np.asarray(L0_attn_wv, np.float32).reshape(D, D))
    wo = np.ascontiguousarray(np.asarray(L0_attn_w_out, np.float32).reshape(D, D))
    nsrow = np.ascontiguousarray(np.asarray(L0_attn_norm_scale, np.float32).reshape(1, D))
    boutc = np.ascontiguousarray(np.asarray(L0_attn_b_out, np.float32).reshape(NDC, 128).T)
    wupT = np.ascontiguousarray(np.asarray(L0_ffn_w_up, np.float32).T)
    wdnT = np.ascontiguousarray(np.asarray(L0_ffn_w_down, np.float32).T)
    xTb = [np.ascontiguousarray(x[b].T) for b in range(2)]

    nc = build_program()
    in_maps = []
    for c in range(8):
        b, qi = c // 4, c % 4
        xTr = np.ascontiguousarray(np.roll(xTb[b], -qi * QOWN, axis=1))
        xTo = np.ascontiguousarray(x[b, qi * QOWN:(qi + 1) * QOWN, :].T)
        in_maps.append({"xT": xTr, "xTo": xTo, "wq": wq, "wk": wk, "wv": wv, "wo": wo,
                        "nsr": nsrow, "boc": boutc, "wupT": wupT, "wdnT": wdnT})
    res = run_bass_kernel_spmd(nc, in_maps, core_ids=list(range(8)))
    y = np.empty((2, S, D), dtype=np.float32)
    for c in range(8):
        b, qi = c // 4, c % 4
        y[b, qi * QOWN:(qi + 1) * QOWN, :] = res.results[c]["yT"].T
    return y
